# version 11
# baseline (speedup 1.0000x reference)
import numpy as np
import concourse.bass as bass
import concourse.mybir as mybir
from concourse.bass_utils import run_bass_kernel_spmd

F32 = mybir.dt.float32
BF16 = mybir.dt.bfloat16
U8 = mybir.dt.uint8
A = mybir.AluOpType
AF = mybir.ActivationFunctionType
AX = mybir.AxisListType

D = 2048
NCH = D // 128
NH = 16
B = 2
L = 8192
NCORES = 8
HALO = 256
TCORE = 2048
TTOT = HALO + TCORE
TT = 256
NTILES = TTOT // TT
C = 128
SCALE = 128.0 ** -0.5
EPS_LN = 1e-5
EPS_RMS_FFN = 1e-8
EPS_FFN = 1e-8
EPS_ATTN = 1e-5
QB = 127.0
LADDER_BITS = [128.0, 64.0, 32.0, 16.0, 8.0, 4.0, 2.0, 1.0]
WT_BITS = [4.0, 2.0, 1.0]

_DBG = []

SK = [f"S{h}" for h in range(NH)]
QH = [f"qh{j}" for j in range(NCH)]
KR = [f"kr{c}" for c in range(NCH)]
VB = [f"vb{c}" for c in range(NCH)]


class Plan:
    def __init__(self):
        self.ops = []
        self.writer = {}
        self.readers = {}

    def op(self, engine, fn, reads=(), writes=(), dma=False, narrow=False,
           stream=None):
        deps = {}
        for bb in reads:
            w = self.writer.get(bb)
            if w is not None:
                deps[w] = "rw"
        for bb in writes:
            w = self.writer.get(bb)
            if w is not None:
                deps[w] = "rw"
        for bb in writes:
            for r in self.readers.get(bb, ()):
                deps.setdefault(r, "war")
        i = len(self.ops)
        assert (stream is not None) == dma, "DMA ops need a stream name"
        self.ops.append(dict(engine=engine, fn=fn, deps=deps, dma=dma,
                             narrow=narrow, stream=stream))
        for bb in reads:
            self.readers.setdefault(bb, []).append(i)
        for bb in writes:
            self.writer[bb] = i
            self.readers[bb] = []
        return i

    def emit(self, nc, sems):
        ops = self.ops
        engseq = [0] * len(ops)
        cnt = {}
        for i, o in enumerate(ops):
            engseq[i] = cnt.get(o["engine"], 0)
            cnt[o["engine"]] = engseq[i] + 1
        interlock = [set() for _ in ops]
        needs_inc = [False] * len(ops)
        for i, o in enumerate(ops):
            for d, typ in o["deps"].items():
                if ops[d]["dma"] or ops[d]["engine"] != o["engine"]:
                    needs_inc[d] = True
                elif typ == "rw" and o["engine"] != "tensor":
                    needs_inc[d] = True
                    interlock[i].add(d)
        for i, o in enumerate(ops):
            if o["dma"]:
                needs_inc[i] = True
        last_of = {}
        for i, o in enumerate(ops):
            last_of[o["engine"]] = i
        for i in last_of.values():
            needs_inc[i] = True

        def stream_of(i):
            return ops[i]["stream"] if ops[i]["dma"] else ops[i]["engine"]

        cum = {}
        opcum = [0] * len(ops)
        pfx = [dict() for _ in ops]
        run_dma = {}
        for i, o in enumerate(ops):
            pfx[i] = dict(run_dma)
            st = stream_of(i)
            amt = 16 if o["dma"] else 1
            if needs_inc[i]:
                cum[st] = cum.get(st, 0) + amt
                if o["dma"]:
                    run_dma[st] = cum[st]
            opcum[i] = cum.get(st, 0)
        hiwater = {}

        def emit_engine(eng_name, eng):
            hw = hiwater.setdefault(eng_name, {})
            for i, o in enumerate(ops):
                if o["engine"] != eng_name:
                    continue
                want = {}
                for d in o["deps"]:
                    do = ops[d]
                    if (do["engine"] == eng_name and not do["dma"]
                            and d not in interlock[i]):
                        continue
                    st = stream_of(d)
                    if do["dma"]:
                        val = pfx[i].get(st, 0)
                    else:
                        val = opcum[d]
                    if want.get(st, 0) < val:
                        want[st] = val
                for st in sorted(want):
                    val = want[st]
                    if hw.get(st, 0) < val:
                        eng.wait_ge(sems[st], val)
                        hw[st] = val
                ins = o["fn"](eng)
                if needs_inc[i]:
                    ins.then_inc(sems[stream_of(i)], 16 if o["dma"] else 1)
        return emit_engine


def _ladder_floor(p, v_, u0, uu, u2, bits, keys, pre_add, eng="vector"):
    kv, k0, ku, k2 = keys
    p.op(eng, lambda e, a=v_, o=u0, c=pre_add: e.tensor_scalar(o, a, c, None, A.add),
         reads=[kv], writes=[k0])
    first = True
    for bit in bits:
        src = u0 if first else uu
        ksrc = k0 if first else ku
        p.op(eng, lambda e, s=src, o=u2, bb=bit: e.tensor_scalar(o, s, bb, -bb, A.is_ge, A.mult),
             reads=[ksrc], writes=[k2])
        p.op(eng, lambda e, s=src, o=uu, c=u2: e.tensor_tensor(o, s, c, A.add),
             reads=[ksrc, k2], writes=[ku])
        first = False
    p.op(eng, lambda e, o=u2, x=u0, y=uu: e.tensor_tensor(o, x, y, A.subtract),
         reads=[k0, ku], writes=[k2])


def build_program():
    nc = bass.Bass("TRN2", target_bir_lowering=False)

    x_fm = nc.dram_tensor("x_fm", [D, TTOT], F32, kind="ExternalInput").ap()
    wq = nc.dram_tensor("wq", [D, D], F32, kind="ExternalInput").ap()
    wk = nc.dram_tensor("wk", [D, D], F32, kind="ExternalInput").ap()
    wi = nc.dram_tensor("wi", [D, D], F32, kind="ExternalInput").ap()
    wo = nc.dram_tensor("wo", [D, D], F32, kind="ExternalInput").ap()
    fw = nc.dram_tensor("fw", [D, D], F32, kind="ExternalInput").ap()
    fbias = nc.dram_tensor("fbias", [128, NCH], F32, kind="ExternalInput").ap()
    gwin = nc.dram_tensor("gwin", [128, NCH], F32, kind="ExternalInput").ap()
    ident_in = nc.dram_tensor("ident_in", [128, 128], F32, kind="ExternalInput").ap()
    causal_in = nc.dram_tensor("causal_in", [128, 128], U8, kind="ExternalInput").ap()
    halom_in = nc.dram_tensor("halom_in", [128, 2], F32, kind="ExternalInput").ap()
    onesc_in = nc.dram_tensor("onesc_in", [128, 128], F32, kind="ExternalInput").ap()
    y_fm = nc.dram_tensor("y_fm", [D, TCORE], F32, kind="ExternalOutput").ap()
    wscr = [nc.dram_tensor(f"wscr{i}", [2 * D, D], BF16, kind="Internal").ap()
            for i in range(4)]
    wtscr = nc.dram_tensor("wtscr", [NCH * 128, D], BF16, kind="Internal").ap()
    dbg_t = {}
    for name in _DBG:
        dbg_t[name] = nc.dram_tensor("dbg_" + name, [128, 4096], F32,
                                     kind="ExternalOutput").ap()

    from contextlib import ExitStack
    es = ExitStack()

    def sb(name, shape, dt=F32):
        return es.enter_context(nc.sbuf_tensor(name, shape, dt))

    xres = [sb(f"xres{i}", [128, NCH * TT]) for i in range(2)]
    xno = [sb(f"xno{i}", [128, NCH * TT]) for i in range(2)]
    hixq = [sb(f"hixq{i}", [128, NCH * TT], BF16) for i in range(2)]
    lobf = [sb(f"lobf{i}", [128, NCH * TT], BF16) for i in range(2)]
    vbuf = sb("vbuf", [128, NCH * TT])
    kraw = sb("kraw", [128, NCH * TT])
    wslA = [sb(f"wslA{i}", [128, 1024], BF16) for i in range(2)]
    wslB = [sb(f"wslB{i}", [128, 1024], BF16) for i in range(2)]
    fwslA = [sb(f"fwslA{i}", [128, 512], BF16) for i in range(2)]
    fwslB = [sb(f"fwslB{i}", [128, 512], BF16) for i in range(2)]
    qhalf = sb("qhalf", [128, NCH * TT])
    kh = [sb(f"kh{i}", [128, TT]) for i in range(2)]
    gh = [sb(f"gh{i}", [128, TT]) for i in range(2)]
    th = [sb(f"th{i}", [128, TT]) for i in range(2)]
    bt = [sb(f"bt{i}", [128, C]) for i in range(2)]
    e1 = [sb(f"e1{i}", [128, C]) for i in range(2)]
    e2 = [sb(f"e2{i}", [128, C]) for i in range(2)]
    qint = [sb(f"qint{i}", [128, C]) for i in range(2)]
    qitr = [sb(f"qitr{i}", [128, C]) for i in range(2)]
    kot = [sb(f"kot{i}", [128, C]) for i in range(2)]
    Pp = [sb(f"Pp{i}", [128, C]) for i in range(2)]
    kTs = [sb(f"kTs{i}", [128, C]) for i in range(2)]
    vTs = [sb(f"vTs{i}", [128, C]) for i in range(2)]
    st8 = [sb(f"st8{i}", [128, 4]) for i in range(2)]
    stE = [sb(f"stE{i}", [128, 4]) for i in range(2)]
    Sst = sb("Sst", [128, NH * 128])
    sq = sb("sq", [128, 2 * TT])
    u0 = sb("u0", [128, TT])
    uu = sb("uu", [128, TT])
    u2 = sb("u2", [128, TT])
    rows = sb("rows", [1, 6 * TT])
    colsb = sb("colsb", [128, 16])
    rb = [sb(f"rb{i}", [128, TT]) for i in range(2)]
    ab = [sb(f"ab{i}", [128, TT]) for i in range(2)]
    xsb = [sb(f"xsb{i}", [128, TT]) for i in range(2)]
    invsb = [sb(f"invsb{i}", [128, TT]) for i in range(2)]
    gamma = sb("gamma", [128, NCH])
    hbias = sb("hbias", [128, NCH])
    gwb = sb("gwb", [128, NCH])
    ident = sb("ident", [128, 128])
    causal = sb("causal", [128, 128], U8)
    halom = sb("halom", [128, 2])
    onesc = sb("onesc", [128, 128])
    ev = sb("ev", [128, 2 * TT])
    ev2 = sb("ev2", [128, 2 * TT])

    PS = es.enter_context(nc.psum_tensor("PS", [128, 4096], F32))
    ps_A = PS[:, 0:128]
    ps_O = PS[:, 256:384]
    ps_T = PS[:, 512:640]
    ps_S = PS[:, 768:896]
    ps_bkA = [PS[:, (2 + j // 2) * 512 + (j % 2) * TT:
                 (2 + j // 2) * 512 + (j % 2) * TT + TT] for j in range(4)]
    ps_bkB = [PS[:, (4 + j // 2) * 512 + (j % 2) * TT:
                 (4 + j // 2) * 512 + (j % 2) * TT + TT] for j in range(4)]
    PBKA = [f"pb{2 + j // 2}" for j in range(4)]
    PBKB = [f"pb{4 + j // 2}" for j in range(4)]
    ps_tr = PS[:, 6 * 512:6 * 512 + 128]
    ps_rr = PS[0:1, 6 * 512 + 128:6 * 512 + 256]
    ps_bc = PS[:, 6 * 512 + 256:6 * 512 + 512]
    ps_row1 = PS[0:1, 7 * 512:7 * 512 + TT]
    ps_row2 = PS[0:1, 7 * 512 + TT:7 * 512 + 2 * TT]

    sems = {}
    for s in ["sync", "tensor", "vector", "scalar", "gpsimd",
              "dc", "dpl", "dps", "dgp", "dx0", "dx1", "do",
              "wa0", "wa1", "wb0", "wb1", "fa0", "fa1", "fb0", "fb1",
              "dpg", "dsc", "dbg"]:
        sems[s] = es.enter_context(nc.semaphore("sem_" + s))

    p = Plan()

    def ch(buf, c, w=TT):
        return buf[:, c * w:(c + 1) * w]

    ROWMAP = {0: 0, 1: 1, 2: 2, 3: 3, 4: 4, 5: 5, 6: 4, 7: 5}

    def row(i):
        s = ROWMAP[i]
        return rows[0:1, s * TT:(s + 1) * TT]

    def rkey(i):
        return f"r{ROWMAP[i]}"

    for dst, src, key in [(ident, ident_in, "ident"), (causal, causal_in, "causal"),
                          (halom, halom_in, "halom"), (onesc, onesc_in, "onesc"),
                          (gwb, gwin, "gwb"), (hbias, fbias, "hbias_raw")]:
        p.op("sync", lambda e, d=dst, s=src: e.dma_start(out=d[:, :], in_=s),
             writes=[key], dma=True, stream="dc")
    p.op("vector", lambda e: e.tensor_scalar(hbias[:, :], hbias[:, :], 0.5, None, A.mult),
         reads=["hbias_raw"], writes=["hbias"], narrow=True)
    for i in range(2):
        p.op("vector", lambda e, i=i: e.memset(Pp[i][:, :], 0.0), writes=[f"Ppz{i}"])
    p.op("vector", lambda e: e.memset(Sst[:, :], 0.0), writes=SK)

    ones_col = onesc[:, 0:1]
    ones_row = onesc[0:1, :]

    for widx, wsrc in enumerate([wq, wk, wi, wo]):
        for k in range(NCH):
            m = widx * NCH + k
            par = m % 2
            slab = vbuf[:, par * D:(par + 1) * D]
            hih = hixq[0][:, par * D:(par + 1) * D]
            loh = lobf[0][:, par * D:(par + 1) * D]
            wkk, hik, lok = f"prep_w{par}", f"prep_hi{par}", f"prep_lo{par}"
            p.op("sync" if par == 0 else "gpsimd",
                 lambda e, s=wsrc, k=k, sl=slab: e.dma_start(
                 out=sl, in_=s[k * 128:(k + 1) * 128, :]),
                 reads=[], writes=[wkk], dma=True,
                 stream="dpl" if par == 0 else "dpg")
            p.op("vector" if par == 0 else "gpsimd",
                 lambda e, sl=slab, hh=hih: e.tensor_copy(hh, sl),
                 reads=[wkk], writes=[hik])
            p.op("gpsimd" if par == 0 else "vector",
                 lambda e, sl=slab, hh=hih, ll=loh: e.tensor_tensor(
                 ll, sl, hh, A.subtract),
                 reads=[wkk, hik], writes=[lok])
            for nh in range(2):
                r0 = nh * D + k * 128
                for qq in range(2):
                    c0 = par * D + nh * 1024 + qq * 512
                    st = "dps" if (nh + qq) % 2 == 0 else "dsc"
                    eng = "sync" if st == "dps" else "scalar"
                    p.op(eng, lambda e, r0=r0, w=widx, qq=qq, c0=c0: e.dma_start(
                        out=wscr[w][r0:r0 + 128, qq * 1024:qq * 1024 + 512],
                        in_=hixq[0][:, c0:c0 + 512]),
                        reads=[hik], writes=[f"wscr{widx}"], dma=True, stream=st)
                    p.op(eng, lambda e, r0=r0, w=widx, qq=qq, c0=c0: e.dma_start(
                        out=wscr[w][r0:r0 + 128, qq * 1024 + 512:qq * 1024 + 1024],
                        in_=lobf[0][:, c0:c0 + 512]),
                        reads=[lok], writes=[f"wscr{widx}"], dma=True, stream=st)

    for n in range(NCH):
        par = n % 2
        slab = vbuf[:, par * D:(par + 1) * D]
        wkk = f"prep_w{par}"
        p.op("sync" if par == 0 else "gpsimd",
             lambda e, n=n, sl=slab: e.dma_start(
             out=sl, in_=fw[n * 128:(n + 1) * 128, :]),
             reads=[f"prep_hi{par}", f"prep_lo{par}"], writes=[wkk], dma=True,
             stream="dpl" if par == 0 else "dpg")
        p.op("vector", lambda e, sl=slab: e.reduce_sum(colsb[:, 1:2], sl, AX.X,
                                                       apply_absolute_value=True),
             reads=[wkk], writes=["c1"], narrow=True)
        p.op("vector", lambda e, n=n: e.tensor_scalar(gamma[:, n:n + 1], colsb[:, 1:2],
                                                      1.0 / 2048.0, None, A.mult),
             reads=["c1"], writes=["gamma"], narrow=True)
        p.op("vector", lambda e, n=n: e.tensor_scalar(colsb[:, 2:3], gamma[:, n:n + 1],
                                                      EPS_FFN, None, A.add),
             reads=["gamma"], writes=["c2"], narrow=True)
        p.op("vector", lambda e: e.reciprocal(colsb[:, 0:1], colsb[:, 2:3]),
             reads=["c2"], writes=["c0"], narrow=True)
        p.op("vector", lambda e, sl=slab: e.tensor_scalar(xno[0][:, 0:D], sl,
                                                          colsb[:, 0:1], None, A.mult),
             reads=[wkk, "c0"], writes=["prep_z"])
        p.op("vector", lambda e: e.tensor_scalar(kraw[:, 0:D], xno[0][:, 0:D],
                                                 0.5, None, A.is_ge),
             reads=["prep_z"], writes=["prep_t0"])
        p.op("gpsimd", lambda e: e.tensor_scalar(kraw[:, D:2 * D], xno[0][:, 0:D],
                                                 -0.5, None, A.is_ge),
             reads=["prep_z"], writes=["prep_t1"])
        p.op("vector", lambda e: e.tensor_tensor(kraw[:, 0:D], kraw[:, 0:D],
                                                 kraw[:, D:2 * D], A.add),
             reads=["prep_t0", "prep_t1"], writes=["prep_t0"])
        p.op("vector", lambda e: e.tensor_scalar(kraw[:, 0:D], kraw[:, 0:D],
                                                 -1.0, None, A.add),
             reads=["prep_t0"], writes=["prep_t0"])
        for j in range(NCH):
            p.op("tensor", lambda e, j=j: e.transpose(ps_tr, kraw[:, j * 128:(j + 1) * 128],
                                                      ident[:, :]),
                 reads=["prep_t0", "ident", "gb6"], writes=["gb6"])
            p.op("scalar", lambda e, j=j: e.copy(
                out=hixq[0][:, j * 128:(j + 1) * 128], in_=ps_tr),
                reads=["gb6"], writes=["prep_hi0"])
        for j in range(NCH):
            p.op("gpsimd", lambda e, j=j, n=n: e.dma_start(
                out=wtscr[j * 128:(j + 1) * 128, n * 128:(n + 1) * 128],
                in_=hixq[0][:, j * 128:(j + 1) * 128]),
                reads=["prep_hi0"], writes=["wtscrk"], dma=True, stream="dgp")

    allkeys = (SK + QH + KR + VB
               + [f"{nm}{pp}_{c}" for nm in ("xr", "xn", "hx", "lo")
                  for pp in (0, 1) for c in range(NCH)]
               + ["lq", "l0", "l1", "l2"])
    p.op("sync", lambda e: e.nop(),
         reads=[],
         writes=allkeys + ["prep_w0", "prep_w1", "prep_hi0", "prep_hi1",
                           "prep_lo0", "prep_lo1", "prep_z",
                           "prep_t0", "prep_t1"])

    def XRk(pr, c):
        return f"xr{pr}_{c}"

    def XNk(pr, c):
        return f"xn{pr}_{c}"

    def HXk(pr, c):
        return f"hx{pr}_{c}"

    def LOk(pr, c):
        return f"lo{pr}_{c}"

    def ln_stats(pr, srcbuf, srck, with_rms):
        for c in range(NCH):
            slot = c % 2
            p.op("scalar", lambda e, c=c, s=slot, sbf=srcbuf: e.activation(
                out=sq[:, s * TT:(s + 1) * TT], in_=ch(sbf, c), func=AF.Square),
                reads=[srck(c)], writes=[f"sq{slot}"])
            p.op("tensor", lambda e, c=c, sbf=srcbuf: e.matmul(
                ps_row1, ones_col, ch(sbf, c), start=(c == 0), stop=False),
                reads=[srck(c), "onesc"], writes=["gb7"])
            p.op("tensor", lambda e, c=c, s=slot: e.matmul(
                ps_row2, ones_col, sq[:, s * TT:(s + 1) * TT],
                start=False, stop=(c == NCH - 1)),
                reads=[f"sq{slot}", "onesc"], writes=["gb7"])
        p.op("vector", lambda e: e.tensor_scalar(row(0), ps_row1, 1.0 / 2048.0, None, A.mult),
             reads=["gb7"], writes=[rkey(0)], narrow=True)
        p.op("vector", lambda e: e.tensor_scalar(row(1), ps_row2, 1.0 / 2048.0, None, A.mult),
             reads=["gb7"], writes=[rkey(1)], narrow=True)
        p.op("vector", lambda e: e.tensor_tensor(row(6), row(0), row(0), A.mult),
             reads=[rkey(0)], writes=[rkey(6)], narrow=True)
        p.op("vector", lambda e: e.tensor_tensor(row(2), row(1), row(6), A.subtract),
             reads=[rkey(1), rkey(6)], writes=[rkey(2)], narrow=True)
        p.op("vector", lambda e: e.tensor_scalar(row(6), row(2), EPS_LN, None, A.add),
             reads=[rkey(2)], writes=[rkey(6)], narrow=True)
        p.op("scalar", lambda e: e.activation(out=row(7), in_=row(6), func=AF.Ln),
             reads=[rkey(6)], writes=[rkey(7)])
        p.op("vector", lambda e: e.tensor_scalar(row(7), row(7), -0.5, None, A.mult),
             reads=[rkey(7)], writes=[rkey(7)], narrow=True)
        p.op("scalar", lambda e: e.activation(out=row(3), in_=row(7), func=AF.Exp),
             reads=[rkey(7)], writes=[rkey(3)])
        if with_rms:
            p.op("vector", lambda e: e.tensor_tensor(row(6), row(3), row(3), A.mult),
                 reads=[rkey(3)], writes=[rkey(6)], narrow=True)
            p.op("vector", lambda e: e.tensor_tensor(row(6), row(2), row(6), A.mult),
                 reads=[rkey(2), rkey(6)], writes=[rkey(6)], narrow=True)
            p.op("vector", lambda e: e.tensor_scalar(row(6), row(6), EPS_RMS_FFN, None, A.add),
                 reads=[rkey(6)], writes=[rkey(6)], narrow=True)
            p.op("scalar", lambda e: e.activation(out=row(7), in_=row(6), func=AF.Ln),
                 reads=[rkey(6)], writes=[rkey(7)])
            p.op("vector", lambda e: e.tensor_scalar(row(7), row(7), -0.5, None, A.mult),
                 reads=[rkey(7)], writes=[rkey(7)], narrow=True)
            p.op("scalar", lambda e: e.activation(out=row(6), in_=row(7), func=AF.Exp),
                 reads=[rkey(7)], writes=[rkey(6)])
            p.op("vector", lambda e: e.tensor_tensor(row(4), row(3), row(6), A.mult),
                 reads=[rkey(3), rkey(6)], writes=[rkey(4)], narrow=True)
        else:
            p.op("vector", lambda e: e.tensor_copy(row(4), row(3)),
                 reads=[rkey(3)], writes=[rkey(4)], narrow=True)
        p.op("vector", lambda e: e.tensor_tensor(row(5), row(0), row(4), A.mult),
             reads=[rkey(0), rkey(4)], writes=[rkey(5)], narrow=True)
        p.op("tensor", lambda e: e.matmul(ps_bc, ones_row, row(4), start=True, stop=True),
             reads=[rkey(4), "onesc", "gb6"], writes=["gb6"])
        p.op("scalar", lambda e, d=rb[pr]: e.copy(out=d[:, :], in_=ps_bc),
             reads=["gb6"], writes=[f"rb{pr}"])
        p.op("tensor", lambda e: e.matmul(ps_bc, ones_row, row(5), start=True, stop=True),
             reads=[rkey(5), "onesc", "gb6"], writes=["gb6"])
        p.op("scalar", lambda e, d=ab[pr]: e.copy(out=d[:, :], in_=ps_bc),
             reads=["gb6"], writes=[f"ab{pr}"])

    def ln_apply(pr, srcbuf, srck, dstbuf, dstk):
        for c in range(NCH):
            eng = "gpsimd" if c % 2 == 0 else "vector"
            p.op(eng, lambda e, c=c, s=srcbuf, d=dstbuf: e.tensor_tensor(
                ch(d, c), ch(s, c), rb[pr][:, :], A.mult),
                reads=[srck(c), f"rb{pr}"], writes=[dstk(c)])
            p.op(eng, lambda e, c=c, d=dstbuf: e.tensor_tensor(
                ch(d, c), ch(d, c), ab[pr][:, :], A.subtract),
                reads=[dstk(c), f"ab{pr}"], writes=[dstk(c)])

    def absmax_calc(pr, srcbuf, srck):
        p.op("vector", lambda e, s=srcbuf: e.tensor_tensor(u0[:, :], ch(s, 0), ch(s, 1), A.max),
             reads=[srck(0), srck(1)], writes=["l0"])
        p.op("gpsimd", lambda e, s=srcbuf: e.tensor_copy(u2[:, :], ch(s, 0)),
             reads=[srck(0)], writes=["l2"])
        for c in range(1, NCH):
            if c > 1:
                p.op("vector", lambda e, c=c, s=srcbuf: e.tensor_tensor(
                    u0[:, :], u0[:, :], ch(s, c), A.max),
                    reads=[srck(c), "l0"], writes=["l0"])
            p.op("vector", lambda e, c=c, s=srcbuf: e.tensor_tensor(
                u2[:, :], u2[:, :], ch(s, c), A.min),
                reads=[srck(c), "l2"], writes=["l2"])
        for j in range(TT // 128):
            p.op("tensor", lambda e, j=j: e.transpose(ps_tr, u0[:, j * 128:(j + 1) * 128],
                                                      ident[:, :]),
                 reads=["l0", "ident", "gb6"], writes=["gb6"])
            p.op("vector", lambda e, j=j: e.reduce_max(colsb[:, j:j + 1], ps_tr, AX.X),
                 reads=["gb6"], writes=["cmax"], narrow=True)
            p.op("tensor", lambda e, j=j: e.transpose(ps_tr, u2[:, j * 128:(j + 1) * 128],
                                                      ident[:, :]),
                 reads=["l2", "ident", "gb6"], writes=["gb6"])
            p.op("vector", lambda e, j=j: e.tensor_reduce(colsb[:, 2 + j:3 + j], ps_tr, AX.X, A.min),
                 reads=["gb6"], writes=["cmin"], narrow=True)
        for j in range(TT // 128):
            p.op("vector", lambda e, j=j: e.tensor_scalar(colsb[:, 4 + j:5 + j],
                                                          colsb[:, 2 + j:3 + j], -1.0, None, A.mult),
                 reads=["cmin"], writes=["cneg"], narrow=True)
            p.op("vector", lambda e, j=j: e.tensor_tensor(colsb[:, 6 + j:7 + j],
                                                          colsb[:, j:j + 1],
                                                          colsb[:, 4 + j:5 + j], A.max),
                 reads=["cmax", "cneg"], writes=["cam"], narrow=True)
            p.op("vector", lambda e, j=j: e.tensor_scalar(colsb[:, 8 + j:9 + j],
                                                          colsb[:, 6 + j:7 + j],
                                                          1.0 / QB, None, A.mult),
                 reads=["cam"], writes=["cxs"], narrow=True)
            p.op("vector", lambda e, j=j: e.tensor_scalar(colsb[:, 10 + j:11 + j],
                                                          colsb[:, 8 + j:9 + j],
                                                          EPS_FFN, None, A.add),
                 reads=["cxs"], writes=["cpe"], narrow=True)
            p.op("vector", lambda e, j=j: e.reciprocal(
                colsb[:, 12 + j:13 + j], colsb[:, 10 + j:11 + j]),
                reads=["cpe"], writes=["cinv"], narrow=True)
        for j in range(TT // 128):
            p.op("tensor", lambda e, j=j: e.transpose(ps_rr, colsb[:, 8 + j:9 + j], ident[:, :]),
                 reads=["cxs", "ident", "gb6"], writes=["gb6"])
            p.op("scalar", lambda e, j=j: e.copy(out=rows[0:1, j * 128:(j + 1) * 128], in_=ps_rr),
                 reads=["gb6"], writes=["r0"])
            p.op("tensor", lambda e, j=j: e.transpose(ps_rr, colsb[:, 12 + j:13 + j], ident[:, :]),
                 reads=["cinv", "ident", "gb6"], writes=["gb6"])
            p.op("scalar", lambda e, j=j: e.copy(out=rows[0:1, TT + j * 128:TT + (j + 1) * 128],
                                                 in_=ps_rr),
                 reads=["gb6"], writes=["r1"])
        p.op("tensor", lambda e: e.matmul(ps_bc, ones_row, rows[0:1, 0:TT], start=True, stop=True),
             reads=["r0", "onesc", "gb6"], writes=["gb6"])
        p.op("scalar", lambda e, d=xsb[pr]: e.copy(out=d[:, :], in_=ps_bc),
             reads=["gb6"], writes=[f"xsb{pr}"])
        p.op("tensor", lambda e: e.matmul(ps_bc, ones_row, rows[0:1, TT:2 * TT], start=True, stop=True),
             reads=["r1", "onesc", "gb6"], writes=["gb6"])
        p.op("scalar", lambda e, d=invsb[pr]: e.copy(out=d[:, :], in_=ps_bc),
             reads=["gb6"], writes=[f"invsb{pr}"])

    def quant_chunk(pr, srcbuf, srck, c):
        p.op("vector", lambda e, c=c, s=srcbuf: e.tensor_tensor(uu[:, :], ch(s, c),
                                                                invsb[pr][:, :], A.mult),
             reads=[srck(c), f"invsb{pr}"], writes=["lq"])
        _ladder_floor(p, uu[:, :], u0[:, :], uu[:, :], u2[:, :],
                      LADDER_BITS, ("lq", "l0", "lq", "l2"), 128.5)
        p.op("vector", lambda e: e.tensor_scalar(u2[:, :], u2[:, :],
                                                 -128.0, -127.0, A.add, A.max),
             reads=["l2"], writes=["l2"])
        p.op("vector", lambda e, c=c, hb=hixq[pr]: e.tensor_scalar(
            ch(hb, c), u2[:, :], 127.0, None, A.min),
            reads=["l2"], writes=[HXk(pr, c)])

    def ffn_quarter(pr, q, ph):
        fwslX = fwslA if ph == "A" else fwslB
        ps_bkX = ps_bkA if ph == "A" else ps_bkB
        PBKX = PBKA if ph == "A" else PBKB
        fs = "fa" if ph == "A" else "fb"
        for k in range(NCH):
            fslot = k % 2
            fkey = f"{fs}sl{fslot}"
            p.op("sync", lambda e, k=k, s=fslot, q=q, fb=fwslX: e.dma_start(
                out=fb[s][:, :], in_=wtscr[k * 128:(k + 1) * 128,
                                          q * 512:(q + 1) * 512]),
                reads=["wtscrk"], writes=[fkey], dma=True,
                stream=f"{fs}{fslot}")
            for j in range(4):
                p.op("tensor", lambda e, k=k, j=j, s=fslot, fb=fwslX,
                     hb=hixq[pr]: e.matmul(
                    ps_bkX[j], fb[s][:, j * 128:(j + 1) * 128], ch(hb, k),
                    start=(k == 0 and j % 2 == 0),
                    stop=(k == NCH - 1 and j % 2 == 1)),
                    reads=[fkey, HXk(pr, k)], writes=[PBKX[j]])
            if k % 4 == 3:
                yield
        for j in range(4):
            n = q * 4 + j
            slot = n % 2
            p.op("scalar", lambda e, n=n, j=j, s=slot: e.mul(
                ev[:, s * TT:(s + 1) * TT], ps_bkX[j], gamma[:, n:n + 1]),
                reads=[PBKX[j], "gamma"], writes=[f"ev{slot}"])
            p.op("vector", lambda e, n=n, s=slot: e.tensor_tensor(
                ev2[:, s * TT:(s + 1) * TT], ev[:, s * TT:(s + 1) * TT],
                xsb[pr][:, :], A.mult),
                reads=[f"ev{slot}", f"xsb{pr}"], writes=[f"fe{slot}"])
            p.op("vector", lambda e, n=n, s=slot: e.tensor_scalar(
                ev2[:, s * TT:(s + 1) * TT], ev2[:, s * TT:(s + 1) * TT],
                0.5, hbias[:, n:n + 1], A.mult, A.add),
                reads=[f"fe{slot}", "hbias"], writes=[f"fe{slot}"])
            p.op("gpsimd" if n % 2 == 0 else "vector",
                 lambda e, n=n, s=slot, xb=xres[pr]: e.tensor_tensor(
                ch(xb, n), ch(xb, n), ev2[:, s * TT:(s + 1) * TT], A.add),
                reads=[f"fe{slot}", XRk(pr, n)], writes=[XRk(pr, n)])
        yield

    def stream_proj(pr, widx, q, evac, ph):
        wslX = wslA if ph == "A" else wslB
        ps_bkX = ps_bkA if ph == "A" else ps_bkB
        PBKX = PBKA if ph == "A" else PBKB
        ws = "wa" if ph == "A" else "wb"
        nh, qq = q // 2, q % 2
        for k in range(NCH):
            slot = k % 2
            wkey = f"{ws}sl{slot}"
            r0 = nh * D + k * 128
            p.op("sync", lambda e, r0=r0, s=slot, w=widx, qq=qq,
                 wb=wslX: e.dma_start(
                out=wb[s][:, :], in_=wscr[w][r0:r0 + 128,
                                             qq * 1024:(qq + 1) * 1024]),
                reads=[f"wscr{widx}"], writes=[wkey], dma=True,
                stream=f"{ws}{slot}")
            for j in range(4):
                whi = wslX[slot][:, j * 128:(j + 1) * 128]
                wlo = wslX[slot][:, 512 + j * 128:512 + (j + 1) * 128]
                xh = ch(hixq[pr], k)
                xl = ch(lobf[pr], k)
                p.op("tensor", lambda e, o=ps_bkX[j], a=whi, bb=xh,
                     f=(k == 0 and j % 2 == 0): e.matmul(o, a, bb, start=f, stop=False),
                     reads=[wkey, HXk(pr, k)], writes=[PBKX[j]])
                p.op("tensor", lambda e, o=ps_bkX[j], a=whi, bb=xl: e.matmul(
                    o, a, bb, start=False, stop=False),
                    reads=[wkey, LOk(pr, k)], writes=[PBKX[j]])
                p.op("tensor", lambda e, o=ps_bkX[j], a=wlo, bb=xh,
                     last=(k == NCH - 1 and j % 2 == 1): e.matmul(
                    o, a, bb, start=False, stop=last),
                    reads=[wkey, HXk(pr, k)], writes=[PBKX[j]])
            if k % 4 == 3:
                yield
        for j in range(4):
            evac(q * 4 + j, ps_bkX[j], PBKX[j])
        yield

    def gla_head(pr, h, has_q, tile_idx):
        i = h % 2
        KHk, GHk, THk = f"kh{i}", f"gh{i}", f"th{i}"
        BTk, E1k, E2k = f"bt{i}", f"e1{i}", f"e2{i}"
        QIk, QTk = f"qint{i}", f"qitr{i}"
        KOk = f"kot{i}"
        PPk, KTk, VTk = f"Pp{i}", f"kTs{i}", f"vTs{i}"
        S8k, SEk = f"st8{i}", f"stE{i}"
        p.op("scalar", lambda e, h=h: e.activation(out=th[i][:, :], in_=ch(kraw, h), func=AF.Exp),
             reads=[KR[h]], writes=[THk])
        p.op("vector", lambda e: e.tensor_scalar(th[i][:, :], th[i][:, :], 1.0, None, A.add),
             reads=[THk], writes=[THk])
        p.op("scalar", lambda e: e.activation(out=gh[i][:, :], in_=th[i][:, :], func=AF.Ln),
             reads=[THk], writes=[GHk])
        p.op("vector", lambda e, h=h: e.tensor_tensor(th[i][:, :], ch(kraw, h), gh[i][:, :], A.subtract),
             reads=[KR[h], GHk], writes=[THk])
        p.op("scalar", lambda e: e.activation(out=kh[i][:, :], in_=th[i][:, :], func=AF.Exp),
             reads=[THk], writes=[KHk])
        p.op("vector", lambda e: e.tensor_scalar(gh[i][:, :], gh[i][:, :], -1.0, None, A.mult),
             reads=[GHk], writes=[GHk])
        Sh = Sst[:, h * 128:(h + 1) * 128]
        H = C // 2
        for c in (0, 1):
            csl = slice(c * C, (c + 1) * C)
            p.op("tensor", lambda e, h=h, c=c: e.transpose(
                ps_T, vbuf[:, h * TT + c * C:h * TT + (c + 1) * C], ident[:, :]),
                reads=[VB[h], "ident", "gb1"], writes=["gb1"])
            p.op("scalar", lambda e: e.copy(out=vTs[i][:, :], in_=ps_T),
                 reads=["gb1"], writes=[VTk])
            if tile_idx == 0:
                p.op("vector", lambda e, c=c: e.tensor_scalar(
                    vTs[i][:, :], vTs[i][:, :], halom[:, c:c + 1], None, A.mult),
                    reads=[VTk, "halom"], writes=[VTk])
            p.op("vector", lambda e, csl=csl: e.tensor_tensor_scan(
                bt[i][:, :], gh[i][:, csl], gh[i][:, csl], 0.0, A.add, A.bypass),
                reads=[GHk], writes=[BTk])
            p.op("vector", lambda e: e.tensor_scalar(st8[i][:, 1:2], bt[i][:, H - 1:H],
                                                     0.5, None, A.mult),
                 reads=[BTk], writes=[S8k], narrow=True)
            p.op("vector", lambda e: e.tensor_scalar(st8[i][:, 2:3], bt[i][:, C - 1:C],
                                                     0.5, st8[i][:, 1:2], A.mult, A.add),
                 reads=[BTk, S8k], writes=[S8k], narrow=True)
            p.op("vector", lambda e: e.tensor_tensor(st8[i][:, 3:4], bt[i][:, C - 1:C],
                                                     st8[i][:, 1:2], A.subtract),
                 reads=[BTk, S8k], writes=[S8k], narrow=True)
            p.op("vector", lambda e: e.tensor_tensor(st8[i][:, 0:1], bt[i][:, C - 1:C],
                                                     bt[i][:, H - 1:H], A.subtract),
                 reads=[BTk, S8k], writes=[S8k], narrow=True)
            p.op("scalar", lambda e: e.activation(out=stE[i][:, 0:1], in_=bt[i][:, C - 1:C],
                                                  func=AF.Exp),
                 reads=[BTk], writes=[SEk])
            p.op("scalar", lambda e: e.activation(out=stE[i][:, 1:2], in_=st8[i][:, 3:4],
                                                  func=AF.Exp),
                 reads=[S8k], writes=[SEk])
            p.op("scalar", lambda e: e.activation(out=stE[i][:, 2:3], in_=st8[i][:, 0:1],
                                                  func=AF.Exp, scale=0.5),
                 reads=[S8k], writes=[SEk])
            p.op("scalar", lambda e: e.activation(out=stE[i][:, 3:4], in_=bt[i][:, H - 1:H],
                                                  func=AF.Exp, scale=0.5),
                 reads=[BTk], writes=[SEk])
            p.op("scalar", lambda e: e.activation(out=e1[i][:, :], in_=bt[i][:, :], func=AF.Exp),
                 reads=[BTk], writes=[E1k])
            if has_q:
                qsl = qhalf[:, h * TT + c * C:h * TT + (c + 1) * C]
                p.op("vector", lambda e, q=qsl: e.tensor_tensor(qint[i][:, :], q, e1[i][:, :], A.mult),
                     reads=[QH[h], E1k], writes=[QIk])
            p.op("vector", lambda e: e.tensor_scalar(bt[i][:, 0:H], bt[i][:, 0:H], -1.0,
                                                     st8[i][:, 1:2], A.mult, A.add),
                 reads=[BTk, S8k], writes=[BTk], narrow=True)
            p.op("vector", lambda e: e.tensor_scalar(bt[i][:, H:C], bt[i][:, H:C], -1.0,
                                                     st8[i][:, 2:3], A.mult, A.add),
                 reads=[BTk, S8k], writes=[BTk], narrow=True)
            p.op("vector", lambda e: e.tensor_scalar(bt[i][:, :], bt[i][:, :],
                                                     80.0, -80.0, A.min, A.max),
                 reads=[BTk], writes=[BTk])
            p.op("scalar", lambda e: e.activation(out=e2[i][:, :], in_=bt[i][:, :], func=AF.Exp),
                 reads=[BTk], writes=[E2k])
            if has_q:
                p.op("scalar", lambda e: e.activation(out=e1[i][:, :], in_=bt[i][:, :],
                                                      func=AF.Exp, scale=-1.0),
                     reads=[BTk], writes=[E1k])
                p.op("vector", lambda e, q=qsl: e.tensor_tensor(qitr[i][:, :], q, e1[i][:, :], A.mult),
                     reads=[QH[h], E1k], writes=[QTk])
            p.op("vector", lambda e, csl=csl: e.tensor_tensor(kot[i][:, :], kh[i][:, csl],
                                                              e2[i][:, :], A.mult),
                 reads=[KHk, E2k], writes=[KOk])
            if has_q:
                p.op("vector", lambda e: e.tensor_scalar(e1[i][:, 0:H], kot[i][:, 0:H],
                                                         stE[i][:, 3:4], None, A.mult),
                     reads=[KOk, SEk], writes=[E1k], narrow=True)
                p.op("vector", lambda e: e.tensor_scalar(e1[i][:, H:C], qitr[i][:, H:C],
                                                         stE[i][:, 2:3], None, A.mult),
                     reads=[QTk, SEk], writes=[E1k], narrow=True)
                p.op("tensor", lambda e: e.matmul(ps_A[0:H, 0:H], kot[i][:, 0:H],
                                                  qitr[i][:, 0:H],
                                                  start=True, stop=False),
                     reads=[KOk, QTk, "gb0"], writes=["gb0"])
                p.op("tensor", lambda e: e.matmul(ps_A[H:C, H:C], kot[i][:, H:C],
                                                  qitr[i][:, H:C],
                                                  start=True, stop=True),
                     reads=[KOk, QTk], writes=["gb0"])
                p.op("tensor", lambda e: e.matmul(ps_A[0:H, H:C], e1[i][:, 0:H],
                                                  e1[i][:, H:C],
                                                  start=False, stop=True),
                     reads=[E1k], writes=["gb0"])
                p.op("vector", lambda e: e.copy_predicated(Pp[i][0:H, 0:H],
                                                           causal[0:H, 0:H],
                                                           ps_A[0:H, 0:H]),
                     reads=["gb0", "causal", f"Ppz{i}"], writes=[PPk])
                p.op("vector", lambda e: e.copy_predicated(Pp[i][H:C, H:C],
                                                           causal[H:C, H:C],
                                                           ps_A[H:C, H:C]),
                     reads=["gb0", "causal", PPk], writes=[PPk])
                p.op("scalar", lambda e: e.copy(out=Pp[i][0:H, H:C], in_=ps_A[0:H, H:C]),
                     reads=["gb0", PPk], writes=[PPk])
                p.op("tensor", lambda e: e.matmul(ps_O, vTs[i][:, :], Pp[i][:, :],
                                                  start=True, stop=False),
                     reads=[VTk, PPk, "gb0"], writes=["gb0"])
                p.op("tensor", lambda e, S=Sh: e.matmul(ps_O, S, qint[i][:, :],
                                                        start=False, stop=True),
                     reads=[SK[h], QIk], writes=["gb0"])
                p.op("scalar", lambda e, h=h, c=c, ob=xno[pr]: e.copy(
                    out=ob[:, h * TT + c * C:h * TT + (c + 1) * C], in_=ps_O),
                    reads=["gb0"], writes=[XNk(pr, h)])
            p.op("vector", lambda e: e.tensor_scalar(kot[i][:, 0:H], kot[i][:, 0:H],
                                                     stE[i][:, 1:2], None, A.mult),
                 reads=[KOk, SEk], writes=[KOk], narrow=True)
            p.op("vector", lambda e: e.tensor_scalar(kot[i][:, H:C], kot[i][:, H:C],
                                                     stE[i][:, 2:3], None, A.mult),
                 reads=[KOk, SEk], writes=[KOk], narrow=True)
            p.op("tensor", lambda e: e.transpose(ps_T, kot[i][:, :], ident[:, :]),
                 reads=[KOk, "ident", "gb1"], writes=["gb1"])
            p.op("scalar", lambda e: e.copy(out=kTs[i][:, :], in_=ps_T),
                 reads=["gb1"], writes=[KTk])
            p.op("tensor", lambda e: e.matmul(ps_S, kTs[i][:, :], vTs[i][:, :],
                                              start=True, stop=True),
                 reads=[KTk, VTk, "gb1"], writes=["gb1"])
            p.op("vector", lambda e, S=Sh: e.tensor_scalar(S, S, stE[i][:, 0:1], None, A.mult),
                 reads=[SK[h], SEk], writes=[SK[h]])
            p.op("vector", lambda e, S=Sh: e.tensor_tensor(S, S, ps_S, A.add),
                 reads=[SK[h], "gb1"], writes=[SK[h]])

    def dbg_dump(name, buf, keys):
        if name in dbg_t:
            w = buf[:, :].shape[1]
            p.op("sync", lambda e, d=dbg_t[name], s=buf, w=w: e.dma_start(
                out=d[:, 0:w], in_=s[:, :]),
                reads=list(keys), writes=[], dma=True, stream="dbg")

    def phaseA(tile):
        if tile >= NTILES:
            return
        pr = tile % 2
        halo_tile = (tile == 0)
        src = x_fm.rearrange("(c p) t -> p c t", p=128)[:, :, tile * TT:(tile + 1) * TT]
        dst = xres[pr][:, :].rearrange("p (c t) -> p c t", t=TT)
        p.op("sync", lambda e, s=src, d=dst: e.dma_start(out=d, in_=s),
             reads=[XRk(pr, c) for c in range(NCH)],
             writes=[XRk(pr, c) for c in range(NCH)], dma=True,
             stream=f"dx{pr}")
        yield
        ln_stats(pr, xres[pr], lambda c: XRk(pr, c), with_rms=True)
        yield
        ln_apply(pr, xres[pr], lambda c: XRk(pr, c), xno[pr], lambda c: XNk(pr, c))
        yield
        absmax_calc(pr, xno[pr], lambda c: XNk(pr, c))
        yield
        for c in range(NCH):
            quant_chunk(pr, xno[pr], lambda cc: XNk(pr, cc), c)
            if c % 4 == 3:
                yield
        for q in range(4):
            for _ in ffn_quarter(pr, q, "A"):
                yield
        if tile == 1:
            dbg_dump("x1", xres[pr], [XRk(pr, c) for c in range(NCH)])
        ln_stats(pr, xres[pr], lambda c: XRk(pr, c), with_rms=False)
        yield
        ln_apply(pr, xres[pr], lambda c: XRk(pr, c), xno[pr], lambda c: XNk(pr, c))
        yield
        for c in range(NCH):
            eng = "gpsimd" if c % 2 == 0 else "vector"
            p.op(eng, lambda e, c=c, xb=xno[pr], hb=hixq[pr]: e.tensor_copy(
                ch(hb, c), ch(xb, c)),
                reads=[XNk(pr, c)], writes=[HXk(pr, c)])
            p.op(eng, lambda e, c=c, xb=xno[pr], hb=hixq[pr], lb=lobf[pr]: e.tensor_tensor(
                ch(lb, c), ch(xb, c), ch(hb, c), A.subtract),
                reads=[XNk(pr, c), HXk(pr, c)], writes=[LOk(pr, c)])
        yield

        def evac_v(n, ps, pbk):
            p.op("scalar", lambda e, n=n, ps=ps: e.copy(out=ch(vbuf, n), in_=ps),
                 reads=[pbk], writes=[VB[n]])
        for q in range(4):
            for _ in stream_proj(pr, 2, q, evac_v, "A"):
                yield

        def evac_k(n, ps, pbk):
            p.op("scalar", lambda e, n=n, ps=ps: e.copy(out=ch(kraw, n), in_=ps),
                 reads=[pbk], writes=[KR[n]])
        for q in range(4):
            for _ in stream_proj(pr, 1, q, evac_k, "A"):
                yield

        if not halo_tile:
            def evac_q(n, ps, pbk):
                p.op("scalar", lambda e, n=n, ps=ps: e.activation(
                    out=qhalf[:, n * TT:(n + 1) * TT], in_=ps,
                    func=AF.Relu, scale=SCALE),
                    reads=[pbk], writes=[QH[n]])
            for q in range(4):
                for _ in stream_proj(pr, 0, q, evac_q, "A"):
                    yield

    def phaseB(tile):
        pr = tile % 2
        halo_tile = (tile == 0)
        if halo_tile:
            for h in range(NH):
                gla_head(pr, h, has_q=False, tile_idx=tile)
                yield
            return
        for h in range(NH):
            gla_head(pr, h, has_q=True, tile_idx=tile)
            yield
        if tile == 1:
            dbg_dump("o1", xno[pr], [XNk(pr, c) for c in range(NCH)])
        for c in range(NCH):
            slot = c % 2
            p.op("scalar", lambda e, c=c, s=slot, ob=xno[pr]: e.activation(
                out=sq[:, s * TT:(s + 1) * TT], in_=ch(ob, c), func=AF.Square),
                reads=[XNk(pr, c)], writes=[f"sq{slot}"])
            p.op("tensor", lambda e, c=c, s=slot: e.matmul(
                ps_row2, ones_col, sq[:, s * TT:(s + 1) * TT],
                start=(c == 0), stop=(c == NCH - 1)),
                reads=[f"sq{slot}", "onesc"], writes=["gb7"])
        p.op("vector", lambda e: e.tensor_scalar(row(6), ps_row2, 1.0 / 2048.0, None, A.mult),
             reads=["gb7"], writes=[rkey(6)], narrow=True)
        p.op("vector", lambda e: e.tensor_scalar(row(6), row(6), EPS_ATTN, None, A.add),
             reads=[rkey(6)], writes=[rkey(6)], narrow=True)
        p.op("scalar", lambda e: e.activation(out=row(7), in_=row(6), func=AF.Ln),
             reads=[rkey(6)], writes=[rkey(7)])
        p.op("vector", lambda e: e.tensor_scalar(row(7), row(7), -0.5, None, A.mult),
             reads=[rkey(7)], writes=[rkey(7)], narrow=True)
        p.op("scalar", lambda e: e.activation(out=row(6), in_=row(7), func=AF.Exp),
             reads=[rkey(7)], writes=[rkey(6)])
        p.op("tensor", lambda e: e.matmul(ps_bc, ones_row, row(6), start=True, stop=True),
             reads=[rkey(6), "onesc", "gb6"], writes=["gb6"])
        p.op("scalar", lambda e, d=rb[pr]: e.copy(out=d[:, :], in_=ps_bc),
             reads=["gb6"], writes=[f"rb{pr}"])
        yield
        for c in range(NCH):
            slot = c % 2
            p.op("vector", lambda e, c=c, s=slot, ob=xno[pr]: e.tensor_tensor(
                ev[:, s * TT:(s + 1) * TT], ch(ob, c), rb[pr][:, :], A.mult),
                reads=[XNk(pr, c), f"rb{pr}"], writes=[f"ev{slot}"])
            p.op("vector", lambda e, c=c, s=slot: e.tensor_scalar(
                ev2[:, s * TT:(s + 1) * TT], ev[:, s * TT:(s + 1) * TT],
                gwb[:, c:c + 1], None, A.mult),
                reads=[f"ev{slot}", "gwb"], writes=[f"fe{slot}"])
            p.op("gpsimd" if c % 2 == 0 else "vector",
                 lambda e, c=c, s=slot, hb=hixq[pr]: e.tensor_copy(
                ch(hb, c), ev2[:, s * TT:(s + 1) * TT]),
                reads=[f"fe{slot}"], writes=[HXk(pr, c)])
            p.op("gpsimd" if c % 2 == 0 else "vector",
                 lambda e, c=c, s=slot, hb=hixq[pr], lb=lobf[pr]: e.tensor_tensor(
                ch(lb, c), ev2[:, s * TT:(s + 1) * TT], ch(hb, c), A.subtract),
                reads=[f"fe{slot}", HXk(pr, c)], writes=[LOk(pr, c)])
        yield

        def evac_wo(n, ps, pbk):
            p.op("vector", lambda e, n=n, ps=ps, xb=xres[pr]: e.tensor_tensor(
                ch(xb, n), ch(xb, n), ps, A.add),
                reads=[pbk, XRk(pr, n)], writes=[XRk(pr, n)])
        for q in range(4):
            for _ in stream_proj(pr, 3, q, evac_wo, "B"):
                yield

        ln_stats(pr, xres[pr], lambda c: XRk(pr, c), with_rms=True)
        yield
        ln_apply(pr, xres[pr], lambda c: XRk(pr, c), xno[pr], lambda c: XNk(pr, c))
        yield
        absmax_calc(pr, xno[pr], lambda c: XNk(pr, c))
        yield
        for c in range(NCH):
            quant_chunk(pr, xno[pr], lambda cc: XNk(pr, cc), c)
            if c % 4 == 3:
                yield
        for q in range(4):
            for _ in ffn_quarter(pr, q, "B"):
                yield

        osrc = xres[pr][:, :].rearrange("p (c t) -> p c t", t=TT)
        odst = y_fm.rearrange("(c p) t -> p c t", p=128)[
            :, :, (tile - 1) * TT:tile * TT]
        p.op("sync", lambda e, s=osrc, d=odst: e.dma_start(out=d, in_=s),
             reads=[XRk(pr, c) for c in range(NCH)] + ["outdma"],
             writes=["outdma"], dma=True, stream="do")
        yield

    def drain(gen):
        for _ in gen:
            pass

    drain(phaseA(0))
    PRE = 8
    for tile in range(NTILES):
        gb = phaseB(tile)
        ga = phaseA(tile + 1)
        for _ in range(PRE):
            if next(ga, StopIteration) is StopIteration:
                break
        while True:
            db = next(gb, StopIteration)
            da = next(ga, StopIteration)
            if db is StopIteration and da is StopIteration:
                break

    p.op("sync", lambda e: e.nop(),
         reads=["outdma"] + [XRk(pp, c) for pp in (0, 1) for c in range(NCH)] + SK,
         writes=[])

    emit_engine = p.emit(nc, sems)
    with nc.Block() as blk:
        blk.sync(lambda e: emit_engine("sync", e))
        blk.tensor(lambda e: emit_engine("tensor", e))
        blk.vector(lambda e: emit_engine("vector", e))
        blk.scalar(lambda e: emit_engine("scalar", e))
        blk.gpsimd(lambda e: emit_engine("gpsimd", e))
    es.close()
    return nc


def kernel(L0_attn_hidden_states, L0_attn_Wq, L0_attn_Wk, L0_attn_Wi, L0_attn_Wo,
           L0_attn_g_weight, L0_ffn_weight, L0_ffn_bias):
    x = np.asarray(L0_attn_hidden_states, dtype=np.float32)
    wq = np.ascontiguousarray(np.asarray(L0_attn_Wq, dtype=np.float32))
    wk = np.ascontiguousarray(np.asarray(L0_attn_Wk, dtype=np.float32))
    wi = np.ascontiguousarray(np.asarray(L0_attn_Wi, dtype=np.float32))
    wo = np.ascontiguousarray(np.asarray(L0_attn_Wo, dtype=np.float32))
    fw = np.ascontiguousarray(np.asarray(L0_ffn_weight, dtype=np.float32))
    fb = np.asarray(L0_ffn_bias, dtype=np.float32)
    gw = np.asarray(L0_attn_g_weight, dtype=np.float32)

    fbias = np.ascontiguousarray(fb.reshape(NCH, 128).T)
    gwin = np.ascontiguousarray(gw.reshape(NCH, 128).T)
    ident = np.eye(128, dtype=np.float32)
    causal = np.triu(np.ones((128, 128), dtype=np.uint8))
    onesc = np.ones((128, 128), dtype=np.float32)

    nc = build_program()
    in_maps = []
    for ci in range(NCORES):
        bidx, j = ci // 4, ci % 4
        t0 = j * TCORE
        if j == 0:
            xs = np.concatenate([np.zeros((HALO, D), np.float32), x[bidx, 0:TCORE]], axis=0)
            hm = np.zeros((128, 2), np.float32)
        else:
            xs = x[bidx, t0 - HALO:t0 + TCORE]
            hm = np.ones((128, 2), np.float32)
        in_maps.append({
            "x_fm": np.ascontiguousarray(xs.T),
            "wq": wq, "wk": wk, "wi": wi, "wo": wo, "fw": fw,
            "fbias": fbias, "gwin": gwin, "ident_in": ident,
            "causal_in": causal, "halom_in": hm, "onesc_in": onesc,
        })
    res = run_bass_kernel_spmd(nc, in_maps, core_ids=list(range(NCORES)))
    out = np.empty((B, L, D), dtype=np.float32)
    for ci in range(NCORES):
        bidx, j = ci // 4, ci % 4
        out[bidx, j * TCORE:(j + 1) * TCORE, :] = res.results[ci]["y_fm"].T
    if _DBG:
        kernel.dbg = [{k: res.results[ci]["dbg_" + k] for k in _DBG}
                      for ci in range(NCORES)]
    return out


# version 12
# speedup vs baseline: 1.1307x; 1.1307x over previous
import numpy as np
import concourse.bass as bass
import concourse.mybir as mybir
from concourse.bass_utils import run_bass_kernel_spmd

F32 = mybir.dt.float32
BF16 = mybir.dt.bfloat16
U8 = mybir.dt.uint8
A = mybir.AluOpType
AF = mybir.ActivationFunctionType
AX = mybir.AxisListType

D = 2048
NCH = D // 128
NH = 16
B = 2
L = 8192
NCORES = 8
HALO = 256
TCORE = 2048
TTOT = HALO + TCORE
TT = 256
NTILES = TTOT // TT
C = 128
SCALE = 128.0 ** -0.5
EPS_LN = 1e-5
EPS_RMS_FFN = 1e-8
EPS_FFN = 1e-8
EPS_ATTN = 1e-5
QB = 127.0
LADDER_BITS = [128.0, 64.0, 32.0, 16.0, 8.0, 4.0, 2.0, 1.0]
WT_BITS = [4.0, 2.0, 1.0]

_DBG = []

SK = [f"S{h}" for h in range(NH)]
QH = [f"qh{j}" for j in range(NCH)]
KR = [f"kr{c}" for c in range(NCH)]
VB = [f"vb{c}" for c in range(NCH)]


class Plan:
    def __init__(self):
        self.ops = []
        self.writer = {}
        self.readers = {}

    def op(self, engine, fn, reads=(), writes=(), dma=False, narrow=False,
           stream=None):
        deps = {}
        for bb in reads:
            w = self.writer.get(bb)
            if w is not None:
                deps[w] = "rw"
        for bb in writes:
            w = self.writer.get(bb)
            if w is not None:
                deps[w] = "rw"
        for bb in writes:
            for r in self.readers.get(bb, ()):
                deps.setdefault(r, "war")
        i = len(self.ops)
        assert (stream is not None) == dma, "DMA ops need a stream name"
        self.ops.append(dict(engine=engine, fn=fn, deps=deps, dma=dma,
                             narrow=narrow, stream=stream))
        for bb in reads:
            self.readers.setdefault(bb, []).append(i)
        for bb in writes:
            self.writer[bb] = i
            self.readers[bb] = []
        return i

    def emit(self, nc, sems):
        ops = self.ops
        engseq = [0] * len(ops)
        cnt = {}
        for i, o in enumerate(ops):
            engseq[i] = cnt.get(o["engine"], 0)
            cnt[o["engine"]] = engseq[i] + 1
        interlock = [set() for _ in ops]
        needs_inc = [False] * len(ops)
        for i, o in enumerate(ops):
            for d, typ in o["deps"].items():
                if ops[d]["dma"] or ops[d]["engine"] != o["engine"]:
                    needs_inc[d] = True
                elif typ == "rw" and o["engine"] != "tensor":
                    needs_inc[d] = True
                    interlock[i].add(d)
        for i, o in enumerate(ops):
            if o["dma"]:
                needs_inc[i] = True
        last_of = {}
        for i, o in enumerate(ops):
            last_of[o["engine"]] = i
        for i in last_of.values():
            needs_inc[i] = True

        def stream_of(i):
            return ops[i]["stream"] if ops[i]["dma"] else ops[i]["engine"]

        cum = {}
        opcum = [0] * len(ops)
        pfx = [dict() for _ in ops]
        run_dma = {}
        for i, o in enumerate(ops):
            pfx[i] = dict(run_dma)
            st = stream_of(i)
            amt = 16 if o["dma"] else 1
            if needs_inc[i]:
                cum[st] = cum.get(st, 0) + amt
                if o["dma"]:
                    run_dma[st] = cum[st]
            opcum[i] = cum.get(st, 0)
        hiwater = {}

        def emit_engine(eng_name, eng):
            hw = hiwater.setdefault(eng_name, {})
            for i, o in enumerate(ops):
                if o["engine"] != eng_name:
                    continue
                want = {}
                for d in o["deps"]:
                    do = ops[d]
                    if (do["engine"] == eng_name and not do["dma"]
                            and d not in interlock[i]):
                        continue
                    st = stream_of(d)
                    if do["dma"]:
                        val = pfx[i].get(st, 0)
                    else:
                        val = opcum[d]
                    if want.get(st, 0) < val:
                        want[st] = val
                for st in sorted(want):
                    val = want[st]
                    if hw.get(st, 0) < val:
                        eng.wait_ge(sems[st], val)
                        hw[st] = val
                ins = o["fn"](eng)
                if needs_inc[i]:
                    ins.then_inc(sems[stream_of(i)], 16 if o["dma"] else 1)
        return emit_engine


def _ladder_floor(p, v_, u0, uu, u2, bits, keys, pre_add, eng="vector"):
    kv, k0, ku, k2 = keys
    p.op(eng, lambda e, a=v_, o=u0, c=pre_add: e.tensor_scalar(o, a, c, None, A.add),
         reads=[kv], writes=[k0])
    first = True
    for bit in bits:
        src = u0 if first else uu
        ksrc = k0 if first else ku
        p.op(eng, lambda e, s=src, o=u2, bb=bit: e.tensor_scalar(o, s, bb, -bb, A.is_ge, A.mult),
             reads=[ksrc], writes=[k2])
        p.op(eng, lambda e, s=src, o=uu, c=u2: e.tensor_tensor(o, s, c, A.add),
             reads=[ksrc, k2], writes=[ku])
        first = False
    p.op(eng, lambda e, o=u2, x=u0, y=uu: e.tensor_tensor(o, x, y, A.subtract),
         reads=[k0, ku], writes=[k2])


def build_program():
    nc = bass.Bass("TRN2", target_bir_lowering=False)

    x_fm = nc.dram_tensor("x_fm", [D, TTOT], F32, kind="ExternalInput").ap()
    wq = nc.dram_tensor("wq", [D, D], F32, kind="ExternalInput").ap()
    wk = nc.dram_tensor("wk", [D, D], F32, kind="ExternalInput").ap()
    wi = nc.dram_tensor("wi", [D, D], F32, kind="ExternalInput").ap()
    wo = nc.dram_tensor("wo", [D, D], F32, kind="ExternalInput").ap()
    fw = nc.dram_tensor("fw", [D, D], F32, kind="ExternalInput").ap()
    fbias = nc.dram_tensor("fbias", [128, NCH], F32, kind="ExternalInput").ap()
    gwin = nc.dram_tensor("gwin", [128, NCH], F32, kind="ExternalInput").ap()
    ident_in = nc.dram_tensor("ident_in", [128, 128], F32, kind="ExternalInput").ap()
    causal_in = nc.dram_tensor("causal_in", [128, 128], U8, kind="ExternalInput").ap()
    halom_in = nc.dram_tensor("halom_in", [128, 2], F32, kind="ExternalInput").ap()
    onesc_in = nc.dram_tensor("onesc_in", [128, 128], F32, kind="ExternalInput").ap()
    y_fm = nc.dram_tensor("y_fm", [D, TCORE], F32, kind="ExternalOutput").ap()
    wscr = [nc.dram_tensor(f"wscr{i}", [2 * D, D], BF16, kind="Internal").ap()
            for i in range(4)]
    wtscr = nc.dram_tensor("wtscr", [NCH * 128, D], BF16, kind="Internal").ap()
    dbg_t = {}
    for name in _DBG:
        dbg_t[name] = nc.dram_tensor("dbg_" + name, [128, 4096], F32,
                                     kind="ExternalOutput").ap()

    from contextlib import ExitStack
    es = ExitStack()

    def sb(name, shape, dt=F32):
        return es.enter_context(nc.sbuf_tensor(name, shape, dt))

    xres = [sb(f"xres{i}", [128, NCH * TT]) for i in range(2)]
    xno = [sb(f"xno{i}", [128, NCH * TT]) for i in range(2)]
    hixq = [sb(f"hixq{i}", [128, NCH * TT], BF16) for i in range(2)]
    lobf = [sb(f"lobf{i}", [128, NCH * TT], BF16) for i in range(2)]
    vbuf = sb("vbuf", [128, NCH * TT])
    kraw = sb("kraw", [128, NCH * TT])
    wslA = [sb(f"wslA{i}", [128, 1024], BF16) for i in range(3)]
    wslB = [sb(f"wslB{i}", [128, 1024], BF16) for i in range(2)]
    fwslA = [sb(f"fwslA{i}", [128, 512], BF16) for i in range(3)]
    fwslB = [sb(f"fwslB{i}", [128, 512], BF16) for i in range(3)]
    qhalf = sb("qhalf", [128, NCH * TT])
    kh = [sb(f"kh{i}", [128, TT]) for i in range(2)]
    gh = [sb(f"gh{i}", [128, TT]) for i in range(2)]
    th = [sb(f"th{i}", [128, TT]) for i in range(2)]
    bt = [sb(f"bt{i}", [128, C]) for i in range(2)]
    e1 = [sb(f"e1{i}", [128, C]) for i in range(2)]
    e2 = [sb(f"e2{i}", [128, C]) for i in range(2)]
    qint = [sb(f"qint{i}", [128, C]) for i in range(2)]
    qitr = [sb(f"qitr{i}", [128, C]) for i in range(2)]
    kot = [sb(f"kot{i}", [128, C]) for i in range(2)]
    Pp = [sb(f"Pp{i}", [128, C]) for i in range(2)]
    kTs = [sb(f"kTs{i}", [128, C]) for i in range(2)]
    vTs = [sb(f"vTs{i}", [128, C]) for i in range(2)]
    st8 = [sb(f"st8{i}", [128, 4]) for i in range(2)]
    stE = [sb(f"stE{i}", [128, 4]) for i in range(2)]
    Sst = sb("Sst", [128, NH * 128])
    sq = sb("sq", [128, 2 * TT])
    u0 = sb("u0", [128, TT])
    uu = sb("uu", [128, TT])
    u2 = sb("u2", [128, TT])
    rows = sb("rows", [1, 6 * TT])
    colsb = sb("colsb", [128, 16])
    rb = [sb(f"rb{i}", [128, TT]) for i in range(2)]
    ab = [sb(f"ab{i}", [128, TT]) for i in range(2)]
    xsb = [sb(f"xsb{i}", [128, TT]) for i in range(2)]
    invsb = [sb(f"invsb{i}", [128, TT]) for i in range(2)]
    gamma = sb("gamma", [128, NCH])
    hbias = sb("hbias", [128, NCH])
    gwb = sb("gwb", [128, NCH])
    ident = sb("ident", [128, 128])
    causal = sb("causal", [128, 128], U8)
    halom = sb("halom", [128, 2])
    onesc = sb("onesc", [128, 128])
    ev = sb("ev", [128, 2 * TT])
    ev2 = sb("ev2", [128, 2 * TT])

    PS = es.enter_context(nc.psum_tensor("PS", [128, 4096], F32))
    ps_A = PS[:, 0:128]
    ps_O = PS[:, 256:384]
    ps_T = PS[:, 512:640]
    ps_S = PS[:, 768:896]
    ps_bkA = [PS[:, (2 + j // 2) * 512 + (j % 2) * TT:
                 (2 + j // 2) * 512 + (j % 2) * TT + TT] for j in range(4)]
    ps_bkB = [PS[:, (4 + j // 2) * 512 + (j % 2) * TT:
                 (4 + j // 2) * 512 + (j % 2) * TT + TT] for j in range(4)]
    PBKA = [f"pb{2 + j // 2}" for j in range(4)]
    PBKB = [f"pb{4 + j // 2}" for j in range(4)]
    ps_tr = PS[:, 6 * 512:6 * 512 + 128]
    ps_rr = PS[0:1, 6 * 512 + 128:6 * 512 + 256]
    ps_bc = PS[:, 6 * 512 + 256:6 * 512 + 512]
    ps_row1 = PS[0:1, 7 * 512:7 * 512 + TT]
    ps_row2 = PS[0:1, 7 * 512 + TT:7 * 512 + 2 * TT]

    sems = {}
    for s in ["sync", "tensor", "vector", "scalar", "gpsimd",
              "dc", "dpl", "dps", "dgp", "dx0", "dx1", "do",
              "wa0", "wa1", "wa2", "wb0", "wb1", "wb2",
              "fa0", "fa1", "fa2", "fb0", "fb1", "fb2",
              "dpg", "dp2", "dp3", "dsc", "dbg"]:
        sems[s] = es.enter_context(nc.semaphore("sem_" + s))

    p = Plan()

    def ch(buf, c, w=TT):
        return buf[:, c * w:(c + 1) * w]

    ROWMAP = {0: 0, 1: 1, 2: 2, 3: 3, 4: 4, 5: 5, 6: 4, 7: 5}

    def row(i):
        s = ROWMAP[i]
        return rows[0:1, s * TT:(s + 1) * TT]

    def rkey(i):
        return f"r{ROWMAP[i]}"

    for dst, src, key in [(ident, ident_in, "ident"), (causal, causal_in, "causal"),
                          (halom, halom_in, "halom"), (onesc, onesc_in, "onesc"),
                          (gwb, gwin, "gwb"), (hbias, fbias, "hbias_raw")]:
        p.op("sync", lambda e, d=dst, s=src: e.dma_start(out=d[:, :], in_=s),
             writes=[key], dma=True, stream="dc")
    p.op("vector", lambda e: e.tensor_scalar(hbias[:, :], hbias[:, :], 0.5, None, A.mult),
         reads=["hbias_raw"], writes=["hbias"], narrow=True)
    for i in range(2):
        p.op("vector", lambda e, i=i: e.memset(Pp[i][:, :], 0.0), writes=[f"Ppz{i}"])
    p.op("vector", lambda e: e.memset(Sst[:, :], 0.0), writes=SK)

    ones_col = onesc[:, 0:1]
    ones_row = onesc[0:1, :]

    PBUF = [vbuf[:, 0:D], vbuf[:, D:2 * D], kraw[:, 0:D], kraw[:, D:2 * D]]
    PLD = [("sync", "dpl"), ("sync", "dp2"), ("gpsimd", "dpg"), ("gpsimd", "dp3")]
    for widx, wsrc in enumerate([wq, wk, wi, wo]):
        for k in range(NCH):
            m = widx * NCH + k
            par = m % 4
            hpar = m % 2
            slab = PBUF[par]
            hih = hixq[0][:, hpar * D:(hpar + 1) * D]
            loh = lobf[0][:, hpar * D:(hpar + 1) * D]
            wkk, hik, lok = f"prep_w{par}", f"prep_hi{hpar}", f"prep_lo{hpar}"
            eng, st = PLD[par]
            p.op(eng, lambda e, s=wsrc, k=k, sl=slab: e.dma_start(
                 out=sl, in_=s[k * 128:(k + 1) * 128, :]),
                 reads=[], writes=[wkk], dma=True, stream=st)
            p.op("vector", lambda e, sl=slab, hh=hih: e.tensor_copy(hh, sl),
                 reads=[wkk], writes=[hik])
            p.op("vector", lambda e, sl=slab, hh=hih, ll=loh: e.tensor_tensor(
                 ll, sl, hh, A.subtract),
                 reads=[wkk, hik], writes=[lok])
            for nh in range(2):
                r0 = nh * D + k * 128
                for qq in range(2):
                    c0 = hpar * D + nh * 1024 + qq * 512
                    sst = "dps" if (nh + qq) % 2 == 0 else "dsc"
                    seng = "sync" if sst == "dps" else "scalar"
                    p.op(seng, lambda e, r0=r0, w=widx, qq=qq, c0=c0: e.dma_start(
                        out=wscr[w][r0:r0 + 128, qq * 1024:qq * 1024 + 512],
                        in_=hixq[0][:, c0:c0 + 512]),
                        reads=[hik], writes=[f"wscr{widx}"], dma=True, stream=sst)
                    p.op(seng, lambda e, r0=r0, w=widx, qq=qq, c0=c0: e.dma_start(
                        out=wscr[w][r0:r0 + 128, qq * 1024 + 512:qq * 1024 + 1024],
                        in_=lobf[0][:, c0:c0 + 512]),
                        reads=[lok], writes=[f"wscr{widx}"], dma=True, stream=sst)

    for n in range(NCH):
        par = n % 2
        slab = vbuf[:, par * D:(par + 1) * D]
        wkk = f"prep_w{par}"
        p.op("sync" if par == 0 else "gpsimd",
             lambda e, n=n, sl=slab: e.dma_start(
             out=sl, in_=fw[n * 128:(n + 1) * 128, :]),
             reads=[f"prep_hi{par % 2}", f"prep_lo{par % 2}"], writes=[wkk],
             dma=True, stream="dpl" if par == 0 else "dpg")
        p.op("vector", lambda e, sl=slab: e.reduce_sum(colsb[:, 1:2], sl, AX.X,
                                                       apply_absolute_value=True),
             reads=[wkk], writes=["c1"], narrow=True)
        p.op("vector", lambda e, n=n: e.tensor_scalar(gamma[:, n:n + 1], colsb[:, 1:2],
                                                      1.0 / 2048.0, None, A.mult),
             reads=["c1"], writes=["gamma"], narrow=True)
        p.op("vector", lambda e, n=n: e.tensor_scalar(colsb[:, 2:3], gamma[:, n:n + 1],
                                                      EPS_FFN, None, A.add),
             reads=["gamma"], writes=["c2"], narrow=True)
        p.op("vector", lambda e: e.reciprocal(colsb[:, 0:1], colsb[:, 2:3]),
             reads=["c2"], writes=["c0"], narrow=True)
        p.op("vector", lambda e, sl=slab: e.tensor_scalar(xno[0][:, 0:D], sl,
                                                          colsb[:, 0:1], None, A.mult),
             reads=[wkk, "c0"], writes=["prep_z"])
        p.op("vector", lambda e: e.tensor_scalar(kraw[:, 0:D], xno[0][:, 0:D],
                                                 0.5, None, A.is_ge),
             reads=["prep_z"], writes=["prep_w2"])
        p.op("vector", lambda e: e.tensor_scalar(kraw[:, D:2 * D], xno[0][:, 0:D],
                                                 -0.5, None, A.is_ge),
             reads=["prep_z"], writes=["prep_w3"])
        p.op("vector", lambda e: e.tensor_tensor(kraw[:, 0:D], kraw[:, 0:D],
                                                 kraw[:, D:2 * D], A.add),
             reads=["prep_w2", "prep_w3"], writes=["prep_w2"])
        p.op("vector", lambda e: e.tensor_scalar(kraw[:, 0:D], kraw[:, 0:D],
                                                 -1.0, None, A.add),
             reads=["prep_w2"], writes=["prep_w2"])
        for j in range(NCH):
            p.op("tensor", lambda e, j=j: e.transpose(ps_tr, kraw[:, j * 128:(j + 1) * 128],
                                                      ident[:, :]),
                 reads=["prep_w2", "ident", "gb6"], writes=["gb6"])
            p.op("scalar", lambda e, j=j: e.copy(
                out=hixq[0][:, j * 128:(j + 1) * 128], in_=ps_tr),
                reads=["gb6"], writes=["prep_hi0"])
        for j in range(NCH):
            p.op("gpsimd", lambda e, j=j, n=n: e.dma_start(
                out=wtscr[j * 128:(j + 1) * 128, n * 128:(n + 1) * 128],
                in_=hixq[0][:, j * 128:(j + 1) * 128]),
                reads=["prep_hi0"], writes=["wtscrk"], dma=True, stream="dgp")

    allkeys = (SK + QH + KR + VB
               + [f"{nm}{pp}_{c}" for nm in ("xr", "xn", "hx", "lo")
                  for pp in (0, 1) for c in range(NCH)]
               + ["lq", "l0", "l1", "l2"])
    p.op("sync", lambda e: e.nop(),
         reads=[],
         writes=allkeys + ["prep_w0", "prep_w1", "prep_w2", "prep_w3",
                           "prep_hi0", "prep_hi1",
                           "prep_lo0", "prep_lo1", "prep_z"])

    def XRk(pr, c):
        return f"xr{pr}_{c}"

    def XNk(pr, c):
        return f"xn{pr}_{c}"

    def HXk(pr, c):
        return f"hx{pr}_{c}"

    def LOk(pr, c):
        return f"lo{pr}_{c}"

    def ln_stats(pr, srcbuf, srck, with_rms):
        for c in range(NCH):
            slot = c % 2
            p.op("scalar", lambda e, c=c, s=slot, sbf=srcbuf: e.activation(
                out=sq[:, s * TT:(s + 1) * TT], in_=ch(sbf, c), func=AF.Square),
                reads=[srck(c)], writes=[f"sq{slot}"])
            p.op("tensor", lambda e, c=c, sbf=srcbuf: e.matmul(
                ps_row1, ones_col, ch(sbf, c), start=(c == 0), stop=False),
                reads=[srck(c), "onesc"], writes=["gb7"])
            p.op("tensor", lambda e, c=c, s=slot: e.matmul(
                ps_row2, ones_col, sq[:, s * TT:(s + 1) * TT],
                start=False, stop=(c == NCH - 1)),
                reads=[f"sq{slot}", "onesc"], writes=["gb7"])
        p.op("vector", lambda e: e.tensor_scalar(row(0), ps_row1, 1.0 / 2048.0, None, A.mult),
             reads=["gb7"], writes=[rkey(0)], narrow=True)
        p.op("vector", lambda e: e.tensor_scalar(row(1), ps_row2, 1.0 / 2048.0, None, A.mult),
             reads=["gb7"], writes=[rkey(1)], narrow=True)
        p.op("vector", lambda e: e.tensor_tensor(row(6), row(0), row(0), A.mult),
             reads=[rkey(0)], writes=[rkey(6)], narrow=True)
        p.op("vector", lambda e: e.tensor_tensor(row(2), row(1), row(6), A.subtract),
             reads=[rkey(1), rkey(6)], writes=[rkey(2)], narrow=True)
        p.op("vector", lambda e: e.tensor_scalar(row(6), row(2), EPS_LN, None, A.add),
             reads=[rkey(2)], writes=[rkey(6)], narrow=True)
        p.op("scalar", lambda e: e.activation(out=row(7), in_=row(6), func=AF.Ln),
             reads=[rkey(6)], writes=[rkey(7)])
        p.op("vector", lambda e: e.tensor_scalar(row(7), row(7), -0.5, None, A.mult),
             reads=[rkey(7)], writes=[rkey(7)], narrow=True)
        p.op("scalar", lambda e: e.activation(out=row(3), in_=row(7), func=AF.Exp),
             reads=[rkey(7)], writes=[rkey(3)])
        if with_rms:
            p.op("vector", lambda e: e.tensor_tensor(row(6), row(3), row(3), A.mult),
                 reads=[rkey(3)], writes=[rkey(6)], narrow=True)
            p.op("vector", lambda e: e.tensor_tensor(row(6), row(2), row(6), A.mult),
                 reads=[rkey(2), rkey(6)], writes=[rkey(6)], narrow=True)
            p.op("vector", lambda e: e.tensor_scalar(row(6), row(6), EPS_RMS_FFN, None, A.add),
                 reads=[rkey(6)], writes=[rkey(6)], narrow=True)
            p.op("scalar", lambda e: e.activation(out=row(7), in_=row(6), func=AF.Ln),
                 reads=[rkey(6)], writes=[rkey(7)])
            p.op("vector", lambda e: e.tensor_scalar(row(7), row(7), -0.5, None, A.mult),
                 reads=[rkey(7)], writes=[rkey(7)], narrow=True)
            p.op("scalar", lambda e: e.activation(out=row(6), in_=row(7), func=AF.Exp),
                 reads=[rkey(7)], writes=[rkey(6)])
            p.op("vector", lambda e: e.tensor_tensor(row(4), row(3), row(6), A.mult),
                 reads=[rkey(3), rkey(6)], writes=[rkey(4)], narrow=True)
        else:
            p.op("vector", lambda e: e.tensor_copy(row(4), row(3)),
                 reads=[rkey(3)], writes=[rkey(4)], narrow=True)
        p.op("vector", lambda e: e.tensor_tensor(row(5), row(0), row(4), A.mult),
             reads=[rkey(0), rkey(4)], writes=[rkey(5)], narrow=True)
        p.op("tensor", lambda e: e.matmul(ps_bc, ones_row, row(4), start=True, stop=True),
             reads=[rkey(4), "onesc", "gb6"], writes=["gb6"])
        p.op("scalar", lambda e, d=rb[pr]: e.copy(out=d[:, :], in_=ps_bc),
             reads=["gb6"], writes=[f"rb{pr}"])
        p.op("tensor", lambda e: e.matmul(ps_bc, ones_row, row(5), start=True, stop=True),
             reads=[rkey(5), "onesc", "gb6"], writes=["gb6"])
        p.op("scalar", lambda e, d=ab[pr]: e.copy(out=d[:, :], in_=ps_bc),
             reads=["gb6"], writes=[f"ab{pr}"])

    def ln_apply(pr, srcbuf, srck, dstbuf, dstk):
        for c in range(NCH):
            eng = "gpsimd" if c % 2 == 0 else "vector"
            p.op(eng, lambda e, c=c, s=srcbuf, d=dstbuf: e.tensor_tensor(
                ch(d, c), ch(s, c), rb[pr][:, :], A.mult),
                reads=[srck(c), f"rb{pr}"], writes=[dstk(c)])
            p.op(eng, lambda e, c=c, d=dstbuf: e.tensor_tensor(
                ch(d, c), ch(d, c), ab[pr][:, :], A.subtract),
                reads=[dstk(c), f"ab{pr}"], writes=[dstk(c)])

    def absmax_calc(pr, srcbuf, srck):
        p.op("vector", lambda e, s=srcbuf: e.tensor_tensor(u0[:, :], ch(s, 0), ch(s, 1), A.max),
             reads=[srck(0), srck(1)], writes=["l0"])
        p.op("gpsimd", lambda e, s=srcbuf: e.tensor_copy(u2[:, :], ch(s, 0)),
             reads=[srck(0)], writes=["l2"])
        for c in range(1, NCH):
            if c > 1:
                p.op("vector", lambda e, c=c, s=srcbuf: e.tensor_tensor(
                    u0[:, :], u0[:, :], ch(s, c), A.max),
                    reads=[srck(c), "l0"], writes=["l0"])
            p.op("vector", lambda e, c=c, s=srcbuf: e.tensor_tensor(
                u2[:, :], u2[:, :], ch(s, c), A.min),
                reads=[srck(c), "l2"], writes=["l2"])
        for j in range(TT // 128):
            p.op("tensor", lambda e, j=j: e.transpose(ps_tr, u0[:, j * 128:(j + 1) * 128],
                                                      ident[:, :]),
                 reads=["l0", "ident", "gb6"], writes=["gb6"])
            p.op("vector", lambda e, j=j: e.reduce_max(colsb[:, j:j + 1], ps_tr, AX.X),
                 reads=["gb6"], writes=["cmax"], narrow=True)
            p.op("tensor", lambda e, j=j: e.transpose(ps_tr, u2[:, j * 128:(j + 1) * 128],
                                                      ident[:, :]),
                 reads=["l2", "ident", "gb6"], writes=["gb6"])
            p.op("vector", lambda e, j=j: e.tensor_reduce(colsb[:, 2 + j:3 + j], ps_tr, AX.X, A.min),
                 reads=["gb6"], writes=["cmin"], narrow=True)
        for j in range(TT // 128):
            p.op("vector", lambda e, j=j: e.tensor_scalar(colsb[:, 4 + j:5 + j],
                                                          colsb[:, 2 + j:3 + j], -1.0, None, A.mult),
                 reads=["cmin"], writes=["cneg"], narrow=True)
            p.op("vector", lambda e, j=j: e.tensor_tensor(colsb[:, 6 + j:7 + j],
                                                          colsb[:, j:j + 1],
                                                          colsb[:, 4 + j:5 + j], A.max),
                 reads=["cmax", "cneg"], writes=["cam"], narrow=True)
            p.op("vector", lambda e, j=j: e.tensor_scalar(colsb[:, 8 + j:9 + j],
                                                          colsb[:, 6 + j:7 + j],
                                                          1.0 / QB, None, A.mult),
                 reads=["cam"], writes=["cxs"], narrow=True)
            p.op("vector", lambda e, j=j: e.tensor_scalar(colsb[:, 10 + j:11 + j],
                                                          colsb[:, 8 + j:9 + j],
                                                          EPS_FFN, None, A.add),
                 reads=["cxs"], writes=["cpe"], narrow=True)
            p.op("vector", lambda e, j=j: e.reciprocal(
                colsb[:, 12 + j:13 + j], colsb[:, 10 + j:11 + j]),
                reads=["cpe"], writes=["cinv"], narrow=True)
        for j in range(TT // 128):
            p.op("tensor", lambda e, j=j: e.transpose(ps_rr, colsb[:, 8 + j:9 + j], ident[:, :]),
                 reads=["cxs", "ident", "gb6"], writes=["gb6"])
            p.op("scalar", lambda e, j=j: e.copy(out=rows[0:1, j * 128:(j + 1) * 128], in_=ps_rr),
                 reads=["gb6"], writes=["r0"])
            p.op("tensor", lambda e, j=j: e.transpose(ps_rr, colsb[:, 12 + j:13 + j], ident[:, :]),
                 reads=["cinv", "ident", "gb6"], writes=["gb6"])
            p.op("scalar", lambda e, j=j: e.copy(out=rows[0:1, TT + j * 128:TT + (j + 1) * 128],
                                                 in_=ps_rr),
                 reads=["gb6"], writes=["r1"])
        p.op("tensor", lambda e: e.matmul(ps_bc, ones_row, rows[0:1, 0:TT], start=True, stop=True),
             reads=["r0", "onesc", "gb6"], writes=["gb6"])
        p.op("scalar", lambda e, d=xsb[pr]: e.copy(out=d[:, :], in_=ps_bc),
             reads=["gb6"], writes=[f"xsb{pr}"])
        p.op("tensor", lambda e: e.matmul(ps_bc, ones_row, rows[0:1, TT:2 * TT], start=True, stop=True),
             reads=["r1", "onesc", "gb6"], writes=["gb6"])
        p.op("scalar", lambda e, d=invsb[pr]: e.copy(out=d[:, :], in_=ps_bc),
             reads=["gb6"], writes=[f"invsb{pr}"])

    def quant_chunk(pr, srcbuf, srck, c):
        p.op("vector", lambda e, c=c, s=srcbuf: e.tensor_tensor(uu[:, :], ch(s, c),
                                                                invsb[pr][:, :], A.mult),
             reads=[srck(c), f"invsb{pr}"], writes=["lq"])
        _ladder_floor(p, uu[:, :], u0[:, :], uu[:, :], u2[:, :],
                      LADDER_BITS, ("lq", "l0", "lq", "l2"), 128.5)
        p.op("vector", lambda e: e.tensor_scalar(u2[:, :], u2[:, :],
                                                 -128.0, -127.0, A.add, A.max),
             reads=["l2"], writes=["l2"])
        p.op("vector", lambda e, c=c, hb=hixq[pr]: e.tensor_scalar(
            ch(hb, c), u2[:, :], 127.0, None, A.min),
            reads=["l2"], writes=[HXk(pr, c)])

    def ffn_quarter(pr, q, ph):
        fwslX = fwslA if ph == "A" else fwslB
        ps_bkX = ps_bkA if ph == "A" else ps_bkB
        PBKX = PBKA if ph == "A" else PBKB
        fs = "fa" if ph == "A" else "fb"
        for k in range(NCH):
            fslot = k % 3
            fkey = f"{fs}sl{fslot}"
            p.op("sync", lambda e, k=k, s=fslot, q=q, fb=fwslX: e.dma_start(
                out=fb[s][:, :], in_=wtscr[k * 128:(k + 1) * 128,
                                          q * 512:(q + 1) * 512]),
                reads=["wtscrk"], writes=[fkey], dma=True,
                stream=f"{fs}{fslot}")
            for j in range(4):
                p.op("tensor", lambda e, k=k, j=j, s=fslot, fb=fwslX,
                     hb=hixq[pr]: e.matmul(
                    ps_bkX[j], fb[s][:, j * 128:(j + 1) * 128], ch(hb, k),
                    start=(k == 0 and j % 2 == 0),
                    stop=(k == NCH - 1 and j % 2 == 1)),
                    reads=[fkey, HXk(pr, k)], writes=[PBKX[j]])
            if k % 4 == 3:
                yield
        for j in range(4):
            n = q * 4 + j
            slot = n % 2
            p.op("scalar", lambda e, n=n, j=j, s=slot: e.mul(
                ev[:, s * TT:(s + 1) * TT], ps_bkX[j], gamma[:, n:n + 1]),
                reads=[PBKX[j], "gamma"], writes=[f"ev{slot}"])
            p.op("vector", lambda e, n=n, s=slot: e.tensor_tensor(
                ev2[:, s * TT:(s + 1) * TT], ev[:, s * TT:(s + 1) * TT],
                xsb[pr][:, :], A.mult),
                reads=[f"ev{slot}", f"xsb{pr}"], writes=[f"fe{slot}"])
            p.op("vector", lambda e, n=n, s=slot: e.tensor_scalar(
                ev2[:, s * TT:(s + 1) * TT], ev2[:, s * TT:(s + 1) * TT],
                0.5, hbias[:, n:n + 1], A.mult, A.add),
                reads=[f"fe{slot}", "hbias"], writes=[f"fe{slot}"])
            p.op("gpsimd" if n % 2 == 0 else "vector",
                 lambda e, n=n, s=slot, xb=xres[pr]: e.tensor_tensor(
                ch(xb, n), ch(xb, n), ev2[:, s * TT:(s + 1) * TT], A.add),
                reads=[f"fe{slot}", XRk(pr, n)], writes=[XRk(pr, n)])
        yield

    def stream_proj(pr, widx, q, evac, ph):
        wslX = wslA if ph == "A" else wslB
        ps_bkX = ps_bkA if ph == "A" else ps_bkB
        PBKX = PBKA if ph == "A" else PBKB
        ws = "wa" if ph == "A" else "wb"
        nslot = 3 if ph == "A" else 2
        nh, qq = q // 2, q % 2
        for k in range(NCH):
            slot = k % nslot
            wkey = f"{ws}sl{slot}"
            r0 = nh * D + k * 128
            p.op("sync", lambda e, r0=r0, s=slot, w=widx, qq=qq,
                 wb=wslX: e.dma_start(
                out=wb[s][:, :], in_=wscr[w][r0:r0 + 128,
                                             qq * 1024:(qq + 1) * 1024]),
                reads=[f"wscr{widx}"], writes=[wkey], dma=True,
                stream=f"{ws}{slot}")
            for j in range(4):
                whi = wslX[slot][:, j * 128:(j + 1) * 128]
                wlo = wslX[slot][:, 512 + j * 128:512 + (j + 1) * 128]
                xh = ch(hixq[pr], k)
                xl = ch(lobf[pr], k)
                p.op("tensor", lambda e, o=ps_bkX[j], a=whi, bb=xh,
                     f=(k == 0 and j % 2 == 0): e.matmul(o, a, bb, start=f, stop=False),
                     reads=[wkey, HXk(pr, k)], writes=[PBKX[j]])
                p.op("tensor", lambda e, o=ps_bkX[j], a=whi, bb=xl: e.matmul(
                    o, a, bb, start=False, stop=False),
                    reads=[wkey, LOk(pr, k)], writes=[PBKX[j]])
                p.op("tensor", lambda e, o=ps_bkX[j], a=wlo, bb=xh,
                     last=(k == NCH - 1 and j % 2 == 1): e.matmul(
                    o, a, bb, start=False, stop=last),
                    reads=[wkey, HXk(pr, k)], writes=[PBKX[j]])
            if k % 4 == 3:
                yield
        for j in range(4):
            evac(q * 4 + j, ps_bkX[j], PBKX[j])
        yield

    def gla_head(pr, h, has_q, tile_idx):
        i = h % 2
        KHk, GHk, THk = f"kh{i}", f"gh{i}", f"th{i}"
        BTk, E1k, E2k = f"bt{i}", f"e1{i}", f"e2{i}"
        QIk, QTk = f"qint{i}", f"qitr{i}"
        KOk = f"kot{i}"
        PPk, KTk, VTk = f"Pp{i}", f"kTs{i}", f"vTs{i}"
        S8k, SEk = f"st8{i}", f"stE{i}"
        p.op("scalar", lambda e, h=h: e.activation(out=th[i][:, :], in_=ch(kraw, h), func=AF.Exp),
             reads=[KR[h]], writes=[THk])
        p.op("vector", lambda e: e.tensor_scalar(th[i][:, :], th[i][:, :], 1.0, None, A.add),
             reads=[THk], writes=[THk])
        p.op("scalar", lambda e: e.activation(out=gh[i][:, :], in_=th[i][:, :], func=AF.Ln),
             reads=[THk], writes=[GHk])
        p.op("vector", lambda e, h=h: e.tensor_tensor(th[i][:, :], ch(kraw, h), gh[i][:, :], A.subtract),
             reads=[KR[h], GHk], writes=[THk])
        p.op("scalar", lambda e: e.activation(out=kh[i][:, :], in_=th[i][:, :], func=AF.Exp),
             reads=[THk], writes=[KHk])
        p.op("vector", lambda e: e.tensor_scalar(gh[i][:, :], gh[i][:, :], -1.0, None, A.mult),
             reads=[GHk], writes=[GHk])
        Sh = Sst[:, h * 128:(h + 1) * 128]
        H = C // 2
        for c in (0, 1):
            csl = slice(c * C, (c + 1) * C)
            p.op("tensor", lambda e, h=h, c=c: e.transpose(
                ps_T, vbuf[:, h * TT + c * C:h * TT + (c + 1) * C], ident[:, :]),
                reads=[VB[h], "ident", "gb1"], writes=["gb1"])
            p.op("scalar", lambda e: e.copy(out=vTs[i][:, :], in_=ps_T),
                 reads=["gb1"], writes=[VTk])
            if tile_idx == 0:
                p.op("vector", lambda e, c=c: e.tensor_scalar(
                    vTs[i][:, :], vTs[i][:, :], halom[:, c:c + 1], None, A.mult),
                    reads=[VTk, "halom"], writes=[VTk])
            p.op("vector", lambda e, csl=csl: e.tensor_tensor_scan(
                bt[i][:, :], gh[i][:, csl], gh[i][:, csl], 0.0, A.add, A.bypass),
                reads=[GHk], writes=[BTk])
            p.op("vector", lambda e: e.tensor_scalar(st8[i][:, 1:2], bt[i][:, H - 1:H],
                                                     0.5, None, A.mult),
                 reads=[BTk], writes=[S8k], narrow=True)
            p.op("vector", lambda e: e.tensor_scalar(st8[i][:, 2:3], bt[i][:, C - 1:C],
                                                     0.5, st8[i][:, 1:2], A.mult, A.add),
                 reads=[BTk, S8k], writes=[S8k], narrow=True)
            p.op("vector", lambda e: e.tensor_tensor(st8[i][:, 3:4], bt[i][:, C - 1:C],
                                                     st8[i][:, 1:2], A.subtract),
                 reads=[BTk, S8k], writes=[S8k], narrow=True)
            p.op("vector", lambda e: e.tensor_tensor(st8[i][:, 0:1], bt[i][:, C - 1:C],
                                                     bt[i][:, H - 1:H], A.subtract),
                 reads=[BTk, S8k], writes=[S8k], narrow=True)
            p.op("scalar", lambda e: e.activation(out=stE[i][:, 0:1], in_=bt[i][:, C - 1:C],
                                                  func=AF.Exp),
                 reads=[BTk], writes=[SEk])
            p.op("scalar", lambda e: e.activation(out=stE[i][:, 1:2], in_=st8[i][:, 3:4],
                                                  func=AF.Exp),
                 reads=[S8k], writes=[SEk])
            p.op("scalar", lambda e: e.activation(out=stE[i][:, 2:3], in_=st8[i][:, 0:1],
                                                  func=AF.Exp, scale=0.5),
                 reads=[S8k], writes=[SEk])
            p.op("scalar", lambda e: e.activation(out=stE[i][:, 3:4], in_=bt[i][:, H - 1:H],
                                                  func=AF.Exp, scale=0.5),
                 reads=[BTk], writes=[SEk])
            p.op("scalar", lambda e: e.activation(out=e1[i][:, :], in_=bt[i][:, :], func=AF.Exp),
                 reads=[BTk], writes=[E1k])
            if has_q:
                qsl = qhalf[:, h * TT + c * C:h * TT + (c + 1) * C]
                p.op("vector", lambda e, q=qsl: e.tensor_tensor(qint[i][:, :], q, e1[i][:, :], A.mult),
                     reads=[QH[h], E1k], writes=[QIk])
            p.op("vector", lambda e: e.tensor_scalar(bt[i][:, 0:H], bt[i][:, 0:H], -1.0,
                                                     st8[i][:, 1:2], A.mult, A.add),
                 reads=[BTk, S8k], writes=[BTk], narrow=True)
            p.op("vector", lambda e: e.tensor_scalar(bt[i][:, H:C], bt[i][:, H:C], -1.0,
                                                     st8[i][:, 2:3], A.mult, A.add),
                 reads=[BTk, S8k], writes=[BTk], narrow=True)
            p.op("vector", lambda e: e.tensor_scalar(bt[i][:, :], bt[i][:, :],
                                                     80.0, -80.0, A.min, A.max),
                 reads=[BTk], writes=[BTk])
            p.op("scalar", lambda e: e.activation(out=e2[i][:, :], in_=bt[i][:, :], func=AF.Exp),
                 reads=[BTk], writes=[E2k])
            if has_q:
                p.op("scalar", lambda e: e.activation(out=e1[i][:, :], in_=bt[i][:, :],
                                                      func=AF.Exp, scale=-1.0),
                     reads=[BTk], writes=[E1k])
                p.op("vector", lambda e, q=qsl: e.tensor_tensor(qitr[i][:, :], q, e1[i][:, :], A.mult),
                     reads=[QH[h], E1k], writes=[QTk])
            p.op("vector", lambda e, csl=csl: e.tensor_tensor(kot[i][:, :], kh[i][:, csl],
                                                              e2[i][:, :], A.mult),
                 reads=[KHk, E2k], writes=[KOk])
            if has_q:
                p.op("vector", lambda e: e.tensor_scalar(e1[i][:, 0:H], kot[i][:, 0:H],
                                                         stE[i][:, 3:4], None, A.mult),
                     reads=[KOk, SEk], writes=[E1k], narrow=True)
                p.op("vector", lambda e: e.tensor_scalar(e1[i][:, H:C], qitr[i][:, H:C],
                                                         stE[i][:, 2:3], None, A.mult),
                     reads=[QTk, SEk], writes=[E1k], narrow=True)
                p.op("tensor", lambda e: e.matmul(ps_A[0:H, 0:H], kot[i][:, 0:H],
                                                  qitr[i][:, 0:H],
                                                  start=True, stop=False),
                     reads=[KOk, QTk, "gb0"], writes=["gb0"])
                p.op("tensor", lambda e: e.matmul(ps_A[H:C, H:C], kot[i][:, H:C],
                                                  qitr[i][:, H:C],
                                                  start=True, stop=True),
                     reads=[KOk, QTk], writes=["gb0"])
                p.op("tensor", lambda e: e.matmul(ps_A[0:H, H:C], e1[i][:, 0:H],
                                                  e1[i][:, H:C],
                                                  start=False, stop=True),
                     reads=[E1k], writes=["gb0"])
                p.op("vector", lambda e: e.copy_predicated(Pp[i][0:H, 0:H],
                                                           causal[0:H, 0:H],
                                                           ps_A[0:H, 0:H]),
                     reads=["gb0", "causal", f"Ppz{i}"], writes=[PPk])
                p.op("vector", lambda e: e.copy_predicated(Pp[i][H:C, H:C],
                                                           causal[H:C, H:C],
                                                           ps_A[H:C, H:C]),
                     reads=["gb0", "causal", PPk], writes=[PPk])
                p.op("scalar", lambda e: e.copy(out=Pp[i][0:H, H:C], in_=ps_A[0:H, H:C]),
                     reads=["gb0", PPk], writes=[PPk])
                p.op("tensor", lambda e: e.matmul(ps_O, vTs[i][:, :], Pp[i][:, :],
                                                  start=True, stop=False),
                     reads=[VTk, PPk, "gb0"], writes=["gb0"])
                p.op("tensor", lambda e, S=Sh: e.matmul(ps_O, S, qint[i][:, :],
                                                        start=False, stop=True),
                     reads=[SK[h], QIk], writes=["gb0"])
                p.op("scalar", lambda e, h=h, c=c, ob=xno[pr]: e.copy(
                    out=ob[:, h * TT + c * C:h * TT + (c + 1) * C], in_=ps_O),
                    reads=["gb0"], writes=[XNk(pr, h)])
            p.op("vector", lambda e: e.tensor_scalar(kot[i][:, 0:H], kot[i][:, 0:H],
                                                     stE[i][:, 1:2], None, A.mult),
                 reads=[KOk, SEk], writes=[KOk], narrow=True)
            p.op("vector", lambda e: e.tensor_scalar(kot[i][:, H:C], kot[i][:, H:C],
                                                     stE[i][:, 2:3], None, A.mult),
                 reads=[KOk, SEk], writes=[KOk], narrow=True)
            p.op("tensor", lambda e: e.transpose(ps_T, kot[i][:, :], ident[:, :]),
                 reads=[KOk, "ident", "gb1"], writes=["gb1"])
            p.op("scalar", lambda e: e.copy(out=kTs[i][:, :], in_=ps_T),
                 reads=["gb1"], writes=[KTk])
            p.op("tensor", lambda e: e.matmul(ps_S, kTs[i][:, :], vTs[i][:, :],
                                              start=True, stop=True),
                 reads=[KTk, VTk, "gb1"], writes=["gb1"])
            p.op("vector", lambda e, S=Sh: e.tensor_scalar(S, S, stE[i][:, 0:1], None, A.mult),
                 reads=[SK[h], SEk], writes=[SK[h]])
            p.op("vector", lambda e, S=Sh: e.tensor_tensor(S, S, ps_S, A.add),
                 reads=[SK[h], "gb1"], writes=[SK[h]])

    def dbg_dump(name, buf, keys):
        if name in dbg_t:
            w = buf[:, :].shape[1]
            p.op("sync", lambda e, d=dbg_t[name], s=buf, w=w: e.dma_start(
                out=d[:, 0:w], in_=s[:, :]),
                reads=list(keys), writes=[], dma=True, stream="dbg")

    def phaseA(tile):
        if tile >= NTILES:
            return
        pr = tile % 2
        halo_tile = (tile == 0)
        src = x_fm.rearrange("(c p) t -> p c t", p=128)[:, :, tile * TT:(tile + 1) * TT]
        dst = xres[pr][:, :].rearrange("p (c t) -> p c t", t=TT)
        p.op("sync", lambda e, s=src, d=dst: e.dma_start(out=d, in_=s),
             reads=[XRk(pr, c) for c in range(NCH)],
             writes=[XRk(pr, c) for c in range(NCH)], dma=True,
             stream=f"dx{pr}")
        yield
        ln_stats(pr, xres[pr], lambda c: XRk(pr, c), with_rms=True)
        yield
        ln_apply(pr, xres[pr], lambda c: XRk(pr, c), xno[pr], lambda c: XNk(pr, c))
        yield
        absmax_calc(pr, xno[pr], lambda c: XNk(pr, c))
        yield
        for c in range(NCH):
            quant_chunk(pr, xno[pr], lambda cc: XNk(pr, cc), c)
            if c % 4 == 3:
                yield
        for q in range(4):
            for _ in ffn_quarter(pr, q, "A"):
                yield
        if tile == 1:
            dbg_dump("x1", xres[pr], [XRk(pr, c) for c in range(NCH)])
        ln_stats(pr, xres[pr], lambda c: XRk(pr, c), with_rms=False)
        yield
        ln_apply(pr, xres[pr], lambda c: XRk(pr, c), xno[pr], lambda c: XNk(pr, c))
        yield
        for c in range(NCH):
            eng = "gpsimd" if c % 2 == 0 else "vector"
            p.op(eng, lambda e, c=c, xb=xno[pr], hb=hixq[pr]: e.tensor_copy(
                ch(hb, c), ch(xb, c)),
                reads=[XNk(pr, c)], writes=[HXk(pr, c)])
            p.op(eng, lambda e, c=c, xb=xno[pr], hb=hixq[pr], lb=lobf[pr]: e.tensor_tensor(
                ch(lb, c), ch(xb, c), ch(hb, c), A.subtract),
                reads=[XNk(pr, c), HXk(pr, c)], writes=[LOk(pr, c)])
        yield

        def evac_v(n, ps, pbk):
            p.op("scalar", lambda e, n=n, ps=ps: e.copy(out=ch(vbuf, n), in_=ps),
                 reads=[pbk], writes=[VB[n]])
        for q in range(4):
            for _ in stream_proj(pr, 2, q, evac_v, "A"):
                yield

        def evac_k(n, ps, pbk):
            p.op("scalar", lambda e, n=n, ps=ps: e.copy(out=ch(kraw, n), in_=ps),
                 reads=[pbk], writes=[KR[n]])
        for q in range(4):
            for _ in stream_proj(pr, 1, q, evac_k, "A"):
                yield

        if not halo_tile:
            def evac_q(n, ps, pbk):
                p.op("scalar", lambda e, n=n, ps=ps: e.activation(
                    out=qhalf[:, n * TT:(n + 1) * TT], in_=ps,
                    func=AF.Relu, scale=SCALE),
                    reads=[pbk], writes=[QH[n]])
            for q in range(4):
                for _ in stream_proj(pr, 0, q, evac_q, "A"):
                    yield

    def phaseB(tile):
        pr = tile % 2
        halo_tile = (tile == 0)
        if halo_tile:
            for h in range(NH):
                gla_head(pr, h, has_q=False, tile_idx=tile)
                yield
            return
        for h in range(NH):
            gla_head(pr, h, has_q=True, tile_idx=tile)
            yield
        if tile == 1:
            dbg_dump("o1", xno[pr], [XNk(pr, c) for c in range(NCH)])
        for c in range(NCH):
            slot = c % 2
            p.op("scalar", lambda e, c=c, s=slot, ob=xno[pr]: e.activation(
                out=sq[:, s * TT:(s + 1) * TT], in_=ch(ob, c), func=AF.Square),
                reads=[XNk(pr, c)], writes=[f"sq{slot}"])
            p.op("tensor", lambda e, c=c, s=slot: e.matmul(
                ps_row2, ones_col, sq[:, s * TT:(s + 1) * TT],
                start=(c == 0), stop=(c == NCH - 1)),
                reads=[f"sq{slot}", "onesc"], writes=["gb7"])
        p.op("vector", lambda e: e.tensor_scalar(row(6), ps_row2, 1.0 / 2048.0, None, A.mult),
             reads=["gb7"], writes=[rkey(6)], narrow=True)
        p.op("vector", lambda e: e.tensor_scalar(row(6), row(6), EPS_ATTN, None, A.add),
             reads=[rkey(6)], writes=[rkey(6)], narrow=True)
        p.op("scalar", lambda e: e.activation(out=row(7), in_=row(6), func=AF.Ln),
             reads=[rkey(6)], writes=[rkey(7)])
        p.op("vector", lambda e: e.tensor_scalar(row(7), row(7), -0.5, None, A.mult),
             reads=[rkey(7)], writes=[rkey(7)], narrow=True)
        p.op("scalar", lambda e: e.activation(out=row(6), in_=row(7), func=AF.Exp),
             reads=[rkey(7)], writes=[rkey(6)])
        p.op("tensor", lambda e: e.matmul(ps_bc, ones_row, row(6), start=True, stop=True),
             reads=[rkey(6), "onesc", "gb6"], writes=["gb6"])
        p.op("scalar", lambda e, d=rb[pr]: e.copy(out=d[:, :], in_=ps_bc),
             reads=["gb6"], writes=[f"rb{pr}"])
        yield
        for c in range(NCH):
            slot = c % 2
            p.op("vector", lambda e, c=c, s=slot, ob=xno[pr]: e.tensor_tensor(
                ev[:, s * TT:(s + 1) * TT], ch(ob, c), rb[pr][:, :], A.mult),
                reads=[XNk(pr, c), f"rb{pr}"], writes=[f"ev{slot}"])
            p.op("vector", lambda e, c=c, s=slot: e.tensor_scalar(
                ev2[:, s * TT:(s + 1) * TT], ev[:, s * TT:(s + 1) * TT],
                gwb[:, c:c + 1], None, A.mult),
                reads=[f"ev{slot}", "gwb"], writes=[f"fe{slot}"])
            p.op("gpsimd" if c % 2 == 0 else "vector",
                 lambda e, c=c, s=slot, hb=hixq[pr]: e.tensor_copy(
                ch(hb, c), ev2[:, s * TT:(s + 1) * TT]),
                reads=[f"fe{slot}"], writes=[HXk(pr, c)])
            p.op("gpsimd" if c % 2 == 0 else "vector",
                 lambda e, c=c, s=slot, hb=hixq[pr], lb=lobf[pr]: e.tensor_tensor(
                ch(lb, c), ev2[:, s * TT:(s + 1) * TT], ch(hb, c), A.subtract),
                reads=[f"fe{slot}", HXk(pr, c)], writes=[LOk(pr, c)])
        yield

        def evac_wo(n, ps, pbk):
            p.op("vector", lambda e, n=n, ps=ps, xb=xres[pr]: e.tensor_tensor(
                ch(xb, n), ch(xb, n), ps, A.add),
                reads=[pbk, XRk(pr, n)], writes=[XRk(pr, n)])
        for q in range(4):
            for _ in stream_proj(pr, 3, q, evac_wo, "B"):
                yield

        ln_stats(pr, xres[pr], lambda c: XRk(pr, c), with_rms=True)
        yield
        ln_apply(pr, xres[pr], lambda c: XRk(pr, c), xno[pr], lambda c: XNk(pr, c))
        yield
        absmax_calc(pr, xno[pr], lambda c: XNk(pr, c))
        yield
        for c in range(NCH):
            quant_chunk(pr, xno[pr], lambda cc: XNk(pr, cc), c)
            if c % 4 == 3:
                yield
        for q in range(4):
            for _ in ffn_quarter(pr, q, "B"):
                yield

        osrc = xres[pr][:, :].rearrange("p (c t) -> p c t", t=TT)
        odst = y_fm.rearrange("(c p) t -> p c t", p=128)[
            :, :, (tile - 1) * TT:tile * TT]
        p.op("sync", lambda e, s=osrc, d=odst: e.dma_start(out=d, in_=s),
             reads=[XRk(pr, c) for c in range(NCH)] + ["outdma"],
             writes=["outdma"], dma=True, stream="do")
        yield

    def drain(gen):
        for _ in gen:
            pass

    drain(phaseA(0))
    PRE = 8
    for tile in range(NTILES):
        gb = phaseB(tile)
        ga = phaseA(tile + 1)
        for _ in range(PRE):
            if next(ga, StopIteration) is StopIteration:
                break
        while True:
            db = next(gb, StopIteration)
            da = next(ga, StopIteration)
            if db is StopIteration and da is StopIteration:
                break

    p.op("sync", lambda e: e.nop(),
         reads=["outdma"] + [XRk(pp, c) for pp in (0, 1) for c in range(NCH)] + SK,
         writes=[])

    emit_engine = p.emit(nc, sems)
    with nc.Block() as blk:
        blk.sync(lambda e: emit_engine("sync", e))
        blk.tensor(lambda e: emit_engine("tensor", e))
        blk.vector(lambda e: emit_engine("vector", e))
        blk.scalar(lambda e: emit_engine("scalar", e))
        blk.gpsimd(lambda e: emit_engine("gpsimd", e))
    es.close()
    return nc


def kernel(L0_attn_hidden_states, L0_attn_Wq, L0_attn_Wk, L0_attn_Wi, L0_attn_Wo,
           L0_attn_g_weight, L0_ffn_weight, L0_ffn_bias):
    x = np.asarray(L0_attn_hidden_states, dtype=np.float32)
    wq = np.ascontiguousarray(np.asarray(L0_attn_Wq, dtype=np.float32))
    wk = np.ascontiguousarray(np.asarray(L0_attn_Wk, dtype=np.float32))
    wi = np.ascontiguousarray(np.asarray(L0_attn_Wi, dtype=np.float32))
    wo = np.ascontiguousarray(np.asarray(L0_attn_Wo, dtype=np.float32))
    fw = np.ascontiguousarray(np.asarray(L0_ffn_weight, dtype=np.float32))
    fb = np.asarray(L0_ffn_bias, dtype=np.float32)
    gw = np.asarray(L0_attn_g_weight, dtype=np.float32)

    fbias = np.ascontiguousarray(fb.reshape(NCH, 128).T)
    gwin = np.ascontiguousarray(gw.reshape(NCH, 128).T)
    ident = np.eye(128, dtype=np.float32)
    causal = np.triu(np.ones((128, 128), dtype=np.uint8))
    onesc = np.ones((128, 128), dtype=np.float32)

    nc = build_program()
    in_maps = []
    for ci in range(NCORES):
        bidx, j = ci // 4, ci % 4
        t0 = j * TCORE
        if j == 0:
            xs = np.concatenate([np.zeros((HALO, D), np.float32), x[bidx, 0:TCORE]], axis=0)
            hm = np.zeros((128, 2), np.float32)
        else:
            xs = x[bidx, t0 - HALO:t0 + TCORE]
            hm = np.ones((128, 2), np.float32)
        in_maps.append({
            "x_fm": np.ascontiguousarray(xs.T),
            "wq": wq, "wk": wk, "wi": wi, "wo": wo, "fw": fw,
            "fbias": fbias, "gwin": gwin, "ident_in": ident,
            "causal_in": causal, "halom_in": hm, "onesc_in": onesc,
        })
    res = run_bass_kernel_spmd(nc, in_maps, core_ids=list(range(NCORES)))
    out = np.empty((B, L, D), dtype=np.float32)
    for ci in range(NCORES):
        bidx, j = ci // 4, ci % 4
        out[bidx, j * TCORE:(j + 1) * TCORE, :] = res.results[ci]["y_fm"].T
    if _DBG:
        kernel.dbg = [{k: res.results[ci]["dbg_" + k] for k in _DBG}
                      for ci in range(NCORES)]
    return out


# version 13
# speedup vs baseline: 1.1359x; 1.0046x over previous
import numpy as np
import concourse.bass as bass
import concourse.mybir as mybir
from concourse.bass_utils import run_bass_kernel_spmd

F32 = mybir.dt.float32
BF16 = mybir.dt.bfloat16
U8 = mybir.dt.uint8
A = mybir.AluOpType
AF = mybir.ActivationFunctionType
AX = mybir.AxisListType

D = 2048
NCH = D // 128
NH = 16
B = 2
L = 8192
NCORES = 8
HALO = 256
TCORE = 2048
TTOT = HALO + TCORE
TT = 256
NTILES = TTOT // TT
C = 128
SCALE = 128.0 ** -0.5
EPS_LN = 1e-5
EPS_RMS_FFN = 1e-8
EPS_FFN = 1e-8
EPS_ATTN = 1e-5
QB = 127.0
LADDER_BITS = [128.0, 64.0, 32.0, 16.0, 8.0, 4.0, 2.0, 1.0]
WT_BITS = [4.0, 2.0, 1.0]

_DBG = []

SK = [f"S{h}" for h in range(NH)]
QH = [f"qh{j}" for j in range(NCH)]
KR = [f"kr{c}" for c in range(NCH)]
VB = [f"vb{c}" for c in range(NCH)]


class Plan:
    def __init__(self):
        self.ops = []
        self.writer = {}
        self.readers = {}

    def op(self, engine, fn, reads=(), writes=(), dma=False, narrow=False,
           stream=None):
        deps = {}
        for bb in reads:
            w = self.writer.get(bb)
            if w is not None:
                deps[w] = "rw"
        for bb in writes:
            w = self.writer.get(bb)
            if w is not None:
                deps[w] = "rw"
        for bb in writes:
            for r in self.readers.get(bb, ()):
                deps.setdefault(r, "war")
        i = len(self.ops)
        assert (stream is not None) == dma, "DMA ops need a stream name"
        self.ops.append(dict(engine=engine, fn=fn, deps=deps, dma=dma,
                             narrow=narrow, stream=stream))
        for bb in reads:
            self.readers.setdefault(bb, []).append(i)
        for bb in writes:
            self.writer[bb] = i
            self.readers[bb] = []
        return i

    def emit(self, nc, sems):
        ops = self.ops
        engseq = [0] * len(ops)
        cnt = {}
        for i, o in enumerate(ops):
            engseq[i] = cnt.get(o["engine"], 0)
            cnt[o["engine"]] = engseq[i] + 1
        interlock = [set() for _ in ops]
        needs_inc = [False] * len(ops)
        for i, o in enumerate(ops):
            for d, typ in o["deps"].items():
                if ops[d]["dma"] or ops[d]["engine"] != o["engine"]:
                    needs_inc[d] = True
                elif typ == "rw" and o["engine"] != "tensor":
                    needs_inc[d] = True
                    interlock[i].add(d)
        for i, o in enumerate(ops):
            if o["dma"]:
                needs_inc[i] = True
        last_of = {}
        for i, o in enumerate(ops):
            last_of[o["engine"]] = i
        for i in last_of.values():
            needs_inc[i] = True

        def stream_of(i):
            return ops[i]["stream"] if ops[i]["dma"] else ops[i]["engine"]

        cum = {}
        opcum = [0] * len(ops)
        pfx = [dict() for _ in ops]
        run_dma = {}
        for i, o in enumerate(ops):
            pfx[i] = dict(run_dma)
            st = stream_of(i)
            amt = 16 if o["dma"] else 1
            if needs_inc[i]:
                cum[st] = cum.get(st, 0) + amt
                if o["dma"]:
                    run_dma[st] = cum[st]
            opcum[i] = cum.get(st, 0)
        hiwater = {}

        def emit_engine(eng_name, eng):
            hw = hiwater.setdefault(eng_name, {})
            for i, o in enumerate(ops):
                if o["engine"] != eng_name:
                    continue
                want = {}
                for d in o["deps"]:
                    do = ops[d]
                    if (do["engine"] == eng_name and not do["dma"]
                            and d not in interlock[i]):
                        continue
                    st = stream_of(d)
                    if do["dma"]:
                        val = pfx[i].get(st, 0)
                    else:
                        val = opcum[d]
                    if want.get(st, 0) < val:
                        want[st] = val
                for st in sorted(want):
                    val = want[st]
                    if hw.get(st, 0) < val:
                        eng.wait_ge(sems[st], val)
                        hw[st] = val
                ins = o["fn"](eng)
                if needs_inc[i]:
                    ins.then_inc(sems[stream_of(i)], 16 if o["dma"] else 1)
        return emit_engine


def _ladder_floor(p, v_, u0, uu, u2, bits, keys, pre_add, eng="vector"):
    kv, k0, ku, k2 = keys
    p.op(eng, lambda e, a=v_, o=u0, c=pre_add: e.tensor_scalar(o, a, c, None, A.add),
         reads=[kv], writes=[k0])
    first = True
    for bit in bits:
        src = u0 if first else uu
        ksrc = k0 if first else ku
        p.op(eng, lambda e, s=src, o=u2, bb=bit: e.tensor_scalar(o, s, bb, -bb, A.is_ge, A.mult),
             reads=[ksrc], writes=[k2])
        p.op(eng, lambda e, s=src, o=uu, c=u2: e.tensor_tensor(o, s, c, A.add),
             reads=[ksrc, k2], writes=[ku])
        first = False
    p.op(eng, lambda e, o=u2, x=u0, y=uu: e.tensor_tensor(o, x, y, A.subtract),
         reads=[k0, ku], writes=[k2])


def build_program():
    nc = bass.Bass("TRN2", target_bir_lowering=False)

    x_fm = nc.dram_tensor("x_fm", [D, TTOT], F32, kind="ExternalInput").ap()
    wq = nc.dram_tensor("wq", [D, D], F32, kind="ExternalInput").ap()
    wk = nc.dram_tensor("wk", [D, D], F32, kind="ExternalInput").ap()
    wi = nc.dram_tensor("wi", [D, D], F32, kind="ExternalInput").ap()
    wo = nc.dram_tensor("wo", [D, D], F32, kind="ExternalInput").ap()
    fw = nc.dram_tensor("fw", [D, D], F32, kind="ExternalInput").ap()
    fbias = nc.dram_tensor("fbias", [128, NCH], F32, kind="ExternalInput").ap()
    gwin = nc.dram_tensor("gwin", [128, NCH], F32, kind="ExternalInput").ap()
    ident_in = nc.dram_tensor("ident_in", [128, 128], F32, kind="ExternalInput").ap()
    causal_in = nc.dram_tensor("causal_in", [128, 128], U8, kind="ExternalInput").ap()
    halom_in = nc.dram_tensor("halom_in", [128, 2], F32, kind="ExternalInput").ap()
    onesc_in = nc.dram_tensor("onesc_in", [128, 128], F32, kind="ExternalInput").ap()
    y_fm = nc.dram_tensor("y_fm", [D, TCORE], F32, kind="ExternalOutput").ap()
    wscr = [nc.dram_tensor(f"wscr{i}", [2 * D, D], BF16, kind="Internal").ap()
            for i in range(4)]
    wtscr = nc.dram_tensor("wtscr", [NCH * 128, D], BF16, kind="Internal").ap()
    dbg_t = {}
    for name in _DBG:
        dbg_t[name] = nc.dram_tensor("dbg_" + name, [128, 4096], F32,
                                     kind="ExternalOutput").ap()

    from contextlib import ExitStack
    es = ExitStack()

    def sb(name, shape, dt=F32):
        return es.enter_context(nc.sbuf_tensor(name, shape, dt))

    xres = [sb(f"xres{i}", [128, NCH * TT]) for i in range(2)]
    xno = [sb(f"xno{i}", [128, NCH * TT]) for i in range(2)]
    hixq = [sb(f"hixq{i}", [128, NCH * TT], BF16) for i in range(2)]
    lobf = [sb(f"lobf{i}", [128, NCH * TT], BF16) for i in range(2)]
    vbuf = sb("vbuf", [128, NCH * TT])
    kraw = sb("kraw", [128, NCH * TT])
    wslA = [sb(f"wslA{i}", [128, 1024], BF16) for i in range(3)]
    wslB = [sb(f"wslB{i}", [128, 1024], BF16) for i in range(2)]
    fwslA = [sb(f"fwslA{i}", [128, 512], BF16) for i in range(3)]
    fwslB = [sb(f"fwslB{i}", [128, 512], BF16) for i in range(3)]
    qhalf = sb("qhalf", [128, NCH * TT])
    kh = [sb(f"kh{i}", [128, TT]) for i in range(2)]
    gh = [sb(f"gh{i}", [128, TT]) for i in range(2)]
    th = [sb(f"th{i}", [128, TT]) for i in range(2)]
    bt = [sb(f"bt{i}", [128, C]) for i in range(2)]
    e1 = [sb(f"e1{i}", [128, C]) for i in range(2)]
    e2 = [sb(f"e2{i}", [128, C]) for i in range(2)]
    qint = [sb(f"qint{i}", [128, C]) for i in range(2)]
    qitr = [sb(f"qitr{i}", [128, C]) for i in range(2)]
    kot = [sb(f"kot{i}", [128, C]) for i in range(2)]
    Pp = [sb(f"Pp{i}", [128, C]) for i in range(2)]
    kTs = [sb(f"kTs{i}", [128, C]) for i in range(2)]
    vTs = [sb(f"vTs{i}", [128, C]) for i in range(2)]
    st8 = [sb(f"st8{i}", [128, 4]) for i in range(2)]
    stE = [sb(f"stE{i}", [128, 4]) for i in range(2)]
    Sst = sb("Sst", [128, NH * 128])
    sq = sb("sq", [128, 2 * TT])
    u0 = sb("u0", [128, TT])
    uu = sb("uu", [128, TT])
    u2 = sb("u2", [128, TT])
    rows = sb("rows", [1, 6 * TT])
    colsb = sb("colsb", [128, 16])
    rb = [sb(f"rb{i}", [128, TT]) for i in range(2)]
    ab = [sb(f"ab{i}", [128, TT]) for i in range(2)]
    xsb = [sb(f"xsb{i}", [128, TT]) for i in range(2)]
    invsb = [sb(f"invsb{i}", [128, TT]) for i in range(2)]
    gamma = sb("gamma", [128, NCH])
    hbias = sb("hbias", [128, NCH])
    gwb = sb("gwb", [128, NCH])
    ident = sb("ident", [128, 128])
    causal = sb("causal", [128, 128], U8)
    halom = sb("halom", [128, 2])
    onesc = sb("onesc", [128, 128])
    ev = sb("ev", [128, 2 * TT])
    ev2 = sb("ev2", [128, 2 * TT])

    PS = es.enter_context(nc.psum_tensor("PS", [128, 4096], F32))
    ps_A = PS[:, 0:128]
    ps_O = PS[:, 256:384]
    ps_T = PS[:, 512:640]
    ps_S = PS[:, 768:896]
    ps_bkA = [PS[:, (2 + j // 2) * 512 + (j % 2) * TT:
                 (2 + j // 2) * 512 + (j % 2) * TT + TT] for j in range(4)]
    ps_bkB = [PS[:, (4 + j // 2) * 512 + (j % 2) * TT:
                 (4 + j // 2) * 512 + (j % 2) * TT + TT] for j in range(4)]
    PBKA = [f"pb{2 + j // 2}" for j in range(4)]
    PBKB = [f"pb{4 + j // 2}" for j in range(4)]
    ps_tr = PS[:, 6 * 512:6 * 512 + 128]
    ps_rr = PS[0:1, 6 * 512 + 128:6 * 512 + 256]
    ps_bc = PS[:, 6 * 512 + 256:6 * 512 + 512]
    ps_row1 = PS[0:1, 7 * 512:7 * 512 + TT]
    ps_row2 = PS[0:1, 7 * 512 + TT:7 * 512 + 2 * TT]

    sems = {}
    for s in ["sync", "tensor", "vector", "scalar", "gpsimd",
              "dc", "dpl", "dps", "dgp", "dx0", "dx1", "do",
              "wa0", "wa1", "wa2", "wb0", "wb1", "wb2",
              "fa0", "fa1", "fa2", "fb0", "fb1", "fb2",
              "dpg", "dp2", "dp3", "dsc", "dbg"]:
        sems[s] = es.enter_context(nc.semaphore("sem_" + s))

    p = Plan()

    def ch(buf, c, w=TT):
        return buf[:, c * w:(c + 1) * w]

    ROWMAP = {0: 0, 1: 1, 2: 2, 3: 3, 4: 4, 5: 5, 6: 4, 7: 5}

    def row(i):
        s = ROWMAP[i]
        return rows[0:1, s * TT:(s + 1) * TT]

    def rkey(i):
        return f"r{ROWMAP[i]}"

    for dst, src, key in [(ident, ident_in, "ident"), (causal, causal_in, "causal"),
                          (halom, halom_in, "halom"), (onesc, onesc_in, "onesc"),
                          (gwb, gwin, "gwb"), (hbias, fbias, "hbias_raw")]:
        p.op("sync", lambda e, d=dst, s=src: e.dma_start(out=d[:, :], in_=s),
             writes=[key], dma=True, stream="dc")
    p.op("vector", lambda e: e.tensor_scalar(hbias[:, :], hbias[:, :], 0.5, None, A.mult),
         reads=["hbias_raw"], writes=["hbias"], narrow=True)
    for i in range(2):
        p.op("vector", lambda e, i=i: e.memset(Pp[i][:, :], 0.0), writes=[f"Ppz{i}"])
    p.op("vector", lambda e: e.memset(Sst[:, :], 0.0), writes=SK)

    ones_col = onesc[:, 0:1]
    ones_row = onesc[0:1, :]

    PBUF = [vbuf[:, 0:D], vbuf[:, D:2 * D], kraw[:, 0:D], kraw[:, D:2 * D]]
    PLD = [("sync", "dpl"), ("sync", "dp2"), ("gpsimd", "dpg"), ("gpsimd", "dp3")]
    WS = [wq, wk, wi, wo]
    NW = 4 * NCH

    def emit_wload(m):
        widx, k = m // NCH, m % NCH
        par = m % 4
        eng, st = PLD[par]
        p.op(eng, lambda e, s=WS[widx], k=k, sl=PBUF[par]: e.dma_start(
             out=sl, in_=s[k * 128:(k + 1) * 128, :]),
             reads=[], writes=[f"prep_w{par}"], dma=True, stream=st)
    for m in range(4):
        emit_wload(m)
    for m in range(NW):
        widx, k = m // NCH, m % NCH
        par = m % 4
        hpar = m % 2
        slab = PBUF[par]
        hih = hixq[0][:, hpar * D:(hpar + 1) * D]
        loh = lobf[0][:, hpar * D:(hpar + 1) * D]
        wkk, hik, lok = f"prep_w{par}", f"prep_hi{hpar}", f"prep_lo{hpar}"
        p.op("vector", lambda e, sl=slab, hh=hih: e.tensor_copy(hh, sl),
             reads=[wkk], writes=[hik])
        p.op("vector", lambda e, sl=slab, hh=hih, ll=loh: e.tensor_tensor(
             ll, sl, hh, A.subtract),
             reads=[wkk, hik], writes=[lok])
        if m + 4 < NW:
            emit_wload(m + 4)
        for nh in range(2):
            r0 = nh * D + k * 128
            for qq in range(2):
                c0 = hpar * D + nh * 1024 + qq * 512
                sst = "dps" if (nh + qq) % 2 == 0 else "dsc"
                seng = "sync" if sst == "dps" else "scalar"
                p.op(seng, lambda e, r0=r0, w=widx, qq=qq, c0=c0: e.dma_start(
                    out=wscr[w][r0:r0 + 128, qq * 1024:qq * 1024 + 512],
                    in_=hixq[0][:, c0:c0 + 512]),
                    reads=[hik], writes=[f"wscr{widx}"], dma=True, stream=sst)
                p.op(seng, lambda e, r0=r0, w=widx, qq=qq, c0=c0: e.dma_start(
                    out=wscr[w][r0:r0 + 128, qq * 1024 + 512:qq * 1024 + 1024],
                    in_=lobf[0][:, c0:c0 + 512]),
                    reads=[lok], writes=[f"wscr{widx}"], dma=True, stream=sst)

    def emit_fload(n):
        par = n % 2
        p.op("sync" if par == 0 else "gpsimd",
             lambda e, n=n, sl=vbuf[:, par * D:(par + 1) * D]: e.dma_start(
             out=sl, in_=fw[n * 128:(n + 1) * 128, :]),
             reads=[f"prep_hi{par}", f"prep_lo{par}"], writes=[f"prep_w{par}"],
             dma=True, stream="dpl" if par == 0 else "dpg")
    for n in range(2):
        emit_fload(n)
    for n in range(NCH):
        par = n % 2
        slab = vbuf[:, par * D:(par + 1) * D]
        wkk = f"prep_w{par}"
        p.op("vector", lambda e, sl=slab: e.reduce_sum(colsb[:, 1:2], sl, AX.X,
                                                       apply_absolute_value=True),
             reads=[wkk], writes=["c1"], narrow=True)
        p.op("vector", lambda e, n=n: e.tensor_scalar(gamma[:, n:n + 1], colsb[:, 1:2],
                                                      1.0 / 2048.0, None, A.mult),
             reads=["c1"], writes=["gamma"], narrow=True)
        p.op("vector", lambda e, n=n: e.tensor_scalar(colsb[:, 2:3], gamma[:, n:n + 1],
                                                      EPS_FFN, None, A.add),
             reads=["gamma"], writes=["c2"], narrow=True)
        p.op("vector", lambda e: e.reciprocal(colsb[:, 0:1], colsb[:, 2:3]),
             reads=["c2"], writes=["c0"], narrow=True)
        p.op("vector", lambda e, sl=slab: e.tensor_scalar(xno[0][:, 0:D], sl,
                                                          colsb[:, 0:1], None, A.mult),
             reads=[wkk, "c0"], writes=["prep_z"])
        p.op("vector", lambda e: e.tensor_scalar(kraw[:, 0:D], xno[0][:, 0:D],
                                                 0.5, None, A.is_ge),
             reads=["prep_z"], writes=["prep_w2"])
        p.op("vector", lambda e: e.tensor_scalar(kraw[:, D:2 * D], xno[0][:, 0:D],
                                                 -0.5, None, A.is_ge),
             reads=["prep_z"], writes=["prep_w3"])
        p.op("vector", lambda e: e.tensor_tensor(kraw[:, 0:D], kraw[:, 0:D],
                                                 kraw[:, D:2 * D], A.add),
             reads=["prep_w2", "prep_w3"], writes=["prep_w2"])
        p.op("vector", lambda e: e.tensor_scalar(kraw[:, 0:D], kraw[:, 0:D],
                                                 -1.0, None, A.add),
             reads=["prep_w2"], writes=["prep_w2"])
        if n + 2 < NCH:
            emit_fload(n + 2)
        for j in range(NCH):
            p.op("tensor", lambda e, j=j: e.transpose(ps_tr, kraw[:, j * 128:(j + 1) * 128],
                                                      ident[:, :]),
                 reads=["prep_w2", "ident", "gb6"], writes=["gb6"])
            p.op("scalar", lambda e, j=j: e.copy(
                out=hixq[0][:, j * 128:(j + 1) * 128], in_=ps_tr),
                reads=["gb6"], writes=["prep_hi0"])
        for j in range(NCH):
            p.op("gpsimd", lambda e, j=j, n=n: e.dma_start(
                out=wtscr[j * 128:(j + 1) * 128, n * 128:(n + 1) * 128],
                in_=hixq[0][:, j * 128:(j + 1) * 128]),
                reads=["prep_hi0"], writes=["wtscrk"], dma=True, stream="dgp")

    allkeys = (SK + QH + KR + VB
               + [f"{nm}{pp}_{c}" for nm in ("xr", "xn", "hx", "lo")
                  for pp in (0, 1) for c in range(NCH)]
               + ["lq", "l0", "l1", "l2"])
    p.op("sync", lambda e: e.nop(),
         reads=[],
         writes=allkeys + ["prep_w0", "prep_w1", "prep_w2", "prep_w3",
                           "prep_hi0", "prep_hi1",
                           "prep_lo0", "prep_lo1", "prep_z"])

    def XRk(pr, c):
        return f"xr{pr}_{c}"

    def XNk(pr, c):
        return f"xn{pr}_{c}"

    def HXk(pr, c):
        return f"hx{pr}_{c}"

    def LOk(pr, c):
        return f"lo{pr}_{c}"

    def ln_stats(pr, srcbuf, srck, with_rms):
        for c in range(NCH):
            slot = c % 2
            p.op("scalar", lambda e, c=c, s=slot, sbf=srcbuf: e.activation(
                out=sq[:, s * TT:(s + 1) * TT], in_=ch(sbf, c), func=AF.Square),
                reads=[srck(c)], writes=[f"sq{slot}"])
            p.op("tensor", lambda e, c=c, sbf=srcbuf: e.matmul(
                ps_row1, ones_col, ch(sbf, c), start=(c == 0), stop=False),
                reads=[srck(c), "onesc"], writes=["gb7"])
            p.op("tensor", lambda e, c=c, s=slot: e.matmul(
                ps_row2, ones_col, sq[:, s * TT:(s + 1) * TT],
                start=False, stop=(c == NCH - 1)),
                reads=[f"sq{slot}", "onesc"], writes=["gb7"])
        p.op("vector", lambda e: e.tensor_scalar(row(0), ps_row1, 1.0 / 2048.0, None, A.mult),
             reads=["gb7"], writes=[rkey(0)], narrow=True)
        p.op("vector", lambda e: e.tensor_scalar(row(1), ps_row2, 1.0 / 2048.0, None, A.mult),
             reads=["gb7"], writes=[rkey(1)], narrow=True)
        p.op("vector", lambda e: e.tensor_tensor(row(6), row(0), row(0), A.mult),
             reads=[rkey(0)], writes=[rkey(6)], narrow=True)
        p.op("vector", lambda e: e.tensor_tensor(row(2), row(1), row(6), A.subtract),
             reads=[rkey(1), rkey(6)], writes=[rkey(2)], narrow=True)
        p.op("vector", lambda e: e.tensor_scalar(row(6), row(2), EPS_LN, None, A.add),
             reads=[rkey(2)], writes=[rkey(6)], narrow=True)
        p.op("scalar", lambda e: e.activation(out=row(7), in_=row(6), func=AF.Ln),
             reads=[rkey(6)], writes=[rkey(7)])
        p.op("vector", lambda e: e.tensor_scalar(row(7), row(7), -0.5, None, A.mult),
             reads=[rkey(7)], writes=[rkey(7)], narrow=True)
        p.op("scalar", lambda e: e.activation(out=row(3), in_=row(7), func=AF.Exp),
             reads=[rkey(7)], writes=[rkey(3)])
        if with_rms:
            p.op("vector", lambda e: e.tensor_tensor(row(6), row(3), row(3), A.mult),
                 reads=[rkey(3)], writes=[rkey(6)], narrow=True)
            p.op("vector", lambda e: e.tensor_tensor(row(6), row(2), row(6), A.mult),
                 reads=[rkey(2), rkey(6)], writes=[rkey(6)], narrow=True)
            p.op("vector", lambda e: e.tensor_scalar(row(6), row(6), EPS_RMS_FFN, None, A.add),
                 reads=[rkey(6)], writes=[rkey(6)], narrow=True)
            p.op("scalar", lambda e: e.activation(out=row(7), in_=row(6), func=AF.Ln),
                 reads=[rkey(6)], writes=[rkey(7)])
            p.op("vector", lambda e: e.tensor_scalar(row(7), row(7), -0.5, None, A.mult),
                 reads=[rkey(7)], writes=[rkey(7)], narrow=True)
            p.op("scalar", lambda e: e.activation(out=row(6), in_=row(7), func=AF.Exp),
                 reads=[rkey(7)], writes=[rkey(6)])
            p.op("vector", lambda e: e.tensor_tensor(row(4), row(3), row(6), A.mult),
                 reads=[rkey(3), rkey(6)], writes=[rkey(4)], narrow=True)
        else:
            p.op("vector", lambda e: e.tensor_copy(row(4), row(3)),
                 reads=[rkey(3)], writes=[rkey(4)], narrow=True)
        p.op("vector", lambda e: e.tensor_tensor(row(5), row(0), row(4), A.mult),
             reads=[rkey(0), rkey(4)], writes=[rkey(5)], narrow=True)
        p.op("tensor", lambda e: e.matmul(ps_bc, ones_row, row(4), start=True, stop=True),
             reads=[rkey(4), "onesc", "gb6"], writes=["gb6"])
        p.op("scalar", lambda e, d=rb[pr]: e.copy(out=d[:, :], in_=ps_bc),
             reads=["gb6"], writes=[f"rb{pr}"])
        p.op("tensor", lambda e: e.matmul(ps_bc, ones_row, row(5), start=True, stop=True),
             reads=[rkey(5), "onesc", "gb6"], writes=["gb6"])
        p.op("scalar", lambda e, d=ab[pr]: e.copy(out=d[:, :], in_=ps_bc),
             reads=["gb6"], writes=[f"ab{pr}"])

    def ln_apply(pr, srcbuf, srck, dstbuf, dstk):
        for c in range(NCH):
            eng = "gpsimd" if c % 2 == 0 else "vector"
            p.op(eng, lambda e, c=c, s=srcbuf, d=dstbuf: e.tensor_tensor(
                ch(d, c), ch(s, c), rb[pr][:, :], A.mult),
                reads=[srck(c), f"rb{pr}"], writes=[dstk(c)])
            p.op(eng, lambda e, c=c, d=dstbuf: e.tensor_tensor(
                ch(d, c), ch(d, c), ab[pr][:, :], A.subtract),
                reads=[dstk(c), f"ab{pr}"], writes=[dstk(c)])

    def absmax_calc(pr, srcbuf, srck):
        p.op("vector", lambda e, s=srcbuf: e.tensor_tensor(u0[:, :], ch(s, 0), ch(s, 1), A.max),
             reads=[srck(0), srck(1)], writes=["l0"])
        p.op("gpsimd", lambda e, s=srcbuf: e.tensor_copy(u2[:, :], ch(s, 0)),
             reads=[srck(0)], writes=["l2"])
        for c in range(1, NCH):
            if c > 1:
                p.op("vector", lambda e, c=c, s=srcbuf: e.tensor_tensor(
                    u0[:, :], u0[:, :], ch(s, c), A.max),
                    reads=[srck(c), "l0"], writes=["l0"])
            p.op("vector", lambda e, c=c, s=srcbuf: e.tensor_tensor(
                u2[:, :], u2[:, :], ch(s, c), A.min),
                reads=[srck(c), "l2"], writes=["l2"])
        for j in range(TT // 128):
            p.op("tensor", lambda e, j=j: e.transpose(ps_tr, u0[:, j * 128:(j + 1) * 128],
                                                      ident[:, :]),
                 reads=["l0", "ident", "gb6"], writes=["gb6"])
            p.op("vector", lambda e, j=j: e.reduce_max(colsb[:, j:j + 1], ps_tr, AX.X),
                 reads=["gb6"], writes=["cmax"], narrow=True)
            p.op("tensor", lambda e, j=j: e.transpose(ps_tr, u2[:, j * 128:(j + 1) * 128],
                                                      ident[:, :]),
                 reads=["l2", "ident", "gb6"], writes=["gb6"])
            p.op("vector", lambda e, j=j: e.tensor_reduce(colsb[:, 2 + j:3 + j], ps_tr, AX.X, A.min),
                 reads=["gb6"], writes=["cmin"], narrow=True)
        for j in range(TT // 128):
            p.op("vector", lambda e, j=j: e.tensor_scalar(colsb[:, 4 + j:5 + j],
                                                          colsb[:, 2 + j:3 + j], -1.0, None, A.mult),
                 reads=["cmin"], writes=["cneg"], narrow=True)
            p.op("vector", lambda e, j=j: e.tensor_tensor(colsb[:, 6 + j:7 + j],
                                                          colsb[:, j:j + 1],
                                                          colsb[:, 4 + j:5 + j], A.max),
                 reads=["cmax", "cneg"], writes=["cam"], narrow=True)
            p.op("vector", lambda e, j=j: e.tensor_scalar(colsb[:, 8 + j:9 + j],
                                                          colsb[:, 6 + j:7 + j],
                                                          1.0 / QB, None, A.mult),
                 reads=["cam"], writes=["cxs"], narrow=True)
            p.op("vector", lambda e, j=j: e.tensor_scalar(colsb[:, 10 + j:11 + j],
                                                          colsb[:, 8 + j:9 + j],
                                                          EPS_FFN, None, A.add),
                 reads=["cxs"], writes=["cpe"], narrow=True)
            p.op("vector", lambda e, j=j: e.reciprocal(
                colsb[:, 12 + j:13 + j], colsb[:, 10 + j:11 + j]),
                reads=["cpe"], writes=["cinv"], narrow=True)
        for j in range(TT // 128):
            p.op("tensor", lambda e, j=j: e.transpose(ps_rr, colsb[:, 8 + j:9 + j], ident[:, :]),
                 reads=["cxs", "ident", "gb6"], writes=["gb6"])
            p.op("scalar", lambda e, j=j: e.copy(out=rows[0:1, j * 128:(j + 1) * 128], in_=ps_rr),
                 reads=["gb6"], writes=["r0"])
            p.op("tensor", lambda e, j=j: e.transpose(ps_rr, colsb[:, 12 + j:13 + j], ident[:, :]),
                 reads=["cinv", "ident", "gb6"], writes=["gb6"])
            p.op("scalar", lambda e, j=j: e.copy(out=rows[0:1, TT + j * 128:TT + (j + 1) * 128],
                                                 in_=ps_rr),
                 reads=["gb6"], writes=["r1"])
        p.op("tensor", lambda e: e.matmul(ps_bc, ones_row, rows[0:1, 0:TT], start=True, stop=True),
             reads=["r0", "onesc", "gb6"], writes=["gb6"])
        p.op("scalar", lambda e, d=xsb[pr]: e.copy(out=d[:, :], in_=ps_bc),
             reads=["gb6"], writes=[f"xsb{pr}"])
        p.op("tensor", lambda e: e.matmul(ps_bc, ones_row, rows[0:1, TT:2 * TT], start=True, stop=True),
             reads=["r1", "onesc", "gb6"], writes=["gb6"])
        p.op("scalar", lambda e, d=invsb[pr]: e.copy(out=d[:, :], in_=ps_bc),
             reads=["gb6"], writes=[f"invsb{pr}"])

    def quant_chunk(pr, srcbuf, srck, c):
        p.op("vector", lambda e, c=c, s=srcbuf: e.tensor_tensor(uu[:, :], ch(s, c),
                                                                invsb[pr][:, :], A.mult),
             reads=[srck(c), f"invsb{pr}"], writes=["lq"])
        _ladder_floor(p, uu[:, :], u0[:, :], uu[:, :], u2[:, :],
                      LADDER_BITS, ("lq", "l0", "lq", "l2"), 128.5)
        p.op("vector", lambda e: e.tensor_scalar(u2[:, :], u2[:, :],
                                                 -128.0, -127.0, A.add, A.max),
             reads=["l2"], writes=["l2"])
        p.op("vector", lambda e, c=c, hb=hixq[pr]: e.tensor_scalar(
            ch(hb, c), u2[:, :], 127.0, None, A.min),
            reads=["l2"], writes=[HXk(pr, c)])

    def ffn_quarter(pr, q, ph):
        fwslX = fwslA if ph == "A" else fwslB
        ps_bkX = ps_bkA if ph == "A" else ps_bkB
        PBKX = PBKA if ph == "A" else PBKB
        fs = "fa" if ph == "A" else "fb"
        for k in range(NCH):
            fslot = k % 3
            fkey = f"{fs}sl{fslot}"
            p.op("sync", lambda e, k=k, s=fslot, q=q, fb=fwslX: e.dma_start(
                out=fb[s][:, :], in_=wtscr[k * 128:(k + 1) * 128,
                                          q * 512:(q + 1) * 512]),
                reads=["wtscrk"], writes=[fkey], dma=True,
                stream=f"{fs}{fslot}")
            for j in range(4):
                p.op("tensor", lambda e, k=k, j=j, s=fslot, fb=fwslX,
                     hb=hixq[pr]: e.matmul(
                    ps_bkX[j], fb[s][:, j * 128:(j + 1) * 128], ch(hb, k),
                    start=(k == 0 and j % 2 == 0),
                    stop=(k == NCH - 1 and j % 2 == 1)),
                    reads=[fkey, HXk(pr, k)], writes=[PBKX[j]])
            if k % 4 == 3:
                yield
        for j in range(4):
            n = q * 4 + j
            slot = n % 2
            p.op("scalar", lambda e, n=n, j=j, s=slot: e.mul(
                ev[:, s * TT:(s + 1) * TT], ps_bkX[j], gamma[:, n:n + 1]),
                reads=[PBKX[j], "gamma"], writes=[f"ev{slot}"])
            p.op("vector", lambda e, n=n, s=slot: e.tensor_tensor(
                ev2[:, s * TT:(s + 1) * TT], ev[:, s * TT:(s + 1) * TT],
                xsb[pr][:, :], A.mult),
                reads=[f"ev{slot}", f"xsb{pr}"], writes=[f"fe{slot}"])
            p.op("vector", lambda e, n=n, s=slot: e.tensor_scalar(
                ev2[:, s * TT:(s + 1) * TT], ev2[:, s * TT:(s + 1) * TT],
                0.5, hbias[:, n:n + 1], A.mult, A.add),
                reads=[f"fe{slot}", "hbias"], writes=[f"fe{slot}"])
            p.op("gpsimd" if n % 2 == 0 else "vector",
                 lambda e, n=n, s=slot, xb=xres[pr]: e.tensor_tensor(
                ch(xb, n), ch(xb, n), ev2[:, s * TT:(s + 1) * TT], A.add),
                reads=[f"fe{slot}", XRk(pr, n)], writes=[XRk(pr, n)])
        yield

    def stream_proj(pr, widx, q, evac, ph):
        wslX = wslA if ph == "A" else wslB
        ps_bkX = ps_bkA if ph == "A" else ps_bkB
        PBKX = PBKA if ph == "A" else PBKB
        ws = "wa" if ph == "A" else "wb"
        nslot = 3 if ph == "A" else 2
        nh, qq = q // 2, q % 2
        for k in range(NCH):
            slot = k % nslot
            wkey = f"{ws}sl{slot}"
            r0 = nh * D + k * 128
            p.op("sync", lambda e, r0=r0, s=slot, w=widx, qq=qq,
                 wb=wslX: e.dma_start(
                out=wb[s][:, :], in_=wscr[w][r0:r0 + 128,
                                             qq * 1024:(qq + 1) * 1024]),
                reads=[f"wscr{widx}"], writes=[wkey], dma=True,
                stream=f"{ws}{slot}")
            for j in range(4):
                whi = wslX[slot][:, j * 128:(j + 1) * 128]
                wlo = wslX[slot][:, 512 + j * 128:512 + (j + 1) * 128]
                xh = ch(hixq[pr], k)
                xl = ch(lobf[pr], k)
                p.op("tensor", lambda e, o=ps_bkX[j], a=whi, bb=xh,
                     f=(k == 0 and j % 2 == 0): e.matmul(o, a, bb, start=f, stop=False),
                     reads=[wkey, HXk(pr, k)], writes=[PBKX[j]])
                p.op("tensor", lambda e, o=ps_bkX[j], a=whi, bb=xl: e.matmul(
                    o, a, bb, start=False, stop=False),
                    reads=[wkey, LOk(pr, k)], writes=[PBKX[j]])
                p.op("tensor", lambda e, o=ps_bkX[j], a=wlo, bb=xh,
                     last=(k == NCH - 1 and j % 2 == 1): e.matmul(
                    o, a, bb, start=False, stop=last),
                    reads=[wkey, HXk(pr, k)], writes=[PBKX[j]])
            if k % 4 == 3:
                yield
        for j in range(4):
            evac(q * 4 + j, ps_bkX[j], PBKX[j])
        yield

    def gla_head(pr, h, has_q, tile_idx):
        i = h % 2
        KHk, GHk, THk = f"kh{i}", f"gh{i}", f"th{i}"
        BTk, E1k, E2k = f"bt{i}", f"e1{i}", f"e2{i}"
        QIk, QTk = f"qint{i}", f"qitr{i}"
        KOk = f"kot{i}"
        PPk, KTk, VTk = f"Pp{i}", f"kTs{i}", f"vTs{i}"
        S8k, SEk = f"st8{i}", f"stE{i}"
        p.op("scalar", lambda e, h=h: e.activation(out=th[i][:, :], in_=ch(kraw, h), func=AF.Exp),
             reads=[KR[h]], writes=[THk])
        p.op("vector", lambda e: e.tensor_scalar(th[i][:, :], th[i][:, :], 1.0, None, A.add),
             reads=[THk], writes=[THk])
        p.op("scalar", lambda e: e.activation(out=gh[i][:, :], in_=th[i][:, :], func=AF.Ln),
             reads=[THk], writes=[GHk])
        p.op("vector", lambda e, h=h: e.tensor_tensor(th[i][:, :], ch(kraw, h), gh[i][:, :], A.subtract),
             reads=[KR[h], GHk], writes=[THk])
        p.op("scalar", lambda e: e.activation(out=kh[i][:, :], in_=th[i][:, :], func=AF.Exp),
             reads=[THk], writes=[KHk])
        p.op("vector", lambda e: e.tensor_scalar(gh[i][:, :], gh[i][:, :], -1.0, None, A.mult),
             reads=[GHk], writes=[GHk])
        Sh = Sst[:, h * 128:(h + 1) * 128]
        H = C // 2
        for c in (0, 1):
            csl = slice(c * C, (c + 1) * C)
            p.op("tensor", lambda e, h=h, c=c: e.transpose(
                ps_T, vbuf[:, h * TT + c * C:h * TT + (c + 1) * C], ident[:, :]),
                reads=[VB[h], "ident", "gb1"], writes=["gb1"])
            p.op("scalar", lambda e: e.copy(out=vTs[i][:, :], in_=ps_T),
                 reads=["gb1"], writes=[VTk])
            if tile_idx == 0:
                p.op("vector", lambda e, c=c: e.tensor_scalar(
                    vTs[i][:, :], vTs[i][:, :], halom[:, c:c + 1], None, A.mult),
                    reads=[VTk, "halom"], writes=[VTk])
            p.op("vector", lambda e, csl=csl: e.tensor_tensor_scan(
                bt[i][:, :], gh[i][:, csl], gh[i][:, csl], 0.0, A.add, A.bypass),
                reads=[GHk], writes=[BTk])
            p.op("vector", lambda e: e.tensor_scalar(st8[i][:, 1:2], bt[i][:, H - 1:H],
                                                     0.5, None, A.mult),
                 reads=[BTk], writes=[S8k], narrow=True)
            p.op("vector", lambda e: e.tensor_scalar(st8[i][:, 2:3], bt[i][:, C - 1:C],
                                                     0.5, st8[i][:, 1:2], A.mult, A.add),
                 reads=[BTk, S8k], writes=[S8k], narrow=True)
            p.op("vector", lambda e: e.tensor_tensor(st8[i][:, 3:4], bt[i][:, C - 1:C],
                                                     st8[i][:, 1:2], A.subtract),
                 reads=[BTk, S8k], writes=[S8k], narrow=True)
            p.op("vector", lambda e: e.tensor_tensor(st8[i][:, 0:1], bt[i][:, C - 1:C],
                                                     bt[i][:, H - 1:H], A.subtract),
                 reads=[BTk, S8k], writes=[S8k], narrow=True)
            p.op("scalar", lambda e: e.activation(out=stE[i][:, 0:1], in_=bt[i][:, C - 1:C],
                                                  func=AF.Exp),
                 reads=[BTk], writes=[SEk])
            p.op("scalar", lambda e: e.activation(out=stE[i][:, 1:2], in_=st8[i][:, 3:4],
                                                  func=AF.Exp),
                 reads=[S8k], writes=[SEk])
            p.op("scalar", lambda e: e.activation(out=stE[i][:, 2:3], in_=st8[i][:, 0:1],
                                                  func=AF.Exp, scale=0.5),
                 reads=[S8k], writes=[SEk])
            p.op("scalar", lambda e: e.activation(out=stE[i][:, 3:4], in_=bt[i][:, H - 1:H],
                                                  func=AF.Exp, scale=0.5),
                 reads=[BTk], writes=[SEk])
            p.op("scalar", lambda e: e.activation(out=e1[i][:, :], in_=bt[i][:, :], func=AF.Exp),
                 reads=[BTk], writes=[E1k])
            if has_q:
                qsl = qhalf[:, h * TT + c * C:h * TT + (c + 1) * C]
                p.op("vector", lambda e, q=qsl: e.tensor_tensor(qint[i][:, :], q, e1[i][:, :], A.mult),
                     reads=[QH[h], E1k], writes=[QIk])
            p.op("vector", lambda e: e.tensor_scalar(bt[i][:, 0:H], bt[i][:, 0:H], -1.0,
                                                     st8[i][:, 1:2], A.mult, A.add),
                 reads=[BTk, S8k], writes=[BTk], narrow=True)
            p.op("vector", lambda e: e.tensor_scalar(bt[i][:, H:C], bt[i][:, H:C], -1.0,
                                                     st8[i][:, 2:3], A.mult, A.add),
                 reads=[BTk, S8k], writes=[BTk], narrow=True)
            p.op("vector", lambda e: e.tensor_scalar(bt[i][:, :], bt[i][:, :],
                                                     80.0, -80.0, A.min, A.max),
                 reads=[BTk], writes=[BTk])
            p.op("scalar", lambda e: e.activation(out=e2[i][:, :], in_=bt[i][:, :], func=AF.Exp),
                 reads=[BTk], writes=[E2k])
            if has_q:
                p.op("scalar", lambda e: e.activation(out=e1[i][:, :], in_=bt[i][:, :],
                                                      func=AF.Exp, scale=-1.0),
                     reads=[BTk], writes=[E1k])
                p.op("vector", lambda e, q=qsl: e.tensor_tensor(qitr[i][:, :], q, e1[i][:, :], A.mult),
                     reads=[QH[h], E1k], writes=[QTk])
            p.op("vector", lambda e, csl=csl: e.tensor_tensor(kot[i][:, :], kh[i][:, csl],
                                                              e2[i][:, :], A.mult),
                 reads=[KHk, E2k], writes=[KOk])
            if has_q:
                p.op("vector", lambda e: e.tensor_scalar(e1[i][:, 0:H], kot[i][:, 0:H],
                                                         stE[i][:, 3:4], None, A.mult),
                     reads=[KOk, SEk], writes=[E1k], narrow=True)
                p.op("vector", lambda e: e.tensor_scalar(e1[i][:, H:C], qitr[i][:, H:C],
                                                         stE[i][:, 2:3], None, A.mult),
                     reads=[QTk, SEk], writes=[E1k], narrow=True)
                p.op("tensor", lambda e: e.matmul(ps_A[0:H, 0:H], kot[i][:, 0:H],
                                                  qitr[i][:, 0:H],
                                                  start=True, stop=False),
                     reads=[KOk, QTk, "gb0"], writes=["gb0"])
                p.op("tensor", lambda e: e.matmul(ps_A[H:C, H:C], kot[i][:, H:C],
                                                  qitr[i][:, H:C],
                                                  start=True, stop=True),
                     reads=[KOk, QTk], writes=["gb0"])
                p.op("tensor", lambda e: e.matmul(ps_A[0:H, H:C], e1[i][:, 0:H],
                                                  e1[i][:, H:C],
                                                  start=False, stop=True),
                     reads=[E1k], writes=["gb0"])
                p.op("vector", lambda e: e.copy_predicated(Pp[i][0:H, 0:H],
                                                           causal[0:H, 0:H],
                                                           ps_A[0:H, 0:H]),
                     reads=["gb0", "causal", f"Ppz{i}"], writes=[PPk])
                p.op("vector", lambda e: e.copy_predicated(Pp[i][H:C, H:C],
                                                           causal[H:C, H:C],
                                                           ps_A[H:C, H:C]),
                     reads=["gb0", "causal", PPk], writes=[PPk])
                p.op("scalar", lambda e: e.copy(out=Pp[i][0:H, H:C], in_=ps_A[0:H, H:C]),
                     reads=["gb0", PPk], writes=[PPk])
                p.op("tensor", lambda e: e.matmul(ps_O, vTs[i][:, :], Pp[i][:, :],
                                                  start=True, stop=False),
                     reads=[VTk, PPk, "gb0"], writes=["gb0"])
                p.op("tensor", lambda e, S=Sh: e.matmul(ps_O, S, qint[i][:, :],
                                                        start=False, stop=True),
                     reads=[SK[h], QIk], writes=["gb0"])
                p.op("scalar", lambda e, h=h, c=c, ob=xno[pr]: e.copy(
                    out=ob[:, h * TT + c * C:h * TT + (c + 1) * C], in_=ps_O),
                    reads=["gb0"], writes=[XNk(pr, h)])
            p.op("vector", lambda e: e.tensor_scalar(kot[i][:, 0:H], kot[i][:, 0:H],
                                                     stE[i][:, 1:2], None, A.mult),
                 reads=[KOk, SEk], writes=[KOk], narrow=True)
            p.op("vector", lambda e: e.tensor_scalar(kot[i][:, H:C], kot[i][:, H:C],
                                                     stE[i][:, 2:3], None, A.mult),
                 reads=[KOk, SEk], writes=[KOk], narrow=True)
            p.op("tensor", lambda e: e.transpose(ps_T, kot[i][:, :], ident[:, :]),
                 reads=[KOk, "ident", "gb1"], writes=["gb1"])
            p.op("scalar", lambda e: e.copy(out=kTs[i][:, :], in_=ps_T),
                 reads=["gb1"], writes=[KTk])
            p.op("tensor", lambda e: e.matmul(ps_S, kTs[i][:, :], vTs[i][:, :],
                                              start=True, stop=True),
                 reads=[KTk, VTk, "gb1"], writes=["gb1"])
            p.op("vector", lambda e, S=Sh: e.tensor_scalar(S, S, stE[i][:, 0:1], None, A.mult),
                 reads=[SK[h], SEk], writes=[SK[h]])
            p.op("vector", lambda e, S=Sh: e.tensor_tensor(S, S, ps_S, A.add),
                 reads=[SK[h], "gb1"], writes=[SK[h]])

    def dbg_dump(name, buf, keys):
        if name in dbg_t:
            w = buf[:, :].shape[1]
            p.op("sync", lambda e, d=dbg_t[name], s=buf, w=w: e.dma_start(
                out=d[:, 0:w], in_=s[:, :]),
                reads=list(keys), writes=[], dma=True, stream="dbg")

    def phaseA(tile):
        if tile >= NTILES:
            return
        pr = tile % 2
        halo_tile = (tile == 0)
        src = x_fm.rearrange("(c p) t -> p c t", p=128)[:, :, tile * TT:(tile + 1) * TT]
        dst = xres[pr][:, :].rearrange("p (c t) -> p c t", t=TT)
        p.op("sync", lambda e, s=src, d=dst: e.dma_start(out=d, in_=s),
             reads=[XRk(pr, c) for c in range(NCH)],
             writes=[XRk(pr, c) for c in range(NCH)], dma=True,
             stream=f"dx{pr}")
        yield
        ln_stats(pr, xres[pr], lambda c: XRk(pr, c), with_rms=True)
        yield
        ln_apply(pr, xres[pr], lambda c: XRk(pr, c), xno[pr], lambda c: XNk(pr, c))
        yield
        absmax_calc(pr, xno[pr], lambda c: XNk(pr, c))
        yield
        for c in range(NCH):
            quant_chunk(pr, xno[pr], lambda cc: XNk(pr, cc), c)
            if c % 4 == 3:
                yield
        for q in range(4):
            for _ in ffn_quarter(pr, q, "A"):
                yield
        if tile == 1:
            dbg_dump("x1", xres[pr], [XRk(pr, c) for c in range(NCH)])
        ln_stats(pr, xres[pr], lambda c: XRk(pr, c), with_rms=False)
        yield
        ln_apply(pr, xres[pr], lambda c: XRk(pr, c), xno[pr], lambda c: XNk(pr, c))
        yield
        for c in range(NCH):
            eng = "gpsimd" if c % 2 == 0 else "vector"
            p.op(eng, lambda e, c=c, xb=xno[pr], hb=hixq[pr]: e.tensor_copy(
                ch(hb, c), ch(xb, c)),
                reads=[XNk(pr, c)], writes=[HXk(pr, c)])
            p.op(eng, lambda e, c=c, xb=xno[pr], hb=hixq[pr], lb=lobf[pr]: e.tensor_tensor(
                ch(lb, c), ch(xb, c), ch(hb, c), A.subtract),
                reads=[XNk(pr, c), HXk(pr, c)], writes=[LOk(pr, c)])
        yield

        def evac_v(n, ps, pbk):
            p.op("scalar", lambda e, n=n, ps=ps: e.copy(out=ch(vbuf, n), in_=ps),
                 reads=[pbk], writes=[VB[n]])
        for q in range(4):
            for _ in stream_proj(pr, 2, q, evac_v, "A"):
                yield

        def evac_k(n, ps, pbk):
            p.op("scalar", lambda e, n=n, ps=ps: e.copy(out=ch(kraw, n), in_=ps),
                 reads=[pbk], writes=[KR[n]])
        for q in range(4):
            for _ in stream_proj(pr, 1, q, evac_k, "A"):
                yield

        if not halo_tile:
            def evac_q(n, ps, pbk):
                p.op("scalar", lambda e, n=n, ps=ps: e.activation(
                    out=qhalf[:, n * TT:(n + 1) * TT], in_=ps,
                    func=AF.Relu, scale=SCALE),
                    reads=[pbk], writes=[QH[n]])
            for q in range(4):
                for _ in stream_proj(pr, 0, q, evac_q, "A"):
                    yield

    def phaseB(tile):
        pr = tile % 2
        halo_tile = (tile == 0)
        if halo_tile:
            for h in range(NH):
                gla_head(pr, h, has_q=False, tile_idx=tile)
                yield
            return
        for h in range(NH):
            gla_head(pr, h, has_q=True, tile_idx=tile)
            yield
        if tile == 1:
            dbg_dump("o1", xno[pr], [XNk(pr, c) for c in range(NCH)])
        for c in range(NCH):
            slot = c % 2
            p.op("scalar", lambda e, c=c, s=slot, ob=xno[pr]: e.activation(
                out=sq[:, s * TT:(s + 1) * TT], in_=ch(ob, c), func=AF.Square),
                reads=[XNk(pr, c)], writes=[f"sq{slot}"])
            p.op("tensor", lambda e, c=c, s=slot: e.matmul(
                ps_row2, ones_col, sq[:, s * TT:(s + 1) * TT],
                start=(c == 0), stop=(c == NCH - 1)),
                reads=[f"sq{slot}", "onesc"], writes=["gb7"])
        p.op("vector", lambda e: e.tensor_scalar(row(6), ps_row2, 1.0 / 2048.0, None, A.mult),
             reads=["gb7"], writes=[rkey(6)], narrow=True)
        p.op("vector", lambda e: e.tensor_scalar(row(6), row(6), EPS_ATTN, None, A.add),
             reads=[rkey(6)], writes=[rkey(6)], narrow=True)
        p.op("scalar", lambda e: e.activation(out=row(7), in_=row(6), func=AF.Ln),
             reads=[rkey(6)], writes=[rkey(7)])
        p.op("vector", lambda e: e.tensor_scalar(row(7), row(7), -0.5, None, A.mult),
             reads=[rkey(7)], writes=[rkey(7)], narrow=True)
        p.op("scalar", lambda e: e.activation(out=row(6), in_=row(7), func=AF.Exp),
             reads=[rkey(7)], writes=[rkey(6)])
        p.op("tensor", lambda e: e.matmul(ps_bc, ones_row, row(6), start=True, stop=True),
             reads=[rkey(6), "onesc", "gb6"], writes=["gb6"])
        p.op("scalar", lambda e, d=rb[pr]: e.copy(out=d[:, :], in_=ps_bc),
             reads=["gb6"], writes=[f"rb{pr}"])
        yield
        for c in range(NCH):
            slot = c % 2
            p.op("vector", lambda e, c=c, s=slot, ob=xno[pr]: e.tensor_tensor(
                ev[:, s * TT:(s + 1) * TT], ch(ob, c), rb[pr][:, :], A.mult),
                reads=[XNk(pr, c), f"rb{pr}"], writes=[f"ev{slot}"])
            p.op("vector", lambda e, c=c, s=slot: e.tensor_scalar(
                ev2[:, s * TT:(s + 1) * TT], ev[:, s * TT:(s + 1) * TT],
                gwb[:, c:c + 1], None, A.mult),
                reads=[f"ev{slot}", "gwb"], writes=[f"fe{slot}"])
            p.op("gpsimd" if c % 2 == 0 else "vector",
                 lambda e, c=c, s=slot, hb=hixq[pr]: e.tensor_copy(
                ch(hb, c), ev2[:, s * TT:(s + 1) * TT]),
                reads=[f"fe{slot}"], writes=[HXk(pr, c)])
            p.op("gpsimd" if c % 2 == 0 else "vector",
                 lambda e, c=c, s=slot, hb=hixq[pr], lb=lobf[pr]: e.tensor_tensor(
                ch(lb, c), ev2[:, s * TT:(s + 1) * TT], ch(hb, c), A.subtract),
                reads=[f"fe{slot}", HXk(pr, c)], writes=[LOk(pr, c)])
        yield

        def evac_wo(n, ps, pbk):
            p.op("vector", lambda e, n=n, ps=ps, xb=xres[pr]: e.tensor_tensor(
                ch(xb, n), ch(xb, n), ps, A.add),
                reads=[pbk, XRk(pr, n)], writes=[XRk(pr, n)])
        for q in range(4):
            for _ in stream_proj(pr, 3, q, evac_wo, "B"):
                yield

        ln_stats(pr, xres[pr], lambda c: XRk(pr, c), with_rms=True)
        yield
        ln_apply(pr, xres[pr], lambda c: XRk(pr, c), xno[pr], lambda c: XNk(pr, c))
        yield
        absmax_calc(pr, xno[pr], lambda c: XNk(pr, c))
        yield
        for c in range(NCH):
            quant_chunk(pr, xno[pr], lambda cc: XNk(pr, cc), c)
            if c % 4 == 3:
                yield
        for q in range(4):
            for _ in ffn_quarter(pr, q, "B"):
                yield

        osrc = xres[pr][:, :].rearrange("p (c t) -> p c t", t=TT)
        odst = y_fm.rearrange("(c p) t -> p c t", p=128)[
            :, :, (tile - 1) * TT:tile * TT]
        p.op("sync", lambda e, s=osrc, d=odst: e.dma_start(out=d, in_=s),
             reads=[XRk(pr, c) for c in range(NCH)] + ["outdma"],
             writes=["outdma"], dma=True, stream="do")
        yield

    def drain(gen):
        for _ in gen:
            pass

    drain(phaseA(0))
    PRE = 8
    for tile in range(NTILES):
        gb = phaseB(tile)
        ga = phaseA(tile + 1)
        for _ in range(PRE):
            if next(ga, StopIteration) is StopIteration:
                break
        while True:
            db = next(gb, StopIteration)
            da = next(ga, StopIteration)
            if db is StopIteration and da is StopIteration:
                break

    p.op("sync", lambda e: e.nop(),
         reads=["outdma"] + [XRk(pp, c) for pp in (0, 1) for c in range(NCH)] + SK,
         writes=[])

    emit_engine = p.emit(nc, sems)
    with nc.Block() as blk:
        blk.sync(lambda e: emit_engine("sync", e))
        blk.tensor(lambda e: emit_engine("tensor", e))
        blk.vector(lambda e: emit_engine("vector", e))
        blk.scalar(lambda e: emit_engine("scalar", e))
        blk.gpsimd(lambda e: emit_engine("gpsimd", e))
    es.close()
    return nc


def kernel(L0_attn_hidden_states, L0_attn_Wq, L0_attn_Wk, L0_attn_Wi, L0_attn_Wo,
           L0_attn_g_weight, L0_ffn_weight, L0_ffn_bias):
    x = np.asarray(L0_attn_hidden_states, dtype=np.float32)
    wq = np.ascontiguousarray(np.asarray(L0_attn_Wq, dtype=np.float32))
    wk = np.ascontiguousarray(np.asarray(L0_attn_Wk, dtype=np.float32))
    wi = np.ascontiguousarray(np.asarray(L0_attn_Wi, dtype=np.float32))
    wo = np.ascontiguousarray(np.asarray(L0_attn_Wo, dtype=np.float32))
    fw = np.ascontiguousarray(np.asarray(L0_ffn_weight, dtype=np.float32))
    fb = np.asarray(L0_ffn_bias, dtype=np.float32)
    gw = np.asarray(L0_attn_g_weight, dtype=np.float32)

    fbias = np.ascontiguousarray(fb.reshape(NCH, 128).T)
    gwin = np.ascontiguousarray(gw.reshape(NCH, 128).T)
    ident = np.eye(128, dtype=np.float32)
    causal = np.triu(np.ones((128, 128), dtype=np.uint8))
    onesc = np.ones((128, 128), dtype=np.float32)

    nc = build_program()
    in_maps = []
    for ci in range(NCORES):
        bidx, j = ci // 4, ci % 4
        t0 = j * TCORE
        if j == 0:
            xs = np.concatenate([np.zeros((HALO, D), np.float32), x[bidx, 0:TCORE]], axis=0)
            hm = np.zeros((128, 2), np.float32)
        else:
            xs = x[bidx, t0 - HALO:t0 + TCORE]
            hm = np.ones((128, 2), np.float32)
        in_maps.append({
            "x_fm": np.ascontiguousarray(xs.T),
            "wq": wq, "wk": wk, "wi": wi, "wo": wo, "fw": fw,
            "fbias": fbias, "gwin": gwin, "ident_in": ident,
            "causal_in": causal, "halom_in": hm, "onesc_in": onesc,
        })
    res = run_bass_kernel_spmd(nc, in_maps, core_ids=list(range(NCORES)))
    out = np.empty((B, L, D), dtype=np.float32)
    for ci in range(NCORES):
        bidx, j = ci // 4, ci % 4
        out[bidx, j * TCORE:(j + 1) * TCORE, :] = res.results[ci]["y_fm"].T
    if _DBG:
        kernel.dbg = [{k: res.results[ci]["dbg_" + k] for k in _DBG}
                      for ci in range(NCORES)]
    return out


# version 16 (final)
# speedup vs baseline: 1.1559x; 1.0175x over previous
import numpy as np
import concourse.bass as bass
import concourse.mybir as mybir
from concourse.bass_utils import run_bass_kernel_spmd

F32 = mybir.dt.float32
BF16 = mybir.dt.bfloat16
U8 = mybir.dt.uint8
A = mybir.AluOpType
AF = mybir.ActivationFunctionType
AX = mybir.AxisListType

D = 2048
NCH = D // 128
NH = 16
B = 2
L = 8192
NCORES = 8
HALO = 256
TCORE = 2048
TTOT = HALO + TCORE
TT = 256
NTILES = TTOT // TT
C = 128
SCALE = 128.0 ** -0.5
EPS_LN = 1e-5
EPS_RMS_FFN = 1e-8
EPS_FFN = 1e-8
EPS_ATTN = 1e-5
QB = 127.0
LADDER_BITS = [128.0, 64.0, 32.0, 16.0, 8.0, 4.0, 2.0, 1.0]
WT_BITS = [4.0, 2.0, 1.0]

_DBG = []

SK = [f"S{h}" for h in range(NH)]
QH = [f"qh{j}" for j in range(NCH)]
KR = [f"kr{c}" for c in range(NCH)]
VB = [f"vb{c}" for c in range(NCH)]


class Plan:
    def __init__(self):
        self.ops = []
        self.writer = {}
        self.readers = {}

    def op(self, engine, fn, reads=(), writes=(), dma=False, narrow=False,
           stream=None):
        deps = {}
        for bb in reads:
            w = self.writer.get(bb)
            if w is not None:
                deps[w] = "rw"
        for bb in writes:
            w = self.writer.get(bb)
            if w is not None:
                deps[w] = "rw"
        for bb in writes:
            for r in self.readers.get(bb, ()):
                deps.setdefault(r, "war")
        i = len(self.ops)
        assert (stream is not None) == dma, "DMA ops need a stream name"
        self.ops.append(dict(engine=engine, fn=fn, deps=deps, dma=dma,
                             narrow=narrow, stream=stream))
        for bb in reads:
            self.readers.setdefault(bb, []).append(i)
        for bb in writes:
            self.writer[bb] = i
            self.readers[bb] = []
        return i

    def emit(self, nc, sems):
        ops = self.ops
        engseq = [0] * len(ops)
        cnt = {}
        for i, o in enumerate(ops):
            engseq[i] = cnt.get(o["engine"], 0)
            cnt[o["engine"]] = engseq[i] + 1
        interlock = [set() for _ in ops]
        needs_inc = [False] * len(ops)
        for i, o in enumerate(ops):
            for d, typ in o["deps"].items():
                if ops[d]["dma"] or ops[d]["engine"] != o["engine"]:
                    needs_inc[d] = True
                elif typ == "rw" and o["engine"] != "tensor":
                    needs_inc[d] = True
                    interlock[i].add(d)
        for i, o in enumerate(ops):
            if o["dma"]:
                needs_inc[i] = True
        last_of = {}
        for i, o in enumerate(ops):
            last_of[o["engine"]] = i
        for i in last_of.values():
            needs_inc[i] = True

        def stream_of(i):
            return ops[i]["stream"] if ops[i]["dma"] else ops[i]["engine"]

        cum = {}
        opcum = [0] * len(ops)
        pfx = [dict() for _ in ops]
        run_dma = {}
        for i, o in enumerate(ops):
            pfx[i] = dict(run_dma)
            st = stream_of(i)
            amt = 16 if o["dma"] else 1
            if needs_inc[i]:
                cum[st] = cum.get(st, 0) + amt
                if o["dma"]:
                    run_dma[st] = cum[st]
            opcum[i] = cum.get(st, 0)
        hiwater = {}

        def emit_engine(eng_name, eng):
            hw = hiwater.setdefault(eng_name, {})
            for i, o in enumerate(ops):
                if o["engine"] != eng_name:
                    continue
                want = {}
                for d in o["deps"]:
                    do = ops[d]
                    if (do["engine"] == eng_name and not do["dma"]
                            and d not in interlock[i]):
                        continue
                    st = stream_of(d)
                    if do["dma"]:
                        val = pfx[i].get(st, 0)
                    else:
                        val = opcum[d]
                    if want.get(st, 0) < val:
                        want[st] = val
                for st in sorted(want):
                    val = want[st]
                    if hw.get(st, 0) < val:
                        eng.wait_ge(sems[st], val)
                        hw[st] = val
                ins = o["fn"](eng)
                if needs_inc[i]:
                    ins.then_inc(sems[stream_of(i)], 16 if o["dma"] else 1)
        return emit_engine


def _ladder_floor(p, v_, u0, uu, u2, bits, keys, pre_add, eng="vector"):
    kv, k0, ku, k2 = keys
    p.op(eng, lambda e, a=v_, o=u0, c=pre_add: e.tensor_scalar(o, a, c, None, A.add),
         reads=[kv], writes=[k0])
    first = True
    for bit in bits:
        src = u0 if first else uu
        ksrc = k0 if first else ku
        p.op(eng, lambda e, s=src, o=u2, bb=bit: e.tensor_scalar(o, s, bb, -bb, A.is_ge, A.mult),
             reads=[ksrc], writes=[k2])
        p.op(eng, lambda e, s=src, o=uu, c=u2: e.tensor_tensor(o, s, c, A.add),
             reads=[ksrc, k2], writes=[ku])
        first = False
    p.op(eng, lambda e, o=u2, x=u0, y=uu: e.tensor_tensor(o, x, y, A.subtract),
         reads=[k0, ku], writes=[k2])


def build_program():
    nc = bass.Bass("TRN2", target_bir_lowering=False)

    x_fm = nc.dram_tensor("x_fm", [D, TTOT], F32, kind="ExternalInput").ap()
    wq = nc.dram_tensor("wq", [D, D], F32, kind="ExternalInput").ap()
    wk = nc.dram_tensor("wk", [D, D], F32, kind="ExternalInput").ap()
    wi = nc.dram_tensor("wi", [D, D], F32, kind="ExternalInput").ap()
    wo = nc.dram_tensor("wo", [D, D], F32, kind="ExternalInput").ap()
    fw = nc.dram_tensor("fw", [D, D], F32, kind="ExternalInput").ap()
    fbias = nc.dram_tensor("fbias", [128, NCH], F32, kind="ExternalInput").ap()
    gwin = nc.dram_tensor("gwin", [128, NCH], F32, kind="ExternalInput").ap()
    ident_in = nc.dram_tensor("ident_in", [128, 128], F32, kind="ExternalInput").ap()
    causal_in = nc.dram_tensor("causal_in", [128, 128], U8, kind="ExternalInput").ap()
    halom_in = nc.dram_tensor("halom_in", [128, 2], F32, kind="ExternalInput").ap()
    onesc_in = nc.dram_tensor("onesc_in", [128, 128], F32, kind="ExternalInput").ap()
    y_fm = nc.dram_tensor("y_fm", [D, TCORE], F32, kind="ExternalOutput").ap()
    wscr = [nc.dram_tensor(f"wscr{i}", [2 * D, D], BF16, kind="Internal").ap()
            for i in range(4)]
    wtscr = nc.dram_tensor("wtscr", [NCH * 128, D], BF16, kind="Internal").ap()
    dbg_t = {}
    for name in _DBG:
        dbg_t[name] = nc.dram_tensor("dbg_" + name, [128, 4096], F32,
                                     kind="ExternalOutput").ap()

    from contextlib import ExitStack
    es = ExitStack()

    def sb(name, shape, dt=F32):
        return es.enter_context(nc.sbuf_tensor(name, shape, dt))

    xres = [sb(f"xres{i}", [128, NCH * TT]) for i in range(2)]
    xno = [sb(f"xno{i}", [128, NCH * TT]) for i in range(2)]
    hixq = [sb(f"hixq{i}", [128, NCH * TT], BF16) for i in range(2)]
    lobf = [sb(f"lobf{i}", [128, NCH * TT], BF16) for i in range(2)]
    vbuf = sb("vbuf", [128, NCH * TT])
    kraw = sb("kraw", [128, NCH * TT])
    wslA = [sb(f"wslA{i}", [128, 1024], BF16) for i in range(3)]
    wslB = [sb(f"wslB{i}", [128, 1024], BF16) for i in range(2)]
    fwslA = [sb(f"fwslA{i}", [128, 512], BF16) for i in range(2)]
    fwslB = [sb(f"fwslB{i}", [128, 512], BF16) for i in range(2)]
    qhalf = sb("qhalf", [128, NCH * TT])
    kh = [sb(f"kh{i}", [128, TT]) for i in range(2)]
    gh = [sb(f"gh{i}", [128, TT]) for i in range(2)]
    th = [sb(f"th{i}", [128, TT]) for i in range(2)]
    bt = [sb(f"bt{i}", [128, C]) for i in range(2)]
    e1 = [sb(f"e1{i}", [128, C]) for i in range(2)]
    e2 = [sb(f"e2{i}", [128, C]) for i in range(2)]
    qint = [sb(f"qint{i}", [128, C]) for i in range(2)]
    qitr = [sb(f"qitr{i}", [128, C]) for i in range(2)]
    kot = [sb(f"kot{i}", [128, C]) for i in range(2)]
    Pp = [sb(f"Pp{i}", [128, C]) for i in range(2)]
    kTs = [sb(f"kTs{i}", [128, C]) for i in range(2)]
    vTs = [sb(f"vTs{i}", [128, C]) for i in range(2)]
    st8 = [sb(f"st8{i}", [128, 4]) for i in range(2)]
    stE = [sb(f"stE{i}", [128, 4]) for i in range(2)]
    Sst = sb("Sst", [128, NH * 128])
    sq = sb("sq", [128, 2 * TT])
    u0 = sb("u0", [128, 2 * TT])
    uu = sb("uu", [128, 2 * TT])
    u2 = sb("u2", [128, 2 * TT])
    rows = sb("rows", [1, 6 * TT])
    colsb = sb("colsb", [128, 16])
    rb = [sb(f"rb{i}", [128, TT]) for i in range(2)]
    ab = [sb(f"ab{i}", [128, TT]) for i in range(2)]
    xsb = [sb(f"xsb{i}", [128, TT]) for i in range(2)]
    invsb = [sb(f"invsb{i}", [128, TT]) for i in range(2)]
    gamma = sb("gamma", [128, NCH])
    hbias = sb("hbias", [128, NCH])
    gwb = sb("gwb", [128, NCH])
    ident = sb("ident", [128, 128])
    causal = sb("causal", [128, 128], U8)
    halom = sb("halom", [128, 2])
    onesc = sb("onesc", [128, 128])
    ev = sb("ev", [128, 2 * TT])

    PS = es.enter_context(nc.psum_tensor("PS", [128, 4096], F32))
    ps_A = PS[:, 0:128]
    ps_O = PS[:, 256:384]
    ps_T = PS[:, 512:640]
    ps_S = PS[:, 768:896]
    ps_bkA = [PS[:, (2 + j // 2) * 512 + (j % 2) * TT:
                 (2 + j // 2) * 512 + (j % 2) * TT + TT] for j in range(4)]
    ps_bkB = [PS[:, (4 + j // 2) * 512 + (j % 2) * TT:
                 (4 + j // 2) * 512 + (j % 2) * TT + TT] for j in range(4)]
    PBKA = [f"pb{2 + j // 2}" for j in range(4)]
    PBKB = [f"pb{4 + j // 2}" for j in range(4)]
    ps_tr = PS[:, 6 * 512:6 * 512 + 128]
    ps_rr = PS[0:1, 6 * 512 + 128:6 * 512 + 256]
    ps_bc = PS[:, 6 * 512 + 256:6 * 512 + 512]
    ps_row1 = PS[0:1, 7 * 512:7 * 512 + TT]
    ps_row2 = PS[0:1, 7 * 512 + TT:7 * 512 + 2 * TT]

    sems = {}
    for s in ["sync", "tensor", "vector", "scalar", "gpsimd",
              "dc", "dpl", "dps", "dgp", "dx0", "dx1", "do",
              "wa0", "wa1", "wa2", "wb0", "wb1", "wb2",
              "fa0", "fa1", "fa2", "fb0", "fb1", "fb2",
              "dpg", "dp2", "dp3", "dsc", "dbg"]:
        sems[s] = es.enter_context(nc.semaphore("sem_" + s))

    p = Plan()

    def ch(buf, c, w=TT):
        return buf[:, c * w:(c + 1) * w]

    ROWMAP = {0: 0, 1: 1, 2: 2, 3: 3, 4: 4, 5: 5, 6: 4, 7: 5}

    def row(i):
        s = ROWMAP[i]
        return rows[0:1, s * TT:(s + 1) * TT]

    def rkey(i):
        return f"r{ROWMAP[i]}"

    for dst, src, key in [(ident, ident_in, "ident"), (causal, causal_in, "causal"),
                          (halom, halom_in, "halom"), (onesc, onesc_in, "onesc"),
                          (gwb, gwin, "gwb"), (hbias, fbias, "hbias_raw")]:
        p.op("sync", lambda e, d=dst, s=src: e.dma_start(out=d[:, :], in_=s),
             writes=[key], dma=True, stream="dc")
    p.op("vector", lambda e: e.tensor_scalar(hbias[:, :], hbias[:, :], 0.5, None, A.mult),
         reads=["hbias_raw"], writes=["hbias"], narrow=True)
    for i in range(2):
        p.op("vector", lambda e, i=i: e.memset(Pp[i][:, :], 0.0), writes=[f"Ppz{i}"])
    p.op("vector", lambda e: e.memset(Sst[:, :], 0.0), writes=SK)

    ones_col = onesc[:, 0:1]
    ones_row = onesc[0:1, :]

    PBUF = [vbuf[:, 0:D], vbuf[:, D:2 * D], kraw[:, 0:D], kraw[:, D:2 * D]]
    PLD = [("sync", "dpl"), ("sync", "dp2"), ("gpsimd", "dpg"), ("gpsimd", "dp3")]
    WS = [wq, wk, wi, wo]
    NW = 4 * NCH

    def emit_wload(m):
        widx, k = m // NCH, m % NCH
        par = m % 4
        eng, st = PLD[par]
        p.op(eng, lambda e, s=WS[widx], k=k, sl=PBUF[par]: e.dma_start(
             out=sl, in_=s[k * 128:(k + 1) * 128, :]),
             reads=[], writes=[f"prep_w{par}"], dma=True, stream=st)
    for m in range(4):
        emit_wload(m)
    for m in range(NW):
        widx, k = m // NCH, m % NCH
        par = m % 4
        hpar = m % 2
        slab = PBUF[par]
        hih = hixq[0][:, hpar * D:(hpar + 1) * D]
        loh = lobf[0][:, hpar * D:(hpar + 1) * D]
        wkk, hik, lok = f"prep_w{par}", f"prep_hi{hpar}", f"prep_lo{hpar}"
        p.op("vector", lambda e, sl=slab, hh=hih: e.tensor_copy(hh, sl),
             reads=[wkk], writes=[hik])
        p.op("vector", lambda e, sl=slab, hh=hih, ll=loh: e.tensor_tensor(
             ll, sl, hh, A.subtract),
             reads=[wkk, hik], writes=[lok])
        if m + 4 < NW:
            emit_wload(m + 4)
        for nh in range(2):
            r0 = nh * D + k * 128
            for qq in range(2):
                c0 = hpar * D + nh * 1024 + qq * 512
                sst = "dps" if (nh + qq) % 2 == 0 else "dsc"
                seng = "sync" if sst == "dps" else "scalar"
                p.op(seng, lambda e, r0=r0, w=widx, qq=qq, c0=c0: e.dma_start(
                    out=wscr[w][r0:r0 + 128, qq * 1024:qq * 1024 + 512],
                    in_=hixq[0][:, c0:c0 + 512]),
                    reads=[hik], writes=[f"wscr{widx}"], dma=True, stream=sst)
                p.op(seng, lambda e, r0=r0, w=widx, qq=qq, c0=c0: e.dma_start(
                    out=wscr[w][r0:r0 + 128, qq * 1024 + 512:qq * 1024 + 1024],
                    in_=lobf[0][:, c0:c0 + 512]),
                    reads=[lok], writes=[f"wscr{widx}"], dma=True, stream=sst)

    def emit_fload(n):
        par = n % 2
        p.op("sync" if par == 0 else "gpsimd",
             lambda e, n=n, sl=vbuf[:, par * D:(par + 1) * D]: e.dma_start(
             out=sl, in_=fw[n * 128:(n + 1) * 128, :]),
             reads=[f"prep_hi{par}", f"prep_lo{par}"], writes=[f"prep_w{par}"],
             dma=True, stream="dpl" if par == 0 else "dpg")
    for n in range(2):
        emit_fload(n)
    for n in range(NCH):
        par = n % 2
        slab = vbuf[:, par * D:(par + 1) * D]
        wkk = f"prep_w{par}"
        p.op("vector", lambda e, sl=slab: e.reduce_sum(colsb[:, 1:2], sl, AX.X,
                                                       apply_absolute_value=True),
             reads=[wkk], writes=["c1"], narrow=True)
        p.op("vector", lambda e, n=n: e.tensor_scalar(gamma[:, n:n + 1], colsb[:, 1:2],
                                                      1.0 / 2048.0, None, A.mult),
             reads=["c1"], writes=["gamma"], narrow=True)
        p.op("vector", lambda e, n=n: e.tensor_scalar(colsb[:, 2:3], gamma[:, n:n + 1],
                                                      EPS_FFN, None, A.add),
             reads=["gamma"], writes=["c2"], narrow=True)
        p.op("vector", lambda e: e.reciprocal(colsb[:, 0:1], colsb[:, 2:3]),
             reads=["c2"], writes=["c0"], narrow=True)
        p.op("vector", lambda e, sl=slab: e.tensor_scalar(xno[0][:, 0:D], sl,
                                                          colsb[:, 0:1], None, A.mult),
             reads=[wkk, "c0"], writes=["prep_z"])
        p.op("vector", lambda e: e.tensor_scalar(kraw[:, 0:D], xno[0][:, 0:D],
                                                 0.5, None, A.is_ge),
             reads=["prep_z"], writes=["prep_w2"])
        p.op("vector", lambda e: e.tensor_scalar(kraw[:, D:2 * D], xno[0][:, 0:D],
                                                 -0.5, None, A.is_ge),
             reads=["prep_z"], writes=["prep_w3"])
        p.op("vector", lambda e: e.tensor_tensor(kraw[:, 0:D], kraw[:, 0:D],
                                                 kraw[:, D:2 * D], A.add),
             reads=["prep_w2", "prep_w3"], writes=["prep_w2"])
        p.op("vector", lambda e: e.tensor_scalar(kraw[:, 0:D], kraw[:, 0:D],
                                                 -1.0, None, A.add),
             reads=["prep_w2"], writes=["prep_w2"])
        if n + 2 < NCH:
            emit_fload(n + 2)
        for j in range(NCH):
            p.op("tensor", lambda e, j=j: e.transpose(ps_tr, kraw[:, j * 128:(j + 1) * 128],
                                                      ident[:, :]),
                 reads=["prep_w2", "ident", "gb6"], writes=["gb6"])
            p.op("scalar", lambda e, j=j: e.copy(
                out=hixq[0][:, j * 128:(j + 1) * 128], in_=ps_tr),
                reads=["gb6"], writes=["prep_hi0"])
        for j in range(NCH):
            p.op("gpsimd", lambda e, j=j, n=n: e.dma_start(
                out=wtscr[j * 128:(j + 1) * 128, n * 128:(n + 1) * 128],
                in_=hixq[0][:, j * 128:(j + 1) * 128]),
                reads=["prep_hi0"], writes=["wtscrk"], dma=True, stream="dgp")

    allkeys = (SK + QH + KR + VB
               + [f"{nm}{pp}_{c}" for nm in ("xr", "xn", "hx", "lo")
                  for pp in (0, 1) for c in range(NCH)]
               + ["lq", "l0", "l1", "l2"])
    p.op("sync", lambda e: e.nop(),
         reads=[],
         writes=allkeys + ["prep_w0", "prep_w1", "prep_w2", "prep_w3",
                           "prep_hi0", "prep_hi1",
                           "prep_lo0", "prep_lo1", "prep_z"])

    def XRk(pr, c):
        return f"xr{pr}_{c}"

    def XNk(pr, c):
        return f"xn{pr}_{c}"

    def HXk(pr, c):
        return f"hx{pr}_{c}"

    def LOk(pr, c):
        return f"lo{pr}_{c}"

    def ln_stats(pr, srcbuf, srck, with_rms):
        for c in range(NCH):
            slot = c % 2
            p.op("scalar", lambda e, c=c, s=slot, sbf=srcbuf: e.activation(
                out=sq[:, s * TT:(s + 1) * TT], in_=ch(sbf, c), func=AF.Square),
                reads=[srck(c)], writes=[f"sq{slot}"])
            p.op("tensor", lambda e, c=c, sbf=srcbuf: e.matmul(
                ps_row1, ones_col, ch(sbf, c), start=(c == 0), stop=False),
                reads=[srck(c), "onesc"], writes=["gb7"])
            p.op("tensor", lambda e, c=c, s=slot: e.matmul(
                ps_row2, ones_col, sq[:, s * TT:(s + 1) * TT],
                start=False, stop=(c == NCH - 1)),
                reads=[f"sq{slot}", "onesc"], writes=["gb7"])
        p.op("vector", lambda e: e.tensor_scalar(row(0), ps_row1, 1.0 / 2048.0, None, A.mult),
             reads=["gb7"], writes=[rkey(0)], narrow=True)
        p.op("vector", lambda e: e.tensor_scalar(row(1), ps_row2, 1.0 / 2048.0, None, A.mult),
             reads=["gb7"], writes=[rkey(1)], narrow=True)
        p.op("vector", lambda e: e.tensor_tensor(row(6), row(0), row(0), A.mult),
             reads=[rkey(0)], writes=[rkey(6)], narrow=True)
        p.op("vector", lambda e: e.tensor_tensor(row(2), row(1), row(6), A.subtract),
             reads=[rkey(1), rkey(6)], writes=[rkey(2)], narrow=True)
        p.op("vector", lambda e: e.tensor_scalar(row(6), row(2), EPS_LN, None, A.add),
             reads=[rkey(2)], writes=[rkey(6)], narrow=True)
        p.op("scalar", lambda e: e.activation(out=row(7), in_=row(6), func=AF.Ln),
             reads=[rkey(6)], writes=[rkey(7)])
        p.op("vector", lambda e: e.tensor_scalar(row(7), row(7), -0.5, None, A.mult),
             reads=[rkey(7)], writes=[rkey(7)], narrow=True)
        p.op("scalar", lambda e: e.activation(out=row(3), in_=row(7), func=AF.Exp),
             reads=[rkey(7)], writes=[rkey(3)])
        if with_rms:
            p.op("vector", lambda e: e.tensor_tensor(row(6), row(3), row(3), A.mult),
                 reads=[rkey(3)], writes=[rkey(6)], narrow=True)
            p.op("vector", lambda e: e.tensor_tensor(row(6), row(2), row(6), A.mult),
                 reads=[rkey(2), rkey(6)], writes=[rkey(6)], narrow=True)
            p.op("vector", lambda e: e.tensor_scalar(row(6), row(6), EPS_RMS_FFN, None, A.add),
                 reads=[rkey(6)], writes=[rkey(6)], narrow=True)
            p.op("scalar", lambda e: e.activation(out=row(7), in_=row(6), func=AF.Ln),
                 reads=[rkey(6)], writes=[rkey(7)])
            p.op("vector", lambda e: e.tensor_scalar(row(7), row(7), -0.5, None, A.mult),
                 reads=[rkey(7)], writes=[rkey(7)], narrow=True)
            p.op("scalar", lambda e: e.activation(out=row(6), in_=row(7), func=AF.Exp),
                 reads=[rkey(7)], writes=[rkey(6)])
            p.op("vector", lambda e: e.tensor_tensor(row(4), row(3), row(6), A.mult),
                 reads=[rkey(3), rkey(6)], writes=[rkey(4)], narrow=True)
        else:
            p.op("vector", lambda e: e.tensor_copy(row(4), row(3)),
                 reads=[rkey(3)], writes=[rkey(4)], narrow=True)
        p.op("vector", lambda e: e.tensor_tensor(row(5), row(0), row(4), A.mult),
             reads=[rkey(0), rkey(4)], writes=[rkey(5)], narrow=True)
        p.op("tensor", lambda e: e.matmul(ps_bc, ones_row, row(4), start=True, stop=True),
             reads=[rkey(4), "onesc", "gb6"], writes=["gb6"])
        p.op("scalar", lambda e, d=rb[pr]: e.copy(out=d[:, :], in_=ps_bc),
             reads=["gb6"], writes=[f"rb{pr}"])
        p.op("tensor", lambda e: e.matmul(ps_bc, ones_row, row(5), start=True, stop=True),
             reads=[rkey(5), "onesc", "gb6"], writes=["gb6"])
        p.op("scalar", lambda e, d=ab[pr]: e.copy(out=d[:, :], in_=ps_bc),
             reads=["gb6"], writes=[f"ab{pr}"])

    def ln_apply(pr, srcbuf, srck, dstbuf, dstk):
        for c in range(NCH):
            eng = "gpsimd" if c % 2 == 0 else "vector"
            p.op(eng, lambda e, c=c, s=srcbuf, d=dstbuf: e.tensor_tensor(
                ch(d, c), ch(s, c), rb[pr][:, :], A.mult),
                reads=[srck(c), f"rb{pr}"], writes=[dstk(c)])
            p.op(eng, lambda e, c=c, d=dstbuf: e.tensor_tensor(
                ch(d, c), ch(d, c), ab[pr][:, :], A.subtract),
                reads=[dstk(c), f"ab{pr}"], writes=[dstk(c)])

    def absmax_calc(pr, srcbuf, srck):
        W2 = 2 * TT
        p.op("vector", lambda e, s=srcbuf: e.tensor_tensor(
            u0[:, :], s[:, 0:W2], s[:, W2:2 * W2], A.max),
            reads=[srck(0), srck(1), srck(2), srck(3)], writes=["l0"])
        p.op("vector", lambda e, s=srcbuf: e.tensor_tensor(
            u2[:, :], s[:, 0:W2], s[:, W2:2 * W2], A.min),
            reads=[srck(0), srck(1), srck(2), srck(3)], writes=["l2"])
        for g in range(2, NCH // 2):
            p.op("vector", lambda e, g=g, s=srcbuf: e.tensor_tensor(
                u0[:, :], u0[:, :], s[:, g * W2:(g + 1) * W2], A.max),
                reads=[srck(2 * g), srck(2 * g + 1), "l0"], writes=["l0"])
            p.op("vector", lambda e, g=g, s=srcbuf: e.tensor_tensor(
                u2[:, :], u2[:, :], s[:, g * W2:(g + 1) * W2], A.min),
                reads=[srck(2 * g), srck(2 * g + 1), "l2"], writes=["l2"])
        p.op("vector", lambda e: e.tensor_tensor(
            u0[:, 0:TT], u0[:, 0:TT], u0[:, TT:2 * TT], A.max),
            reads=["l0"], writes=["l0"])
        p.op("vector", lambda e: e.tensor_tensor(
            u2[:, 0:TT], u2[:, 0:TT], u2[:, TT:2 * TT], A.min),
            reads=["l2"], writes=["l2"])
        for j in range(TT // 128):
            p.op("tensor", lambda e, j=j: e.transpose(ps_tr, u0[:, j * 128:(j + 1) * 128],
                                                      ident[:, :]),
                 reads=["l0", "ident", "gb6"], writes=["gb6"])
            p.op("vector", lambda e, j=j: e.reduce_max(colsb[:, j:j + 1], ps_tr, AX.X),
                 reads=["gb6"], writes=["cmax"], narrow=True)
            p.op("tensor", lambda e, j=j: e.transpose(ps_tr, u2[:, j * 128:(j + 1) * 128],
                                                      ident[:, :]),
                 reads=["l2", "ident", "gb6"], writes=["gb6"])
            p.op("vector", lambda e, j=j: e.tensor_reduce(colsb[:, 2 + j:3 + j], ps_tr, AX.X, A.min),
                 reads=["gb6"], writes=["cmin"], narrow=True)
        for j in range(TT // 128):
            p.op("vector", lambda e, j=j: e.tensor_scalar(colsb[:, 4 + j:5 + j],
                                                          colsb[:, 2 + j:3 + j], -1.0, None, A.mult),
                 reads=["cmin"], writes=["cneg"], narrow=True)
            p.op("vector", lambda e, j=j: e.tensor_tensor(colsb[:, 6 + j:7 + j],
                                                          colsb[:, j:j + 1],
                                                          colsb[:, 4 + j:5 + j], A.max),
                 reads=["cmax", "cneg"], writes=["cam"], narrow=True)
            p.op("vector", lambda e, j=j: e.tensor_scalar(colsb[:, 8 + j:9 + j],
                                                          colsb[:, 6 + j:7 + j],
                                                          1.0 / QB, None, A.mult),
                 reads=["cam"], writes=["cxs"], narrow=True)
            p.op("vector", lambda e, j=j: e.tensor_scalar(colsb[:, 10 + j:11 + j],
                                                          colsb[:, 8 + j:9 + j],
                                                          EPS_FFN, None, A.add),
                 reads=["cxs"], writes=["cpe"], narrow=True)
            p.op("vector", lambda e, j=j: e.reciprocal(
                colsb[:, 12 + j:13 + j], colsb[:, 10 + j:11 + j]),
                reads=["cpe"], writes=["cinv"], narrow=True)
        for j in range(TT // 128):
            p.op("tensor", lambda e, j=j: e.transpose(ps_rr, colsb[:, 8 + j:9 + j], ident[:, :]),
                 reads=["cxs", "ident", "gb6"], writes=["gb6"])
            p.op("scalar", lambda e, j=j: e.copy(out=rows[0:1, j * 128:(j + 1) * 128], in_=ps_rr),
                 reads=["gb6"], writes=["r0"])
            p.op("tensor", lambda e, j=j: e.transpose(ps_rr, colsb[:, 12 + j:13 + j], ident[:, :]),
                 reads=["cinv", "ident", "gb6"], writes=["gb6"])
            p.op("scalar", lambda e, j=j: e.copy(out=rows[0:1, TT + j * 128:TT + (j + 1) * 128],
                                                 in_=ps_rr),
                 reads=["gb6"], writes=["r1"])
        p.op("tensor", lambda e: e.matmul(ps_bc, ones_row, rows[0:1, 0:TT], start=True, stop=True),
             reads=["r0", "onesc", "gb6"], writes=["gb6"])
        p.op("scalar", lambda e, d=xsb[pr]: e.copy(out=d[:, :], in_=ps_bc),
             reads=["gb6"], writes=[f"xsb{pr}"])
        p.op("tensor", lambda e: e.matmul(ps_bc, ones_row, rows[0:1, TT:2 * TT], start=True, stop=True),
             reads=["r1", "onesc", "gb6"], writes=["gb6"])
        p.op("scalar", lambda e, d=invsb[pr]: e.copy(out=d[:, :], in_=ps_bc),
             reads=["gb6"], writes=[f"invsb{pr}"])

    def quant_pair(pr, srcbuf, srck, g):
        c0, c1 = 2 * g, 2 * g + 1
        p.op("vector", lambda e, c=c0, s=srcbuf: e.tensor_tensor(
            uu[:, 0:TT], ch(s, c), invsb[pr][:, :], A.mult),
            reads=[srck(c0), f"invsb{pr}"], writes=["lq"])
        p.op("vector", lambda e, c=c1, s=srcbuf: e.tensor_tensor(
            uu[:, TT:2 * TT], ch(s, c), invsb[pr][:, :], A.mult),
            reads=[srck(c1), f"invsb{pr}"], writes=["lq"])
        _ladder_floor(p, uu[:, :], u0[:, :], uu[:, :], u2[:, :],
                      LADDER_BITS, ("lq", "l0", "lq", "l2"), 128.5)
        p.op("vector", lambda e: e.tensor_scalar(u2[:, :], u2[:, :],
                                                 -128.0, -127.0, A.add, A.max),
             reads=["l2"], writes=["l2"])
        p.op("vector", lambda e, c=c0, hb=hixq[pr]: e.tensor_scalar(
            hb[:, c * TT:(c + 2) * TT], u2[:, :], 127.0, None, A.min),
            reads=["l2"], writes=[HXk(pr, c0), HXk(pr, c1)])

    def ffn_quarter(pr, q, ph):
        fwslX = fwslA if ph == "A" else fwslB
        ps_bkX = ps_bkA if ph == "A" else ps_bkB
        PBKX = PBKA if ph == "A" else PBKB
        fs = "fa" if ph == "A" else "fb"
        for k in range(NCH):
            fslot = k % 2
            fkey = f"{fs}sl{fslot}"
            p.op("sync", lambda e, k=k, s=fslot, q=q, fb=fwslX: e.dma_start(
                out=fb[s][:, :], in_=wtscr[k * 128:(k + 1) * 128,
                                          q * 512:(q + 1) * 512]),
                reads=["wtscrk"], writes=[fkey], dma=True,
                stream=f"{fs}{fslot}")
            for j in range(4):
                p.op("tensor", lambda e, k=k, j=j, s=fslot, fb=fwslX,
                     hb=hixq[pr]: e.matmul(
                    ps_bkX[j], fb[s][:, j * 128:(j + 1) * 128], ch(hb, k),
                    start=(k == 0 and j % 2 == 0),
                    stop=(k == NCH - 1 and j % 2 == 1)),
                    reads=[fkey, HXk(pr, k)], writes=[PBKX[j]])
            if k % 4 == 3:
                yield
        for j in range(4):
            n = q * 4 + j
            slot = n % 2
            sl = ev[:, slot * TT:(slot + 1) * TT]
            p.op("scalar", lambda e, n=n, j=j, sl=sl: e.mul(
                sl, ps_bkX[j], gamma[:, n:n + 1]),
                reads=[PBKX[j], "gamma"], writes=[f"ev{slot}"])
            p.op("vector", lambda e, sl=sl: e.tensor_tensor(
                sl, sl, xsb[pr][:, :], A.mult),
                reads=[f"ev{slot}", f"xsb{pr}"], writes=[f"ev{slot}"])
            p.op("vector", lambda e, n=n, sl=sl: e.tensor_scalar(
                sl, sl, 0.5, hbias[:, n:n + 1], A.mult, A.add),
                reads=[f"ev{slot}", "hbias"], writes=[f"ev{slot}"])
            p.op("gpsimd" if n % 2 == 0 else "vector",
                 lambda e, n=n, sl=sl, xb=xres[pr]: e.tensor_tensor(
                ch(xb, n), ch(xb, n), sl, A.add),
                reads=[f"ev{slot}", XRk(pr, n)], writes=[XRk(pr, n)])
        yield

    def stream_proj(pr, widx, q, evac, ph):
        wslX = wslA if ph == "A" else wslB
        ps_bkX = ps_bkA if ph == "A" else ps_bkB
        PBKX = PBKA if ph == "A" else PBKB
        ws = "wa" if ph == "A" else "wb"
        nslot = 3 if ph == "A" else 2
        nh, qq = q // 2, q % 2
        for k in range(NCH):
            slot = k % nslot
            wkey = f"{ws}sl{slot}"
            r0 = nh * D + k * 128
            p.op("sync", lambda e, r0=r0, s=slot, w=widx, qq=qq,
                 wb=wslX: e.dma_start(
                out=wb[s][:, :], in_=wscr[w][r0:r0 + 128,
                                             qq * 1024:(qq + 1) * 1024]),
                reads=[f"wscr{widx}"], writes=[wkey], dma=True,
                stream=f"{ws}{slot}")
            for j in range(4):
                whi = wslX[slot][:, j * 128:(j + 1) * 128]
                wlo = wslX[slot][:, 512 + j * 128:512 + (j + 1) * 128]
                xh = ch(hixq[pr], k)
                xl = ch(lobf[pr], k)
                p.op("tensor", lambda e, o=ps_bkX[j], a=whi, bb=xh,
                     f=(k == 0 and j % 2 == 0): e.matmul(o, a, bb, start=f, stop=False),
                     reads=[wkey, HXk(pr, k)], writes=[PBKX[j]])
                p.op("tensor", lambda e, o=ps_bkX[j], a=whi, bb=xl: e.matmul(
                    o, a, bb, start=False, stop=False),
                    reads=[wkey, LOk(pr, k)], writes=[PBKX[j]])
                p.op("tensor", lambda e, o=ps_bkX[j], a=wlo, bb=xh,
                     last=(k == NCH - 1 and j % 2 == 1): e.matmul(
                    o, a, bb, start=False, stop=last),
                    reads=[wkey, HXk(pr, k)], writes=[PBKX[j]])
            if k % 4 == 3:
                yield
        for j in range(4):
            evac(q * 4 + j, ps_bkX[j], PBKX[j])
        yield

    def gla_head(pr, h, has_q, tile_idx):
        i = h % 2
        KHk, GHk, THk = f"kh{i}", f"gh{i}", f"th{i}"
        BTk, E1k, E2k = f"bt{i}", f"e1{i}", f"e2{i}"
        QIk, QTk = f"qint{i}", f"qitr{i}"
        KOk = f"kot{i}"
        PPk, KTk, VTk = f"Pp{i}", f"kTs{i}", f"vTs{i}"
        S8k, SEk = f"st8{i}", f"stE{i}"
        p.op("scalar", lambda e, h=h: e.activation(out=th[i][:, :], in_=ch(kraw, h), func=AF.Exp),
             reads=[KR[h]], writes=[THk])
        p.op("vector", lambda e: e.tensor_scalar(th[i][:, :], th[i][:, :], 1.0, None, A.add),
             reads=[THk], writes=[THk])
        p.op("scalar", lambda e: e.activation(out=gh[i][:, :], in_=th[i][:, :], func=AF.Ln),
             reads=[THk], writes=[GHk])
        p.op("vector", lambda e, h=h: e.tensor_tensor(th[i][:, :], ch(kraw, h), gh[i][:, :], A.subtract),
             reads=[KR[h], GHk], writes=[THk])
        p.op("scalar", lambda e: e.activation(out=kh[i][:, :], in_=th[i][:, :], func=AF.Exp),
             reads=[THk], writes=[KHk])
        p.op("vector", lambda e: e.tensor_scalar(gh[i][:, :], gh[i][:, :], -1.0, None, A.mult),
             reads=[GHk], writes=[GHk])
        Sh = Sst[:, h * 128:(h + 1) * 128]
        H = C // 2
        for c in (0, 1):
            csl = slice(c * C, (c + 1) * C)
            p.op("tensor", lambda e, h=h, c=c: e.transpose(
                ps_T, vbuf[:, h * TT + c * C:h * TT + (c + 1) * C], ident[:, :]),
                reads=[VB[h], "ident", "gb1"], writes=["gb1"])
            p.op("scalar", lambda e: e.copy(out=vTs[i][:, :], in_=ps_T),
                 reads=["gb1"], writes=[VTk])
            if tile_idx == 0:
                p.op("vector", lambda e, c=c: e.tensor_scalar(
                    vTs[i][:, :], vTs[i][:, :], halom[:, c:c + 1], None, A.mult),
                    reads=[VTk, "halom"], writes=[VTk])
            p.op("vector", lambda e, csl=csl: e.tensor_tensor_scan(
                bt[i][:, :], gh[i][:, csl], gh[i][:, csl], 0.0, A.add, A.bypass),
                reads=[GHk], writes=[BTk])
            p.op("vector", lambda e: e.tensor_scalar(st8[i][:, 1:2], bt[i][:, H - 1:H],
                                                     0.5, None, A.mult),
                 reads=[BTk], writes=[S8k], narrow=True)
            p.op("vector", lambda e: e.tensor_scalar(st8[i][:, 2:3], bt[i][:, C - 1:C],
                                                     0.5, st8[i][:, 1:2], A.mult, A.add),
                 reads=[BTk, S8k], writes=[S8k], narrow=True)
            p.op("vector", lambda e: e.tensor_tensor(st8[i][:, 3:4], bt[i][:, C - 1:C],
                                                     st8[i][:, 1:2], A.subtract),
                 reads=[BTk, S8k], writes=[S8k], narrow=True)
            p.op("vector", lambda e: e.tensor_tensor(st8[i][:, 0:1], bt[i][:, C - 1:C],
                                                     bt[i][:, H - 1:H], A.subtract),
                 reads=[BTk, S8k], writes=[S8k], narrow=True)
            p.op("scalar", lambda e: e.activation(out=stE[i][:, 0:1], in_=bt[i][:, C - 1:C],
                                                  func=AF.Exp),
                 reads=[BTk], writes=[SEk])
            p.op("scalar", lambda e: e.activation(out=stE[i][:, 1:2], in_=st8[i][:, 3:4],
                                                  func=AF.Exp),
                 reads=[S8k], writes=[SEk])
            p.op("scalar", lambda e: e.activation(out=stE[i][:, 2:3], in_=st8[i][:, 0:1],
                                                  func=AF.Exp, scale=0.5),
                 reads=[S8k], writes=[SEk])
            p.op("scalar", lambda e: e.activation(out=stE[i][:, 3:4], in_=bt[i][:, H - 1:H],
                                                  func=AF.Exp, scale=0.5),
                 reads=[BTk], writes=[SEk])
            p.op("scalar", lambda e: e.activation(out=e1[i][:, :], in_=bt[i][:, :], func=AF.Exp),
                 reads=[BTk], writes=[E1k])
            if has_q:
                qsl = qhalf[:, h * TT + c * C:h * TT + (c + 1) * C]
                p.op("vector", lambda e, q=qsl: e.tensor_tensor(qint[i][:, :], q, e1[i][:, :], A.mult),
                     reads=[QH[h], E1k], writes=[QIk])
            p.op("vector", lambda e: e.tensor_scalar(bt[i][:, 0:H], bt[i][:, 0:H], -1.0,
                                                     st8[i][:, 1:2], A.mult, A.add),
                 reads=[BTk, S8k], writes=[BTk], narrow=True)
            p.op("vector", lambda e: e.tensor_scalar(bt[i][:, H:C], bt[i][:, H:C], -1.0,
                                                     st8[i][:, 2:3], A.mult, A.add),
                 reads=[BTk, S8k], writes=[BTk], narrow=True)
            p.op("vector", lambda e: e.tensor_scalar(bt[i][:, :], bt[i][:, :],
                                                     80.0, -80.0, A.min, A.max),
                 reads=[BTk], writes=[BTk])
            p.op("scalar", lambda e: e.activation(out=e2[i][:, :], in_=bt[i][:, :], func=AF.Exp),
                 reads=[BTk], writes=[E2k])
            if has_q:
                p.op("scalar", lambda e: e.activation(out=e1[i][:, :], in_=bt[i][:, :],
                                                      func=AF.Exp, scale=-1.0),
                     reads=[BTk], writes=[E1k])
                p.op("vector", lambda e, q=qsl: e.tensor_tensor(qitr[i][:, :], q, e1[i][:, :], A.mult),
                     reads=[QH[h], E1k], writes=[QTk])
            p.op("vector", lambda e, csl=csl: e.tensor_tensor(kot[i][:, :], kh[i][:, csl],
                                                              e2[i][:, :], A.mult),
                 reads=[KHk, E2k], writes=[KOk])
            if has_q:
                p.op("vector", lambda e: e.tensor_scalar(e1[i][:, 0:H], kot[i][:, 0:H],
                                                         stE[i][:, 3:4], None, A.mult),
                     reads=[KOk, SEk], writes=[E1k], narrow=True)
                p.op("vector", lambda e: e.tensor_scalar(e1[i][:, H:C], qitr[i][:, H:C],
                                                         stE[i][:, 2:3], None, A.mult),
                     reads=[QTk, SEk], writes=[E1k], narrow=True)
                p.op("tensor", lambda e: e.matmul(ps_A[0:H, 0:H], kot[i][:, 0:H],
                                                  qitr[i][:, 0:H],
                                                  start=True, stop=False),
                     reads=[KOk, QTk, "gb0"], writes=["gb0"])
                p.op("tensor", lambda e: e.matmul(ps_A[H:C, H:C], kot[i][:, H:C],
                                                  qitr[i][:, H:C],
                                                  start=True, stop=True),
                     reads=[KOk, QTk], writes=["gb0"])
                p.op("tensor", lambda e: e.matmul(ps_A[0:H, H:C], e1[i][:, 0:H],
                                                  e1[i][:, H:C],
                                                  start=False, stop=True),
                     reads=[E1k], writes=["gb0"])
                p.op("vector", lambda e: e.copy_predicated(Pp[i][0:H, 0:H],
                                                           causal[0:H, 0:H],
                                                           ps_A[0:H, 0:H]),
                     reads=["gb0", "causal", f"Ppz{i}"], writes=[PPk])
                p.op("vector", lambda e: e.copy_predicated(Pp[i][H:C, H:C],
                                                           causal[H:C, H:C],
                                                           ps_A[H:C, H:C]),
                     reads=["gb0", "causal", PPk], writes=[PPk])
                p.op("scalar", lambda e: e.copy(out=Pp[i][0:H, H:C], in_=ps_A[0:H, H:C]),
                     reads=["gb0", PPk], writes=[PPk])
                p.op("tensor", lambda e: e.matmul(ps_O, vTs[i][:, :], Pp[i][:, :],
                                                  start=True, stop=False),
                     reads=[VTk, PPk, "gb0"], writes=["gb0"])
                p.op("tensor", lambda e, S=Sh: e.matmul(ps_O, S, qint[i][:, :],
                                                        start=False, stop=True),
                     reads=[SK[h], QIk], writes=["gb0"])
                p.op("scalar", lambda e, h=h, c=c, ob=xno[pr]: e.copy(
                    out=ob[:, h * TT + c * C:h * TT + (c + 1) * C], in_=ps_O),
                    reads=["gb0"], writes=[XNk(pr, h)])
            p.op("vector", lambda e: e.tensor_scalar(kot[i][:, 0:H], kot[i][:, 0:H],
                                                     stE[i][:, 1:2], None, A.mult),
                 reads=[KOk, SEk], writes=[KOk], narrow=True)
            p.op("vector", lambda e: e.tensor_scalar(kot[i][:, H:C], kot[i][:, H:C],
                                                     stE[i][:, 2:3], None, A.mult),
                 reads=[KOk, SEk], writes=[KOk], narrow=True)
            p.op("tensor", lambda e: e.transpose(ps_T, kot[i][:, :], ident[:, :]),
                 reads=[KOk, "ident", "gb1"], writes=["gb1"])
            p.op("scalar", lambda e: e.copy(out=kTs[i][:, :], in_=ps_T),
                 reads=["gb1"], writes=[KTk])
            p.op("tensor", lambda e: e.matmul(ps_S, kTs[i][:, :], vTs[i][:, :],
                                              start=True, stop=True),
                 reads=[KTk, VTk, "gb1"], writes=["gb1"])
            p.op("vector", lambda e, S=Sh: e.tensor_scalar(S, S, stE[i][:, 0:1], None, A.mult),
                 reads=[SK[h], SEk], writes=[SK[h]])
            p.op("vector", lambda e, S=Sh: e.tensor_tensor(S, S, ps_S, A.add),
                 reads=[SK[h], "gb1"], writes=[SK[h]])

    def dbg_dump(name, buf, keys):
        if name in dbg_t:
            w = buf[:, :].shape[1]
            p.op("sync", lambda e, d=dbg_t[name], s=buf, w=w: e.dma_start(
                out=d[:, 0:w], in_=s[:, :]),
                reads=list(keys), writes=[], dma=True, stream="dbg")

    def phaseA(tile):
        if tile >= NTILES:
            return
        pr = tile % 2
        halo_tile = (tile == 0)
        src = x_fm.rearrange("(c p) t -> p c t", p=128)[:, :, tile * TT:(tile + 1) * TT]
        dst = xres[pr][:, :].rearrange("p (c t) -> p c t", t=TT)
        p.op("sync", lambda e, s=src, d=dst: e.dma_start(out=d, in_=s),
             reads=[XRk(pr, c) for c in range(NCH)],
             writes=[XRk(pr, c) for c in range(NCH)], dma=True,
             stream=f"dx{pr}")
        yield
        ln_stats(pr, xres[pr], lambda c: XRk(pr, c), with_rms=True)
        yield
        ln_apply(pr, xres[pr], lambda c: XRk(pr, c), xno[pr], lambda c: XNk(pr, c))
        yield
        absmax_calc(pr, xno[pr], lambda c: XNk(pr, c))
        yield
        for g in range(NCH // 2):
            quant_pair(pr, xno[pr], lambda cc: XNk(pr, cc), g)
            if g % 2 == 1:
                yield
        for q in range(4):
            for _ in ffn_quarter(pr, q, "A"):
                yield
        if tile == 1:
            dbg_dump("x1", xres[pr], [XRk(pr, c) for c in range(NCH)])
        ln_stats(pr, xres[pr], lambda c: XRk(pr, c), with_rms=False)
        yield
        ln_apply(pr, xres[pr], lambda c: XRk(pr, c), xno[pr], lambda c: XNk(pr, c))
        yield
        for c in range(NCH):
            eng = "gpsimd" if c % 2 == 0 else "vector"
            p.op(eng, lambda e, c=c, xb=xno[pr], hb=hixq[pr]: e.tensor_copy(
                ch(hb, c), ch(xb, c)),
                reads=[XNk(pr, c)], writes=[HXk(pr, c)])
            p.op(eng, lambda e, c=c, xb=xno[pr], hb=hixq[pr], lb=lobf[pr]: e.tensor_tensor(
                ch(lb, c), ch(xb, c), ch(hb, c), A.subtract),
                reads=[XNk(pr, c), HXk(pr, c)], writes=[LOk(pr, c)])
        yield

        def evac_k(n, ps, pbk):
            p.op("scalar", lambda e, n=n, ps=ps: e.copy(out=ch(kraw, n), in_=ps),
                 reads=[pbk], writes=[KR[n]])
        for q in range(4):
            for _ in stream_proj(pr, 1, q, evac_k, "A"):
                yield

        if not halo_tile:
            def evac_q(n, ps, pbk):
                p.op("scalar", lambda e, n=n, ps=ps: e.activation(
                    out=qhalf[:, n * TT:(n + 1) * TT], in_=ps,
                    func=AF.Relu, scale=SCALE),
                    reads=[pbk], writes=[QH[n]])
            for q in range(4):
                for _ in stream_proj(pr, 0, q, evac_q, "A"):
                    yield

        def evac_v(n, ps, pbk):
            p.op("scalar", lambda e, n=n, ps=ps: e.copy(out=ch(vbuf, n), in_=ps),
                 reads=[pbk], writes=[VB[n]])
        for q in range(4):
            for _ in stream_proj(pr, 2, q, evac_v, "A"):
                yield

    def phaseB(tile):
        pr = tile % 2
        halo_tile = (tile == 0)
        if halo_tile:
            for h in range(NH):
                gla_head(pr, h, has_q=False, tile_idx=tile)
                yield
            return
        for h in range(NH):
            gla_head(pr, h, has_q=True, tile_idx=tile)
            yield
        if tile == 1:
            dbg_dump("o1", xno[pr], [XNk(pr, c) for c in range(NCH)])
        for c in range(NCH):
            slot = c % 2
            p.op("scalar", lambda e, c=c, s=slot, ob=xno[pr]: e.activation(
                out=sq[:, s * TT:(s + 1) * TT], in_=ch(ob, c), func=AF.Square),
                reads=[XNk(pr, c)], writes=[f"sq{slot}"])
            p.op("tensor", lambda e, c=c, s=slot: e.matmul(
                ps_row2, ones_col, sq[:, s * TT:(s + 1) * TT],
                start=(c == 0), stop=(c == NCH - 1)),
                reads=[f"sq{slot}", "onesc"], writes=["gb7"])
        p.op("vector", lambda e: e.tensor_scalar(row(6), ps_row2, 1.0 / 2048.0, None, A.mult),
             reads=["gb7"], writes=[rkey(6)], narrow=True)
        p.op("vector", lambda e: e.tensor_scalar(row(6), row(6), EPS_ATTN, None, A.add),
             reads=[rkey(6)], writes=[rkey(6)], narrow=True)
        p.op("scalar", lambda e: e.activation(out=row(7), in_=row(6), func=AF.Ln),
             reads=[rkey(6)], writes=[rkey(7)])
        p.op("vector", lambda e: e.tensor_scalar(row(7), row(7), -0.5, None, A.mult),
             reads=[rkey(7)], writes=[rkey(7)], narrow=True)
        p.op("scalar", lambda e: e.activation(out=row(6), in_=row(7), func=AF.Exp),
             reads=[rkey(7)], writes=[rkey(6)])
        p.op("tensor", lambda e: e.matmul(ps_bc, ones_row, row(6), start=True, stop=True),
             reads=[rkey(6), "onesc", "gb6"], writes=["gb6"])
        p.op("scalar", lambda e, d=rb[pr]: e.copy(out=d[:, :], in_=ps_bc),
             reads=["gb6"], writes=[f"rb{pr}"])
        yield
        for c in range(NCH):
            slot = c % 2
            sl = ev[:, slot * TT:(slot + 1) * TT]
            p.op("vector", lambda e, c=c, sl=sl, ob=xno[pr]: e.tensor_tensor(
                sl, ch(ob, c), rb[pr][:, :], A.mult),
                reads=[XNk(pr, c), f"rb{pr}"], writes=[f"ev{slot}"])
            p.op("vector", lambda e, c=c, sl=sl: e.tensor_scalar(
                sl, sl, gwb[:, c:c + 1], None, A.mult),
                reads=[f"ev{slot}", "gwb"], writes=[f"ev{slot}"])
            p.op("gpsimd" if c % 2 == 0 else "vector",
                 lambda e, c=c, sl=sl, hb=hixq[pr]: e.tensor_copy(
                ch(hb, c), sl),
                reads=[f"ev{slot}"], writes=[HXk(pr, c)])
            p.op("gpsimd" if c % 2 == 0 else "vector",
                 lambda e, c=c, sl=sl, hb=hixq[pr], lb=lobf[pr]: e.tensor_tensor(
                ch(lb, c), sl, ch(hb, c), A.subtract),
                reads=[f"ev{slot}", HXk(pr, c)], writes=[LOk(pr, c)])
        yield

        def evac_wo(n, ps, pbk):
            p.op("vector", lambda e, n=n, ps=ps, xb=xres[pr]: e.tensor_tensor(
                ch(xb, n), ch(xb, n), ps, A.add),
                reads=[pbk, XRk(pr, n)], writes=[XRk(pr, n)])
        for q in range(4):
            for _ in stream_proj(pr, 3, q, evac_wo, "B"):
                yield

        ln_stats(pr, xres[pr], lambda c: XRk(pr, c), with_rms=True)
        yield
        ln_apply(pr, xres[pr], lambda c: XRk(pr, c), xno[pr], lambda c: XNk(pr, c))
        yield
        absmax_calc(pr, xno[pr], lambda c: XNk(pr, c))
        yield
        for g in range(NCH // 2):
            quant_pair(pr, xno[pr], lambda cc: XNk(pr, cc), g)
            if g % 2 == 1:
                yield
        for q in range(4):
            for _ in ffn_quarter(pr, q, "B"):
                yield

        osrc = xres[pr][:, :].rearrange("p (c t) -> p c t", t=TT)
        odst = y_fm.rearrange("(c p) t -> p c t", p=128)[
            :, :, (tile - 1) * TT:tile * TT]
        p.op("sync", lambda e, s=osrc, d=odst: e.dma_start(out=d, in_=s),
             reads=[XRk(pr, c) for c in range(NCH)] + ["outdma"],
             writes=["outdma"], dma=True, stream="do")
        yield

    def drain(gen):
        for _ in gen:
            pass

    drain(phaseA(0))
    PRE = 8
    for tile in range(NTILES):
        gb = phaseB(tile)
        ga = phaseA(tile + 1)
        for _ in range(PRE):
            if next(ga, StopIteration) is StopIteration:
                break
        while True:
            db = next(gb, StopIteration)
            da = next(ga, StopIteration)
            if db is StopIteration and da is StopIteration:
                break

    p.op("sync", lambda e: e.nop(),
         reads=["outdma"] + [XRk(pp, c) for pp in (0, 1) for c in range(NCH)] + SK,
         writes=[])

    emit_engine = p.emit(nc, sems)
    with nc.Block() as blk:
        blk.sync(lambda e: emit_engine("sync", e))
        blk.tensor(lambda e: emit_engine("tensor", e))
        blk.vector(lambda e: emit_engine("vector", e))
        blk.scalar(lambda e: emit_engine("scalar", e))
        blk.gpsimd(lambda e: emit_engine("gpsimd", e))
    es.close()
    return nc


def kernel(L0_attn_hidden_states, L0_attn_Wq, L0_attn_Wk, L0_attn_Wi, L0_attn_Wo,
           L0_attn_g_weight, L0_ffn_weight, L0_ffn_bias):
    x = np.asarray(L0_attn_hidden_states, dtype=np.float32)
    wq = np.ascontiguousarray(np.asarray(L0_attn_Wq, dtype=np.float32))
    wk = np.ascontiguousarray(np.asarray(L0_attn_Wk, dtype=np.float32))
    wi = np.ascontiguousarray(np.asarray(L0_attn_Wi, dtype=np.float32))
    wo = np.ascontiguousarray(np.asarray(L0_attn_Wo, dtype=np.float32))
    fw = np.ascontiguousarray(np.asarray(L0_ffn_weight, dtype=np.float32))
    fb = np.asarray(L0_ffn_bias, dtype=np.float32)
    gw = np.asarray(L0_attn_g_weight, dtype=np.float32)

    fbias = np.ascontiguousarray(fb.reshape(NCH, 128).T)
    gwin = np.ascontiguousarray(gw.reshape(NCH, 128).T)
    ident = np.eye(128, dtype=np.float32)
    causal = np.triu(np.ones((128, 128), dtype=np.uint8))
    onesc = np.ones((128, 128), dtype=np.float32)

    nc = build_program()
    in_maps = []
    for ci in range(NCORES):
        bidx, j = ci // 4, ci % 4
        t0 = j * TCORE
        if j == 0:
            xs = np.concatenate([np.zeros((HALO, D), np.float32), x[bidx, 0:TCORE]], axis=0)
            hm = np.zeros((128, 2), np.float32)
        else:
            xs = x[bidx, t0 - HALO:t0 + TCORE]
            hm = np.ones((128, 2), np.float32)
        in_maps.append({
            "x_fm": np.ascontiguousarray(xs.T),
            "wq": wq, "wk": wk, "wi": wi, "wo": wo, "fw": fw,
            "fbias": fbias, "gwin": gwin, "ident_in": ident,
            "causal_in": causal, "halom_in": hm, "onesc_in": onesc,
        })
    res = run_bass_kernel_spmd(nc, in_maps, core_ids=list(range(NCORES)))
    out = np.empty((B, L, D), dtype=np.float32)
    for ci in range(NCORES):
        bidx, j = ci // 4, ci % 4
        out[bidx, j * TCORE:(j + 1) * TCORE, :] = res.results[ci]["y_fm"].T
    if _DBG:
        kernel.dbg = [{k: res.results[ci]["dbg_" + k] for k in _DBG}
                      for ci in range(NCORES)]
    return out
